# version 21
# baseline (speedup 1.0000x reference)
import math
from contextlib import ExitStack

import numpy as np

import concourse.bass as bass
import concourse.mybir as mybir
from concourse.bass_utils import run_bass_kernel_spmd

F32 = mybir.dt.float32
BF16 = mybir.dt.bfloat16
F32R = mybir.dt.float32r
AF = mybir.ActivationFunctionType
ALU = mybir.AluOpType
AX = mybir.AxisListType

P = 128
CSHIFT = 12.0


class Cfg:
    def __init__(self, D=2048, S=8192, NT=2048, HALO=1024, H=16, KVB=4, F=4096,
                 QC=512, ACH=1024, FTC=1024):
        self.D, self.S, self.NT, self.HALO = D, S, NT, HALO
        self.H, self.KVB, self.F = H, KVB, F
        self.QC, self.ACH, self.FTC = QC, ACH, FTC
        self.DB = D // P
        self.EXT = NT + HALO
        self.FB = F // P
        self.W = HALO
        self.RH = H // KVB
        assert self.EXT % ACH == 0 and ACH % QC == 0 and S % QC == 0
        assert self.EXT % FTC == 0 and FTC % QC == 0 and NT % FTC == 0
        assert HALO % P == 0 and NT % P == 0


class Sem:
    __slots__ = ("h", "n")

    def __init__(self, nc, name):
        self.h = nc.alloc_semaphore(name=name)
        self.n = 0

    def inc(self, instr, k=1):
        instr.then_inc(self.h, k)
        self.n += k
        return self.n


def build_program(cfg: Cfg):
    nc = bass.Bass("TRN2", target_bir_lowering=False, debug=False)
    ctx = ExitStack()
    D, DB, S, NT, HALO, EXT = cfg.D, cfg.DB, cfg.S, cfg.NT, cfg.HALO, cfg.EXT
    H, KVB, F, FB, RH = cfg.H, cfg.KVB, cfg.F, cfg.FB, cfg.RH
    QC, ACH, FTC, W = cfg.QC, cfg.ACH, cfg.FTC, cfg.W
    SC = S // QC
    NSLAB = EXT // ACH
    KB = S // P
    WB = W // P

    dram = {}

    def din(name, shape, dtype=F32):
        dram[name] = nc.dram_tensor(name, list(shape), dtype, kind="ExternalInput")
        return dram[name]

    xTb = din("xTb", (D, S))
    xTe = din("xTe", (D, EXT))
    Wq = din("Wq", (D, H * P))
    Wk = din("Wk", (D, P))
    Wv = din("Wv", (D, P))
    Wo = din("Wo", (H * P, D))
    bqT = din("bqT", (P, H))
    bk_ = din("bk", (P, 1))
    bv_ = din("bv", (P, 1))
    boT = din("boT", (P, DB))
    WgT = din("WgT", (D, 2))
    W0aT = din("W0aT", (D, F))
    W0bT = din("W0bT", (F, D))
    W1aT = din("W1aT", (D, F))
    W1bT = din("W1bT", (F, D))
    b0aT = din("b0aT", (P, FB))
    b0bT = din("b0bT", (P, DB))
    b1aT = din("b1aT", (P, FB))
    b1bT = din("b1bT", (P, DB))
    wqT = din("wqT", (D, H * P))
    wkT = din("wkT", (D, KVB * P))
    wvT = din("wvT", (D, KVB * P))
    woT = din("woT", (H * P, D))
    qnw_ = din("qnw", (P, 1))
    knw_ = din("knw", (P, 1))
    pos0_ = din("pos0", (1, 1))
    ropeT_ = din("ropeT", (P, 2 * (NT + HALO)))
    halob_ = din("halob", (P, 1))

    res = nc.dram_tensor("res", [D, EXT], F32)
    outT = nc.dram_tensor("outT", [D, NT], F32, kind="ExternalOutput")

    sbufs = {}

    def sb(name, shape, dtype):
        sbufs[name] = ctx.enter_context(nc.sbuf_tensor(name, list(shape), dtype))
        return sbufs[name]

    lnxt = sb("lnxt", (P, DB * max(ACH, FTC)), BF16)
    WQCH = max(1, DB // 4) * P
    bigB = sb("bigB", (P, max(H * ACH + 2 * H * WQCH, FB * FTC,
                              H * NT, 2 * H * WQCH)), BF16)
    kvb = sb("kvb", (P, max(2 * S, 2 * KVB * EXT, DB * FTC + FB * 256)), BF16)
    wbuf = sb("wbuf", (P, 2 * DB * P), BF16)
    xt2 = sb("xt2", (P, 2 * QC), F32)
    xb2 = sb("xb2", (P, 2 * QC), BF16)
    sqb = sb("sqb", (P, 2 * QC), BF16)
    sq2 = sb("sq2", (P, 2 * QC), F32)
    rest = sb("rest", (P, 2 * QC), F32)
    ptil = sb("ptil", (P, max(4 * QC, 2 * ACH, 2 * min(1024, NT))), BF16)
    rtil = sb("rtil", (P, max(2 * QC, ACH, min(1024, NT))), BF16)
    mtil = sb("mtil", (P, FTC), BF16)
    qsc = sb("qsc", (P, 3 * QC), F32)
    ropet = sb("ropet", (P, 2 * EXT), BF16)
    rtc = sb("rtc", (P, P), BF16)
    rows = sb("rows", (1, 2112), F32)
    rowsb = sb("rowsb", (1, 1024), BF16)
    small = sb("small", (P, 288), F32)
    smb = sb("smb", (P, 160), BF16)
    mask2 = sb("mask2", (P, 2 * P), BF16)
    ident = sb("ident", (P, P), BF16)

    ONES = 0
    BQ = 1
    BK = BQ + H
    BV = BK + 1
    BO = BV + 1
    B0A = BO + DB
    B0B = B0A + FB
    B1A = B0B + DB
    B1B = B1A + FB
    QNW = B1B + DB
    KNW = QNW + 1
    HALOB = KNW + 1
    WBARK = HALOB + 1
    WBARV = WBARK + 1
    WGD = WBARV + 1
    ONE_ROW_P0 = WGD + DB
    C_NSHIFT = ONE_ROW_P0 + P
    C_EPSLN = C_NSHIFT + 1
    C_EPSRMS = C_EPSLN + 1
    C_EPSRMS128 = C_EPSRMS + 1
    C_LNHALF128 = C_EPSRMS128 + 1
    HALOR = C_LNHALF128 + 1
    assert HALOR + 1 <= 288

    R_MEAN, R_RSTD, R_T0, R_T1 = 0, 512, 1024, 1536
    R_DEN = 1024
    R_MISC = 2048
    RB_RSTD, RB_MB, RB_DEN, RB_MASK = 0, 512, 0, 0


    pe = Sem(nc, "pe")
    act = Sem(nc, "act")
    dve = Sem(nc, "dve")
    pool = Sem(nc, "pool")
    s_bias = Sem(nc, "s_bias")
    s_wkv = Sem(nc, "s_wkv")
    s_x = [Sem(nc, "s_x0"), Sem(nc, "s_x1")]
    s_xb = [Sem(nc, "s_xb0"), Sem(nc, "s_xb1")]
    s_resw = [Sem(nc, "s_resw0"), Sem(nc, "s_resw1")]
    s_w = [Sem(nc, "s_w0"), Sem(nc, "s_w1")]
    s_wo = [Sem(nc, "s_wo0"), Sem(nc, "s_wo1")]
    s_dbg = Sem(nc, "s_dbg")
    all_dma_sems = [s_bias, s_wkv, s_x[0], s_x[1], s_xb[0], s_xb[1],
                    s_resw[0], s_resw[1], s_w[0], s_w[1], s_wo[0], s_wo[1],
                    s_dbg]

    block = ctx.enter_context(nc.Block())

    ones_col = small[:, ONES:ONES + 1]
    ones_row = small[0:1, ONE_ROW_P0:ONE_ROW_P0 + P]
    ones_bf = smb[:, 0:1]
    ones128_bf = smb[:, 1:2]
    ones_row_bf = smb[0:1, 8:8 + P]
    WGDB = 140
    scale_a = 1.0 / math.sqrt(P)


    def PE(f): block.tensor(f)
    def ACT(f): block.scalar(f)
    def DVE(f): block.vector(f)
    def POOL(f): block.gpsimd(f)
    def SP(f): block.sync(f)

    def join_all():
        marks = [(s, s.n) for s in ([pe, act, dve, pool] + all_dma_sems)]

        def mk():
            def f(e):
                for s, n in marks:
                    if n:
                        e.wait_ge(s.h, n)
            return f
        PE(mk()); ACT(mk()); DVE(mk()); POOL(mk()); SP(mk())

    def join_light():
        marks = [(s, s.n) for s in (pe, act, dve, pool, s_resw[0], s_resw[1],
                                    s_x[0], s_x[1], s_xb[0], s_xb[1])]

        def mk():
            def f(e):
                for s, n in marks:
                    if n:
                        e.wait_ge(s.h, n)
            return f
        PE(mk()); ACT(mk()); DVE(mk()); POOL(mk()); SP(mk())

    def phase_setup():
        def f(g):
            pool.inc(g.memset(small[:, ONES:ONES + 1], 1.0))
            pool.inc(g.memset(small[0:1, ONE_ROW_P0:ONE_ROW_P0 + P], 1.0))
            pool.inc(g.memset(smb[:, 0:1], 1.0))
            pool.inc(g.memset(smb[:, 1:2], 1.0 / 128.0))
            pool.inc(g.memset(smb[0:1, 8:8 + P], 1.0))
            pool.inc(g.memset(small[:, C_NSHIFT:C_NSHIFT + 1], -CSHIFT))
            pool.inc(g.memset(small[:, C_EPSLN:C_EPSLN + 1], 1e-5))
            pool.inc(g.memset(small[:, C_EPSRMS:C_EPSRMS + 1], 1e-6))
            pool.inc(g.memset(small[:, C_EPSRMS128:C_EPSRMS128 + 1], 128e-6))
            pool.inc(g.memset(small[:, C_LNHALF128:C_LNHALF128 + 1],
                              0.5 * math.log(128.0)))
            pool.inc(g.memset(ident[:], 0.0))
            pool.inc(g.memset(mask2[:, 0:P], 0.0))
            pool.inc(g.memset(mask2[:, P:2 * P], 0.0))
            g.wait_ge(pool.h, pool.n)
            pool.inc(g.affine_select(out=ident[:], in_=ident[:],
                                     compare_op=ALU.not_equal, fill=1.0,
                                     base=0, pattern=[[-1, P]], channel_multiplier=1))
            pool.inc(g.affine_select(out=mask2[:, 0:P], in_=mask2[:, 0:P],
                                     compare_op=ALU.is_ge, fill=-30000.0,
                                     base=-1, pattern=[[-1, P]], channel_multiplier=1))
            pool.inc(g.affine_select(out=mask2[:, P:2 * P], in_=mask2[:, P:2 * P],
                                     compare_op=ALU.is_ge, fill=-30000.0,
                                     base=0, pattern=[[1, P]], channel_multiplier=-1))
        POOL(f)

        def fs(s):
            s_bias.inc(s.dma_start(out=small[:, BQ:BQ + H], in_=bqT.ap()), 16)
            s_bias.inc(s.dma_start(out=small[:, BK:BK + 1], in_=bk_.ap()), 16)
            s_bias.inc(s.dma_start(out=small[:, BV:BV + 1], in_=bv_.ap()), 16)
            s_bias.inc(s.dma_start(out=small[:, BO:BO + DB], in_=boT.ap()), 16)
            s_bias.inc(s.dma_start(out=small[:, B0A:B0A + FB], in_=b0aT.ap()), 16)
            s_bias.inc(s.dma_start(out=small[:, B0B:B0B + DB], in_=b0bT.ap()), 16)
            s_bias.inc(s.dma_start(out=small[:, B1A:B1A + FB], in_=b1aT.ap()), 16)
            s_bias.inc(s.dma_start(out=small[:, B1B:B1B + DB], in_=b1bT.ap()), 16)
            s_bias.inc(s.dma_start(out=small[:, QNW:QNW + 1], in_=qnw_.ap()), 16)
            s_bias.inc(s.dma_start(out=small[:, KNW:KNW + 1], in_=knw_.ap()), 16)
            s_bias.inc(s.dma_start(out=small[:, HALOB:HALOB + 1], in_=halob_.ap()), 16)
            s_bias.inc(s.dma_start(out=rows[0:1, R_MISC:R_MISC + 1], in_=pos0_.ap()), 16)
        SP(fs)

        def fv(v):
            v.wait_ge(s_bias.h, s_bias.n)
            dve.inc(v.tensor_scalar_mul(small[:, BQ:BQ + H], small[:, BQ:BQ + H], scale_a))
            v.wait_ge(dve.h, dve.n)
            dve.inc(v.tensor_scalar_mul(small[:, QNW:QNW + 1], small[:, QNW:QNW + 1], scale_a))
            v.wait_ge(dve.h, dve.n)
            dve.inc(v.tensor_copy(small[:, HALOR:HALOR + 1], small[:, HALOB:HALOB + 1]))
            v.wait_ge(dve.h, dve.n)
            dve.inc(v.tensor_scalar_add(small[:, HALOB:HALOB + 1], small[:, HALOB:HALOB + 1], -CSHIFT))
        DVE(fv)

    def emit_ln_stats(src, col0, psA, psB, ps_sx, ps_sx2, eps_col, gate=False, ps_g=None):
        act0, pe0, dve0 = act.n, pe.n, dve.n
        per_db = 3 if gate else 2
        xw = []

        def fs(g):
            for db in range(DB):
                if db >= 2:
                    g.wait_ge(dve.h, dve0 + db - 1)
                    g.wait_ge(pe.h, pe0 + per_db * (db - 1))
                else:
                    if act0:
                        g.wait_ge(act.h, act0)
                    if pe0:
                        g.wait_ge(pe.h, pe0)
                    if dve0:
                        g.wait_ge(dve.h, dve0)
                xw.append(s_xb[db % 2].inc(g.dma_start(
                    out=xb2[:, (db % 2) * QC:(db % 2) * QC + QC],
                    in_=src[db * P:(db + 1) * P, col0:col0 + QC]), 16))
        POOL(fs)

        def fa(v):
            for db in range(DB):
                v.wait_ge(s_xb[db % 2].h, xw[db])
                if db >= 2:
                    v.wait_ge(pe.h, pe0 + per_db * (db - 1))
                dve.inc(v.tensor_tensor(sqb[:, (db % 2) * QC:(db % 2) * QC + QC],
                                        xb2[:, (db % 2) * QC:(db % 2) * QC + QC],
                                        xb2[:, (db % 2) * QC:(db % 2) * QC + QC],
                                        ALU.mult))
        DVE(fa)

        def fp(t):
            for db in range(DB):
                sl = (db % 2) * QC
                t.wait_ge(s_xb[db % 2].h, xw[db])
                pe.inc(t.matmul(ps_sx[0:1, 0:QC], ones_bf, xb2[:, sl:sl + QC],
                                start=(db == 0), stop=(db == DB - 1)))
                if gate:
                    pe.inc(t.matmul(ps_g[0:1, 0:QC], smb[:, WGDB + db:WGDB + db + 1],
                                    xb2[:, sl:sl + QC],
                                    start=(db == 0), stop=(db == DB - 1)))
                t.wait_ge(dve.h, dve0 + db + 1)
                pe.inc(t.matmul(ps_sx2[0:1, 0:QC], ones_bf, sqb[:, sl:sl + QC],
                                start=(db == 0), stop=(db == DB - 1)))
        PE(fp)

        def fv(v):
            v.wait_ge(pe.h, pe.n)
            dve.inc(v.tensor_scalar_mul(rows[0:1, R_MEAN:R_MEAN + QC],
                                        ps_sx[0:1, 0:QC], 1.0 / D))
            if gate:
                v.wait_ge(dve.h, dve.n)
                dve.inc(v.tensor_copy(rows[0:1, R_T1:R_T1 + QC], ps_g[0:1, 0:QC]))
            v.wait_ge(dve.h, dve.n)
            dve.inc(v.tensor_scalar_mul(rows[0:1, R_T0:R_T0 + QC],
                                        ps_sx2[0:1, 0:QC], 1.0 / D))
            v.wait_ge(dve.h, dve.n)
            dve.inc(v.tensor_tensor(rows[0:1, R_RSTD:R_RSTD + QC],
                                    rows[0:1, R_MEAN:R_MEAN + QC],
                                    rows[0:1, R_MEAN:R_MEAN + QC], ALU.mult))
            v.wait_ge(dve.h, dve.n)
            dve.inc(v.scalar_tensor_tensor(rows[0:1, R_T0:R_T0 + QC],
                                           rows[0:1, R_T0:R_T0 + QC],
                                           small[0:1, eps_col:eps_col + 1],
                                           rows[0:1, R_RSTD:R_RSTD + QC],
                                           ALU.add, ALU.subtract))
        DVE(fv)

        def fa2(s):
            s.wait_ge(dve.h, dve.n)
            act.inc(s.activation(rows[0:1, R_T0:R_T0 + QC],
                                 rows[0:1, R_T0:R_T0 + QC], AF.Ln))
        ACT(fa2)

        def fv2a(v):
            v.wait_ge(act.h, act.n)
            dve.inc(v.tensor_scalar_mul(rows[0:1, R_T0:R_T0 + QC],
                                        rows[0:1, R_T0:R_T0 + QC], -0.5))
        DVE(fv2a)

        def fa2b(s):
            s.wait_ge(dve.h, dve.n)
            act.inc(s.activation(rows[0:1, R_RSTD:R_RSTD + QC],
                                 rows[0:1, R_T0:R_T0 + QC], AF.Exp))
            s.wait_ge(act.h, act.n)
            act.inc(s.activation(rowsb[0:1, RB_RSTD:RB_RSTD + QC],
                                 rows[0:1, R_T0:R_T0 + QC], AF.Exp))
        ACT(fa2b)

        def fv2(v):
            v.wait_ge(act.h, act.n)
            dve.inc(v.tensor_tensor(rowsb[0:1, RB_MB:RB_MB + QC],
                                    rows[0:1, R_MEAN:R_MEAN + QC],
                                    rows[0:1, R_RSTD:R_RSTD + QC], ALU.mult))
        DVE(fv2)

        def fp2(t):
            t.wait_ge(dve.h, dve.n)
            pe.inc(t.matmul(psA[:, 0:QC], ones_row_bf,
                            rowsb[0:1, RB_RSTD:RB_RSTD + QC],
                            start=True, stop=True))
            pe.inc(t.matmul(psB[:, 0:QC], ones_row_bf,
                            rowsb[0:1, RB_MB:RB_MB + QC],
                            start=True, stop=True))
        PE(fp2)

    def emit_ln_apply(src, col0, psA, psB, dst, span, dcol):
        dve0, pe0a, act0a = dve.n, pe.n, act.n
        xw = []

        def fs(s):
            for db in range(DB):
                if db >= 2:
                    s.wait_ge(dve.h, dve0 + 2 * (db - 1))
                else:
                    if pe0a:
                        s.wait_ge(pe.h, pe0a)
                    if act0a:
                        s.wait_ge(act.h, act0a)
                xw.append(s_x[db % 2].inc(s.dma_start(
                    out=xt2[:, (db % 2) * QC:(db % 2) * QC + QC],
                    in_=src[db * P:(db + 1) * P, col0:col0 + QC]), 16))
        SP(fs)

        def fv(v):
            if pe0a:
                v.wait_ge(pe.h, pe0a)
            for db in range(DB):
                sl = (db % 2) * QC
                v.wait_ge(s_x[db % 2].h, xw[db])
                dve.inc(v.tensor_tensor(sq2[:, sl:sl + QC], xt2[:, sl:sl + QC],
                                        psA[:, 0:QC], ALU.mult))
                v.wait_ge(dve.h, dve.n)
                dve.inc(v.tensor_tensor(dst[:, db * span + dcol:db * span + dcol + QC],
                                        sq2[:, sl:sl + QC], psB[:, 0:QC], ALU.subtract))
        DVE(fv)

    KOFF, VOFF = 0, S

    def phase_kv():
        def fs(g):
            s_wkv.inc(g.dma_start(
                out=wbuf[:, 0:DB * P].rearrange("p (db c) -> p db c", db=DB),
                in_=Wk.ap().rearrange("(db p) c -> p db c", p=P)), 16)
            s_wkv.inc(g.dma_start(
                out=wbuf[:, DB * P:2 * DB * P].rearrange("p (db c) -> p db c", db=DB),
                in_=Wv.ap().rearrange("(db p) c -> p db c", p=P)), 16)
        POOL(fs)

        with nc.psum_tensor([P, 2], F32) as psw:
            def fp(t):
                t.wait_ge(s_wkv.h, s_wkv.n)
                for db in range(DB):
                    pe.inc(t.matmul(psw[:, 0:1], wbuf[:, db * P:(db + 1) * P],
                                    ones_bf, start=(db == 0), stop=(db == DB - 1)))
                for db in range(DB):
                    pe.inc(t.matmul(psw[:, 1:2], wbuf[:, (DB + db) * P:(DB + db + 1) * P],
                                    ones_bf, start=(db == 0), stop=(db == DB - 1)))
            PE(fp)

            def fv(v):
                v.wait_ge(pe.h, pe.n)
                dve.inc(v.tensor_copy(small[:, WBARK:WBARK + 1], psw[:, 0:1]))
                v.wait_ge(dve.h, dve.n)
                dve.inc(v.tensor_copy(small[:, WBARV:WBARV + 1], psw[:, 1:2]))
            DVE(fv)

        for c in range(SC):
            col0 = c * QC
            with nc.psum_tensor([P, QC], F32) as psA, \
                 nc.psum_tensor([P, QC], F32) as psB, \
                 nc.psum_tensor([1, QC], F32) as ps_sx, \
                 nc.psum_tensor([1, QC], F32) as ps_sx2, \
                 nc.psum_tensor([P, QC], F32) as psK, \
                 nc.psum_tensor([P, QC], F32) as psV:
                emit_ln_stats(xTb.ap(), col0, psA, psB, ps_sx, ps_sx2, C_EPSLN)

                pe1 = pe.n
                xw2 = []
                act1k = act.n

                def fs2(g, col0=col0):
                    for db in range(DB):
                        if db >= 2:
                            g.wait_ge(pe.h, pe1 + 2 * (db - 1))
                        else:
                            g.wait_ge(pe.h, pe1)
                            g.wait_ge(act.h, act1k)
                        xw2.append(s_xb[db % 2].inc(g.dma_start(
                            out=xb2[:, (db % 2) * QC:(db % 2) * QC + QC],
                            in_=xTb.ap()[db * P:(db + 1) * P, col0:col0 + QC]), 16))
                POOL(fs2)

                def fp2(t):
                    for db in range(DB):
                        xa = xb2[:, (db % 2) * QC:(db % 2) * QC + QC]
                        t.wait_ge(s_xb[db % 2].h, xw2[db])
                        pe.inc(t.matmul(psK[:, 0:QC],
                                        wbuf[:, db * P:(db + 1) * P],
                                        xa, start=(db == 0), stop=(db == DB - 1)))
                        pe.inc(t.matmul(psV[:, 0:QC],
                                        wbuf[:, (DB + db) * P:(DB + db + 1) * P],
                                        xa, start=(db == 0), stop=(db == DB - 1)))
                PE(fp2)

                def fv2(v, col0=col0):
                    v.wait_ge(pe.h, pe.n)
                    dve.inc(v.tensor_copy(xt2[:, 0:QC], psA[:, 0:QC]))
                    v.wait_ge(dve.h, dve.n)
                    dve.inc(v.tensor_scalar(sq2[:, 0:QC], psB[:, 0:QC],
                                            small[:, WBARK:WBARK + 1], None, ALU.mult))
                    v.wait_ge(dve.h, dve.n)
                    dve.inc(v.tensor_scalar(sq2[:, QC:2 * QC], psB[:, 0:QC],
                                            small[:, WBARV:WBARV + 1], None, ALU.mult))
                    v.wait_ge(dve.h, dve.n)
                    dve.inc(v.tensor_tensor(qsc[:, 0:QC], psK[:, 0:QC],
                                            xt2[:, 0:QC], ALU.mult))
                    v.wait_ge(dve.h, dve.n)
                    dve.inc(v.tensor_tensor(qsc[:, QC:2 * QC], psV[:, 0:QC],
                                            xt2[:, 0:QC], ALU.mult))
                    v.wait_ge(dve.h, dve.n)
                    dve.inc(v.scalar_tensor_tensor(
                        kvb[:, KOFF + col0:KOFF + col0 + QC],
                        qsc[:, 0:QC], small[:, BK:BK + 1], sq2[:, 0:QC],
                        ALU.add, ALU.subtract))
                    v.wait_ge(dve.h, dve.n)
                    dve.inc(v.scalar_tensor_tensor(
                        ptil[:, 0:QC],
                        qsc[:, QC:2 * QC], small[:, BV:BV + 1], sq2[:, QC:2 * QC],
                        ALU.add, ALU.subtract))
                DVE(fv2)

                nblk = QC // P
                with nc.psum_tensor([P, P], BF16) as pst:
                    pe2 = pe.n
                    dve2 = dve.n

                    def ft(t, col0=col0):
                        for i in range(nblk):
                            t.wait_ge(dve.h, dve2 + i)
                            pe.inc(t.transpose(pst[:], ptil[:, i * P:(i + 1) * P], ident[:]))
                    PE(ft)

                    def fv3(v, col0=col0):
                        for i in range(nblk):
                            v.wait_ge(pe.h, pe2 + i + 1)
                            dve.inc(v.tensor_copy(
                                kvb[:, VOFF + col0 + i * P:VOFF + col0 + (i + 1) * P],
                                pst[:]))
                    DVE(fv3)

    def emit_oproj(wdram, wreg, wreg_off, av, av_span, av_off, tok0, ntok,
                   bias_off, src, dst, dst_off, wpre_mark=None):
        DQ = max(1, DB // 4)
        nq = (DB + DQ - 1) // DQ
        nch = ntok // QC
        ncols = DQ * P
        wn = [0, 0]
        pe_done = {}
        dve_done = {}
        tiles = [(q, cc, db) for q in range(nq) for cc in range(nch)
                 for db in range(q * DQ, min(DB, (q + 1) * DQ))]
        qtiles = {}
        for i, (q, cc, db) in enumerate(tiles):
            qtiles.setdefault(q, []).append(i)

        pending_w = [None]

        def flush_w():
            if pending_w[0] is not None:
                SP(pending_w[0])
                pending_w[0] = None

        with nc.psum_tensor([P, QC], F32) as po0, \
             nc.psum_tensor([P, QC], F32) as po1:
            pos = [po0, po1]
            for q in range(nq):
                def fg(g, q=q,
                       pm=(wpre_mark if wpre_mark is not None else pe.n)):
                    if q >= 2:
                        g.wait_ge(pe.h, pe_done[qtiles[q - 2][-1]])
                    elif pm:
                        g.wait_ge(pe.h, pm)
                    s_wo[q % 2].inc(g.dma_start(
                        out=wreg[:, wreg_off + (q % 2) * H * ncols:
                                 wreg_off + (q % 2) * H * ncols + H * ncols]
                            .rearrange("p (hb c) -> p hb c", hb=H),
                        in_=wdram.rearrange("(hb p) c -> p hb c", p=P)
                            [:, :, q * ncols:q * ncols + min(ncols, D - q * ncols)]), 16)
                    wn[q % 2] = s_wo[q % 2].n
                POOL(fg)

                for i in qtiles[q]:
                    _, cc, db = tiles[i]
                    col0 = tok0 + cc * QC
                    sl = i % 2
                    woff2 = wreg_off + (q % 2) * H * ncols

                    def fp(t, q=q, cc=cc, db=db, sl=sl, woff2=woff2, i=i,
                           wmark=wn[q % 2], dm0=dve.n):
                        t.wait_ge(s_wo[q % 2].h, wmark)
                        if i >= 2:
                            t.wait_ge(dve.h, dve_done[i - 2])
                        elif dm0:
                            t.wait_ge(dve.h, dm0)
                        for hb in range(H):
                            pe.inc(t.matmul(
                                pos[sl][:, 0:QC],
                                wreg[:, woff2 + hb * ncols + (db - q * DQ) * P:
                                     woff2 + hb * ncols + (db - q * DQ) * P + P],
                                av[:, hb * av_span + av_off + cc * QC:
                                   hb * av_span + av_off + (cc + 1) * QC],
                                start=(hb == 0), stop=(hb == H - 1)))
                        pe_done[i] = pe.n
                    PE(fp)

                    def fs(g, db=db, col0=col0, sl=sl,
                           dm=dve_done.get(i - 2, dve.n)):
                        if dm:
                            g.wait_ge(dve.h, dm)
                        s_x[sl].inc(g.dma_start(
                            out=xt2[:, sl * QC:sl * QC + QC],
                            in_=src[db * P:(db + 1) * P, col0:col0 + QC]), 16)
                    SP(fs)
                    flush_w()
                    xn = s_x[sl].n

                    def fv(v, db=db, sl=sl, i=i):
                        v.wait_ge(pe.h, pe_done[i])
                        v.wait_ge(s_x[sl].h, xn)
                        v.wait_ge(s_resw[sl].h, s_resw[sl].n)
                        if bias_off is not None:
                            dve.inc(v.scalar_tensor_tensor(
                                rest[:, sl * QC:sl * QC + QC], pos[sl][:, 0:QC],
                                small[:, bias_off + db:bias_off + db + 1],
                                xt2[:, sl * QC:sl * QC + QC], ALU.add, ALU.add))
                        else:
                            dve.inc(v.tensor_tensor(
                                rest[:, sl * QC:sl * QC + QC], pos[sl][:, 0:QC],
                                xt2[:, sl * QC:sl * QC + QC], ALU.add))
                        dve_done[i] = dve.n
                    DVE(fv)

                    def fs2(g, db=db, col0=col0, sl=sl, i=i,
                            dd=None):
                        g.wait_ge(dve.h, dve_done[i])
                        s_resw[sl].inc(g.dma_start(
                            out=dst[db * P:(db + 1) * P,
                                    col0 - dst_off:col0 - dst_off + QC],
                            in_=rest[:, sl * QC:sl * QC + QC]), 16)
                    pending_w[0] = fs2
        flush_w()

    def phase_attn_a_slab(slab):
        t0 = slab * ACH
        nch = ACH // QC
        for cc in range(nch):
            col0 = t0 + cc * QC
            with nc.psum_tensor([P, QC], F32) as psA, \
                 nc.psum_tensor([P, QC], F32) as psB, \
                 nc.psum_tensor([1, QC], F32) as ps_sx, \
                 nc.psum_tensor([1, QC], F32) as ps_sx2:
                emit_ln_stats(xTe.ap(), col0, psA, psB, ps_sx, ps_sx2, C_EPSLN)
                emit_ln_apply(xTe.ap(), col0, psA, psB, lnxt, ACH, cc * QC)

        for h in range(H):
            wslot = (h % 2) * DB * P
            pe_before = pe.n

            def fg(g, h=h, wslot=wslot):
                if h >= 2:
                    g.wait_ge(pe.h, pe_before)
                s_w[h % 2].inc(g.dma_start(
                    out=wbuf[:, wslot:wslot + DB * P].rearrange("p (db c) -> p db c", db=DB),
                    in_=Wq.ap().rearrange("(db p) c -> p db c", p=P)[:, :, h * P:(h + 1) * P]), 16)
            POOL(fg)

            with nc.psum_tensor([P, QC], F32) as psq0, \
                 nc.psum_tensor([P, QC], F32) as psq1:
                psqs = [psq0, psq1]
                wsem, wn = s_w[h % 2], s_w[h % 2].n
                pe1 = pe.n
                dve1 = dve.n

                def fp(t, h=h, wslot=wslot):
                    t.wait_ge(wsem.h, wn)
                    for cc in range(nch):
                        if cc >= 2:
                            t.wait_ge(dve.h, dve1 + cc - 1)
                        elif dve1:
                            t.wait_ge(dve.h, dve1)
                        for db in range(DB):
                            pe.inc(t.matmul(
                                psqs[cc % 2][:, 0:QC],
                                wbuf[:, wslot + db * P:wslot + (db + 1) * P],
                                lnxt[:, db * ACH + cc * QC:db * ACH + cc * QC + QC],
                                start=(db == 0), stop=(db == DB - 1)))
                PE(fp)

                def fa(v, h=h):
                    for cc in range(nch):
                        v.wait_ge(pe.h, pe1 + DB * (cc + 1))
                        dve.inc(v.tensor_scalar(
                            bigB[:, h * ACH + cc * QC:h * ACH + cc * QC + QC],
                            psqs[cc % 2][:, 0:QC], scale_a,
                            small[:, BQ + h:BQ + h + 1], ALU.mult, ALU.add))
                DVE(fa)

        preflash_pe = pe.n
        FQ = QC
        for pair in range(H // 2):
            hA, hB = 2 * pair, 2 * pair + 1
            for qi in range(ACH // FQ):
                qoffA = hA * ACH + qi * FQ
                qoffB = hB * ACH + qi * FQ
                with nc.psum_tensor([P, FQ], F32) as psOA, \
                     nc.psum_tensor([P, FQ], F32) as psOB, \
                     nc.psum_tensor([1, FQ], F32) as psDA, \
                     nc.psum_tensor([1, FQ], F32) as psDB, \
                     nc.psum_tensor([P, FQ], F32) as psSA, \
                     nc.psum_tensor([P, FQ], F32) as psSB:
                    pe0, act0, dve0 = pe.n, act.n, dve.n

                    def fp(t):
                        if dve0:
                            t.wait_ge(dve.h, dve0)
                        for kb in range(KB):
                            if kb >= 1:
                                t.wait_ge(act.h, act0 + 2 * (kb - 1) + 1)
                            pe.inc(t.matmul(psSA[:, 0:FQ],
                                            kvb[:, KOFF + kb * P:KOFF + (kb + 1) * P],
                                            bigB[:, qoffA:qoffA + FQ],
                                            start=True, stop=True))
                            if kb >= 1:
                                t.wait_ge(act.h, act0 + 2 * (kb - 1) + 2)
                            pe.inc(t.matmul(psSB[:, 0:FQ],
                                            kvb[:, KOFF + kb * P:KOFF + (kb + 1) * P],
                                            bigB[:, qoffB:qoffB + FQ],
                                            start=True, stop=True))
                            if kb >= 1:
                                for (pD, pO, sl) in ((psDA, psOA, ((kb - 1) % 2)),
                                                     (psDB, psOB, 2 + ((kb - 1) % 2))):
                                    pe.inc(t.matmul(pO[:, 0:FQ],
                                                    kvb[:, VOFF + (kb - 1) * P:VOFF + kb * P],
                                                    ptil[:, sl * FQ:(sl + 1) * FQ],
                                                    start=(kb == 1), stop=False))
                                    pe.inc(t.matmul(pD[0:1, 0:FQ], ones_bf,
                                                    ptil[:, sl * FQ:(sl + 1) * FQ],
                                                    start=(kb == 1), stop=False))
                        t.wait_ge(act.h, act0 + 2 * KB)
                        for (pD, pO, sl) in ((psDA, psOA, ((KB - 1) % 2)),
                                             (psDB, psOB, 2 + ((KB - 1) % 2))):
                            pe.inc(t.matmul(pO[:, 0:FQ],
                                            kvb[:, VOFF + (KB - 1) * P:VOFF + KB * P],
                                            ptil[:, sl * FQ:(sl + 1) * FQ],
                                            start=False, stop=True))
                            pe.inc(t.matmul(pD[0:1, 0:FQ], ones_bf,
                                            ptil[:, sl * FQ:(sl + 1) * FQ],
                                            start=False, stop=True))
                    PE(fp)
                    pe_all = pe.n

                    def fa(s):
                        for kb in range(KB):
                            s.wait_ge(pe.h, pe0 + (1 if kb == 0 else 6 * kb - 3))
                            act.inc(s.activation(
                                ptil[:, (kb % 2) * FQ:(kb % 2) * FQ + FQ],
                                psSA[:, 0:FQ], AF.Exp,
                                bias=small[:, C_NSHIFT:C_NSHIFT + 1]))
                            s.wait_ge(pe.h, pe0 + (2 if kb == 0 else 6 * kb - 2))
                            act.inc(s.activation(
                                ptil[:, (2 + kb % 2) * FQ:(2 + kb % 2) * FQ + FQ],
                                psSB[:, 0:FQ], AF.Exp,
                                bias=small[:, C_NSHIFT:C_NSHIFT + 1]))
                    ACT(fa)

                    with nc.psum_tensor([P, FQ], F32) as psR:
                        for (pD, pO, qoff, rbo) in ((psDA, psOA, qoffA, 0),
                                                    (psDB, psOB, qoffB, FQ)):
                            def ft(v, pD=pD, rbo=rbo):
                                v.wait_ge(pe.h, pe_all)
                                dve.inc(v.reciprocal(
                                    out=rows[0:1, R_DEN + rbo:R_DEN + rbo + FQ],
                                    in_=pD[0:1, 0:FQ]))
                                v.wait_ge(dve.h, dve.n)
                                dve.inc(v.tensor_copy(
                                    rowsb[0:1, RB_DEN + rbo:RB_DEN + rbo + FQ],
                                    rows[0:1, R_DEN + rbo:R_DEN + rbo + FQ]))
                            DVE(ft)

                            def fpr(t, rbo=rbo):
                                t.wait_ge(dve.h, dve.n)
                                pe.inc(t.matmul(psR[:, 0:FQ], ones_row_bf,
                                                rowsb[0:1, RB_DEN + rbo:RB_DEN + rbo + FQ],
                                                start=True, stop=True))
                            PE(fpr)

                            def fvt(v, pO=pO, qoff=qoff, rbo=rbo):
                                v.wait_ge(pe.h, pe.n)
                                dve.inc(v.tensor_copy(rtil[:, rbo:rbo + FQ], psR[:, 0:FQ]))
                                v.wait_ge(dve.h, dve.n)
                                dve.inc(v.tensor_tensor(bigB[:, qoff:qoff + FQ],
                                                        pO[:, 0:FQ], rtil[:, rbo:rbo + FQ],
                                                        ALU.mult))
                            DVE(fvt)

        emit_oproj(Wo.ap(), bigB, H * ACH, bigB, ACH, 0, t0, ACH, BO,
                   xTe.ap(), res.ap(), 0, wpre_mark=preflash_pe)
        join_light()

    W0B_CH = P
    OUT0_OFF = 0
    WB_OFF = DB * FTC

    def phase_ffn_gate_setup():
        def fs(g):
            s_wkv.inc(g.dma_start(
                out=wbuf[:, 0:DB * 2].rearrange("p (db e) -> p db e", e=2),
                in_=WgT.ap().rearrange("(db p) e -> p db e", p=P)), 16)
        POOL(fs)

        def fv(v):
            v.wait_ge(s_wkv.h, s_wkv.n)
            for db in range(DB):
                dve.inc(v.tensor_tensor(
                    smb[:, WGDB + db:WGDB + db + 1],
                    wbuf[:, 2 * db:2 * db + 1],
                    wbuf[:, 2 * db + 1:2 * db + 2],
                    ALU.subtract))
        DVE(fv)

        with nc.psum_tensor([1, 1], F32) as psb:
            def fp(t):
                t.wait_ge(dve.h, dve.n)
                for db in range(DB):
                    pe.inc(t.matmul(psb[0:1, 0:1], smb[:, WGDB + db:WGDB + db + 1],
                                    ones_bf, start=(db == 0), stop=(db == DB - 1)))
            PE(fp)

            def fv2(v):
                v.wait_ge(pe.h, pe.n)
                dve.inc(v.tensor_copy(rows[0:1, R_MISC + 4:R_MISC + 5], psb[0:1, 0:1]))
            DVE(fv2)

    def phase_ffn(src, dst, tok0, ntok, dst_off):
        NCH = ntok // FTC
        NTC = FTC // QC
        wb_free = [pe.n]
        for ch in range(NCH):
            base = tok0 + ch * FTC
            for tc in range(NTC):
                col0 = base + tc * QC
                with nc.psum_tensor([P, QC], F32) as psA, \
                     nc.psum_tensor([P, QC], F32) as psB, \
                     nc.psum_tensor([1, QC], F32) as ps_sx, \
                     nc.psum_tensor([1, QC], F32) as ps_sx2, \
                     nc.psum_tensor([1, QC], F32) as ps_g:
                    emit_ln_stats(src, col0, psA, psB, ps_sx, ps_sx2, C_EPSLN,
                                  gate=True, ps_g=ps_g)
                    def fvg(v, tc=tc):
                        v.wait_ge(pe.h, pe.n)
                        v.wait_ge(dve.h, dve.n)
                        dve.inc(v.tensor_single_scalar(
                            rows[0:1, R_T0:R_T0 + QC], rows[0:1, R_MEAN:R_MEAN + QC],
                            rows[0:1, R_MISC + 4:R_MISC + 5], ALU.mult))
                        v.wait_ge(dve.h, dve.n)
                        dve.inc(v.tensor_tensor(rows[0:1, R_T1:R_T1 + QC],
                                                rows[0:1, R_T1:R_T1 + QC],
                                                rows[0:1, R_T0:R_T0 + QC],
                                                ALU.subtract))
                        v.wait_ge(dve.h, dve.n)
                        dve.inc(v.tensor_single_scalar(
                            rowsb[0:1, RB_MASK:RB_MASK + QC], rows[0:1, R_T1:R_T1 + QC],
                            0.0, ALU.is_ge))
                    DVE(fvg)

                    with nc.psum_tensor([P, QC], F32) as psM:
                        def fpm(t, tc=tc):
                            t.wait_ge(dve.h, dve.n)
                            pe.inc(t.matmul(psM[:, 0:QC], ones_row_bf,
                                            rowsb[0:1, RB_MASK:RB_MASK + QC],
                                            start=True, stop=True))
                        PE(fpm)

                        def fvm(v, tc=tc):
                            v.wait_ge(pe.h, pe.n)
                            dve.inc(v.tensor_copy(mtil[:, tc * QC:(tc + 1) * QC],
                                                  psM[:, 0:QC]))
                        DVE(fvm)

                    emit_ln_apply(src, col0, psA, psB, lnxt, FTC, tc * QC)

            for e in range(2):
                WaT = W0aT if e == 0 else W1aT
                WbT = W0bT if e == 0 else W1bT
                BA = B0A if e == 0 else B1A
                BB = B0B if e == 0 else B1B
                with nc.psum_tensor([P, QC], F32) as ph00, \
                     nc.psum_tensor([P, QC], F32) as ph01, \
                     nc.psum_tensor([P, QC], F32) as ph10, \
                     nc.psum_tensor([P, QC], F32) as ph11:
                    psH = [[ph00, ph01], [ph10, ph11]]
                    pe0, act0 = pe.n, act.n

                    def fgw(g, e=e, WaT=WaT):
                        for fb in range(FB):
                            if fb >= 2:
                                g.wait_ge(pe.h, pe0 + DB * NTC * (fb - 1))
                            else:
                                g.wait_ge(pe.h, pe0)
                            s_w[fb % 2].inc(g.dma_start(
                                out=wbuf[:, (fb % 2) * DB * P:(fb % 2 + 1) * DB * P]
                                    .rearrange("p (db c) -> p db c", db=DB),
                                in_=WaT.ap().rearrange("(db p) c -> p db c", p=P)
                                    [:, :, fb * P:(fb + 1) * P]), 16)
                    POOL(fgw)

                    wn0 = [s_w[0].n - (FB + 1) // 2 * 16, s_w[1].n - FB // 2 * 16]

                    def fp1(t, e=e, dm=dve.n):
                        if dm:
                            t.wait_ge(dve.h, dm)
                        for fb in range(FB):
                            t.wait_ge(s_w[fb % 2].h, wn0[fb % 2] + (fb // 2 + 1) * 16)
                            if fb >= 2:
                                t.wait_ge(act.h, act0 + NTC * (fb - 1))
                            elif act0:
                                t.wait_ge(act.h, act0)
                            for db in range(DB):
                                for tc in range(NTC):
                                    pe.inc(t.matmul(
                                        psH[fb % 2][tc][:, 0:QC],
                                        wbuf[:, (fb % 2) * DB * P + db * P:
                                             (fb % 2) * DB * P + (db + 1) * P],
                                        lnxt[:, db * FTC + tc * QC:db * FTC + (tc + 1) * QC],
                                        start=(db == 0), stop=(db == DB - 1)))
                    PE(fp1)

                    AFUN = AF.Gelu if e == 0 else AF.Relu

                    def fa1(s, e=e, BA=BA, AFUN=AFUN):
                        for fb in range(FB):
                            s.wait_ge(pe.h, pe0 + DB * NTC * fb + DB * NTC)
                            for tc in range(NTC):
                                act.inc(s.activation(
                                    bigB[:, fb * FTC + tc * QC:fb * FTC + (tc + 1) * QC],
                                    psH[fb % 2][tc][:, 0:QC], AFUN,
                                    bias=small[:, BA + fb:BA + fb + 1]))
                    ACT(fa1)

                ndq = D // W0B_CH
                wn2 = [0, 0]
                pe2_done = {}
                dve2_done = {}
                pend2 = [None]

                def flush2():
                    if pend2[0] is not None:
                        SP(pend2[0])
                        pend2[0] = None

                with nc.psum_tensor([P, QC], F32) as q20, \
                     nc.psum_tensor([P, QC], F32) as q21:
                    pq2 = [q20, q21]
                    mm2_pre = wb_free[0]
                    for dq in range(ndq):
                        def fgw2(g, e=e, WbT=WbT, dq=dq, pm=mm2_pre):
                            if dq >= 2:
                                g.wait_ge(pe.h, pe2_done[(dq - 2, NTC - 1)])
                            elif pm:
                                g.wait_ge(pe.h, pm)
                            s_wo[dq % 2].inc(g.dma_start(
                                out=kvb[:, WB_OFF + (dq % 2) * FB * W0B_CH:
                                     WB_OFF + (dq % 2 + 1) * FB * W0B_CH]
                                    .rearrange("p (fb c) -> p fb c", fb=FB),
                                in_=WbT.ap().rearrange("(fb p) c -> p fb c", p=P)
                                    [:, :, dq * W0B_CH:(dq + 1) * W0B_CH]), 16)
                            wn2[dq % 2] = s_wo[dq % 2].n
                        POOL(fgw2)

                        db = dq
                        woff2 = WB_OFF + (dq % 2) * FB * W0B_CH
                        for tc in range(NTC):
                            col0 = base + tc * QC
                            i2 = dq * NTC + tc
                            sl = i2 % 2

                            def fp2(t, db=db, tc=tc, sl=sl, i2=i2, dq=dq,
                                    woff2=woff2, am=act.n, wmark=wn2[dq % 2]):
                                t.wait_ge(s_wo[dq % 2].h, wmark)
                                t.wait_ge(act.h, am)
                                if i2 >= 2:
                                    t.wait_ge(dve.h, dve2_done[i2 - 2])
                                elif dve.n:
                                    t.wait_ge(dve.h, dve.n)
                                for fb in range(FB):
                                    pe.inc(t.matmul(
                                        pq2[sl][:, 0:QC],
                                        kvb[:, woff2 + fb * W0B_CH:
                                             woff2 + fb * W0B_CH + P],
                                        bigB[:, fb * FTC + tc * QC:fb * FTC + (tc + 1) * QC],
                                        start=(fb == 0), stop=(fb == FB - 1)))
                                pe2_done[(dq, tc)] = pe.n
                            PE(fp2)
                            pe2_done_i = None

                            if e == 0:
                                def fv20(v, db=db, tc=tc, BB=BB, sl=sl, i2=i2, dq=dq):
                                    v.wait_ge(pe.h, pe2_done[(dq, tc)])
                                    dve.inc(v.tensor_scalar(
                                        kvb[:, OUT0_OFF + db * FTC + tc * QC:
                                             OUT0_OFF + db * FTC + (tc + 1) * QC],
                                        pq2[sl][:, 0:QC],
                                        small[:, BB + db:BB + db + 1], None,
                                        ALU.add))
                                    dve2_done[i2] = dve.n
                                DVE(fv20)
                            else:
                                def fsx(g, db=db, col0=col0, sl=sl,
                                        dm=dve2_done.get(i2 - 2, dve.n)):
                                    if dm:
                                        g.wait_ge(dve.h, dm)
                                    s_x[sl].inc(g.dma_start(
                                        out=xt2[:, sl * QC:sl * QC + QC],
                                        in_=src[db * P:(db + 1) * P, col0:col0 + QC]), 16)
                                SP(fsx)
                                flush2()
                                xn = s_x[sl].n

                                def fv21(v, db=db, tc=tc, BB=BB, sl=sl, i2=i2,
                                         dq=dq, xn=xn):
                                    v.wait_ge(pe.h, pe2_done[(dq, tc)])
                                    v.wait_ge(s_x[sl].h, xn)
                                    v.wait_ge(s_resw[sl].h, s_resw[sl].n)
                                    dve.inc(v.scalar_tensor_tensor(
                                        sq2[:, sl * QC:sl * QC + QC], pq2[sl][:, 0:QC],
                                        small[:, BB + db:BB + db + 1],
                                        kvb[:, OUT0_OFF + db * FTC + tc * QC:
                                             OUT0_OFF + db * FTC + (tc + 1) * QC],
                                        ALU.add, ALU.subtract))
                                    v.wait_ge(dve.h, dve.n)
                                    dve.inc(v.tensor_tensor(
                                        qsc[:, sl * QC:sl * QC + QC],
                                        sq2[:, sl * QC:sl * QC + QC],
                                        mtil[:, tc * QC:(tc + 1) * QC], ALU.mult))
                                    v.wait_ge(dve.h, dve.n)
                                    dve.inc(v.scalar_tensor_tensor(
                                        sq2[:, sl * QC:sl * QC + QC], pq2[sl][:, 0:QC],
                                        small[:, BB + db:BB + db + 1],
                                        qsc[:, sl * QC:sl * QC + QC],
                                        ALU.add, ALU.subtract))
                                    v.wait_ge(dve.h, dve.n)
                                    dve.inc(v.tensor_tensor(
                                        rest[:, sl * QC:sl * QC + QC],
                                        sq2[:, sl * QC:sl * QC + QC],
                                        xt2[:, sl * QC:sl * QC + QC], ALU.add))
                                    dve2_done[i2] = dve.n
                                DVE(fv21)

                                def fso(g, db=db, col0=col0, dst=dst, sl=sl, i2=i2,
                                        dst_off=dst_off):
                                    g.wait_ge(dve.h, dve2_done[i2])
                                    s_resw[sl].inc(g.dma_start(
                                        out=dst[db * P:(db + 1) * P,
                                                col0 - dst_off:col0 - dst_off + QC],
                                        in_=rest[:, sl * QC:sl * QC + QC]), 16)
                                pend2[0] = fso

                    flush2()
                    wb_free[0] = pe.n

    EB = EXT // P
    NQB2 = NT // 1024 if NT % 1024 == 0 else NT // min(NT, 1024)
    FQ2 = min(1024, NT)
    VB_OFF = KVB * EXT

    def phase_ropetab():
        def fg(g):
            pool.inc(g.memset(sq2[:, 0:P], 0.0))
            g.wait_ge(pool.h, pool.n)
            pool.inc(g.affine_select(out=sq2[:, 0:P], in_=sq2[:, 0:P],
                                     compare_op=ALU.not_equal, fill=1.0,
                                     base=-(P // 2), pattern=[[1, P]],
                                     channel_multiplier=-1))
            g.wait_ge(pool.h, pool.n)
            pool.inc(g.affine_select(out=sq2[:, 0:P], in_=sq2[:, 0:P],
                                     compare_op=ALU.not_equal, fill=-1.0,
                                     base=(P // 2), pattern=[[1, P]],
                                     channel_multiplier=-1))
            g.wait_ge(pool.h, pool.n)
            s_wkv.inc(g.dma_start(out=ropet[:, 0:2 * EXT], in_=ropeT_.ap()), 16)
        POOL(fg)

        def fv0(v):
            v.wait_ge(pool.h, pool.n)
            dve.inc(v.tensor_copy(rtc[:, 0:P], sq2[:, 0:P]))
            v.wait_ge(s_wkv.h, s_wkv.n)
        DVE(fv0)

    def phase_attn_b_proj():
        NCH = EXT // FTC
        NTC = FTC // QC
        qsc4 = qsc

        with nc.psum_tensor([P, QC], F32) as pq0, \
             nc.psum_tensor([P, QC], F32) as pq1, \
             nc.psum_tensor([1, QC], F32) as pn0, \
             nc.psum_tensor([1, QC], F32) as pn1, \
             nc.psum_tensor([P, QC], F32) as pR0, \
             nc.psum_tensor([P, QC], F32) as pR1, \
             nc.psum_tensor([P, QC], F32) as psrot, \
             nc.psum_tensor([P, P], BF16) as pst:
            psq = [pq0, pq1]
            psn = [pn0, pn1]
            psR = [pR0, pR1]
            st = {"i": 0, "dve_done": {}, "pe_done": {}, "rot_done": 0,
                  "pst_done": dve.n, "ln_dve": 0}

            def emit_proj(wslot, wsem, wmark, tc_col):
                i = st["i"]
                sl = i % 2

                def fp(t, i=i, sl=sl, wslot=wslot, tc_col=tc_col, wmark=wmark,
                       dm=max(st["dve_done"].get(i - 2, dve.n), st["ln_dve"])):
                    t.wait_ge(wsem.h, wmark)
                    if dm:
                        t.wait_ge(dve.h, dm)
                    for db in range(DB):
                        pe.inc(t.matmul(
                            psq[sl][:, 0:QC],
                            wbuf[:, wslot + db * P:wslot + (db + 1) * P],
                            lnxt[:, db * FTC + tc_col:db * FTC + tc_col + QC],
                            start=(db == 0), stop=(db == DB - 1)))
                    st["pe_done"][i] = pe.n
                PE(fp)

            def emit_tail_qk(gain_off, tc_abs, dst, dcol):
                i = st["i"]
                sl = i % 2

                def t1(v, i=i, sl=sl):
                    v.wait_ge(pe.h, st["pe_done"][i])
                    dve.inc(v.tensor_copy(qsc4[:, sl * QC:sl * QC + QC],
                                          psq[sl][:, 0:QC]))
                    v.wait_ge(dve.h, dve.n)
                    dve.inc(v.tensor_tensor(sqb[:, sl * QC:sl * QC + QC],
                                            qsc4[:, sl * QC:sl * QC + QC],
                                            qsc4[:, sl * QC:sl * QC + QC],
                                            ALU.mult))
                DVE(t1)
                a1 = dve.n

                def t2(t, i=i, sl=sl):
                    t.wait_ge(dve.h, a1)
                    pe.inc(t.matmul(psn[sl][0:1, 0:QC], ones128_bf,
                                    sqb[:, sl * QC:sl * QC + QC],
                                    start=True, stop=True))
                PE(t2)
                p2 = pe.n

                rT = R_T0 + sl * 512
                rB = sl * 512

                def t45a(v, i=i, sl=sl):
                    v.wait_ge(pe.h, p2)
                    dve.inc(v.tensor_scalar_add(psn[sl][0:1, 0:QC],
                                                psn[sl][0:1, 0:QC], 1e-6))
                DVE(t45a)
                a4 = dve.n

                def t45(s, i=i, rT=rT, rB=rB):
                    s.wait_ge(dve.h, a4)
                    s.wait_ge(act.h, act.n)
                    act.inc(s.activation(rows[0:1, rT:rT + QC],
                                         psn[sl][0:1, 0:QC], AF.Ln))
                ACT(t45)
                a5 = act.n

                def t45m(v, rT=rT):
                    v.wait_ge(act.h, a5)
                    dve.inc(v.tensor_scalar_mul(rows[0:1, rT:rT + QC],
                                                rows[0:1, rT:rT + QC], -0.5))
                DVE(t45m)
                d5 = dve.n

                def t45b(s, rT=rT, rB=rB):
                    s.wait_ge(dve.h, d5)
                    act.inc(s.activation(rowsb[0:1, rB:rB + QC],
                                         rows[0:1, rT:rT + QC], AF.Exp))
                ACT(t45b)
                a5b = act.n

                def t6(t, i=i, sl=sl, rB=rB):
                    t.wait_ge(act.h, a5b)
                    pe.inc(t.matmul(psR[sl][:, 0:QC], ones_row_bf,
                                    rowsb[0:1, rB:rB + QC], start=True, stop=True))
                PE(t6)
                p6 = pe.n

                def t78(v, i=i, sl=sl, gain_off=gain_off):
                    v.wait_ge(pe.h, p6)
                    dve.inc(v.tensor_scalar(qsc4[:, 2 * QC:3 * QC],
                                            qsc4[:, sl * QC:sl * QC + QC],
                                            small[:, gain_off:gain_off + 1], None,
                                            ALU.mult))
                    v.wait_ge(dve.h, dve.n)
                    dve.inc(v.tensor_tensor(ptil[:, sl * QC:sl * QC + QC],
                                            qsc4[:, 2 * QC:3 * QC],
                                            psR[sl][:, 0:QC], ALU.mult))
                DVE(t78)
                d8 = dve.n

                def t9(t, i=i, sl=sl, rd=st["rot_done"]):
                    t.wait_ge(dve.h, d8)
                    if rd:
                        t.wait_ge(dve.h, rd)
                    pe.inc(t.matmul(psrot[:, 0:QC], rtc[:, 0:P],
                                    ptil[:, sl * QC:sl * QC + QC],
                                    start=True, stop=True))
                PE(t9)
                p9 = pe.n

                def t10(v, i=i, sl=sl, tc_abs=tc_abs, dst=dst, dcol=dcol):
                    v.wait_ge(pe.h, p9)
                    dve.inc(v.tensor_tensor(qsc4[:, sl * QC:sl * QC + QC],
                                            ptil[:, sl * QC:sl * QC + QC],
                                            ropet[:, tc_abs:tc_abs + QC], ALU.mult))
                    v.wait_ge(dve.h, dve.n)
                    dve.inc(v.tensor_tensor(qsc4[:, 2 * QC:3 * QC],
                                            psrot[:, 0:QC],
                                            ropet[:, EXT + tc_abs:EXT + tc_abs + QC],
                                            ALU.mult))
                    st["rot_done"] = dve.n
                    v.wait_ge(dve.h, dve.n)
                    dve.inc(v.tensor_tensor(dst[:, dcol:dcol + QC],
                                            qsc4[:, sl * QC:sl * QC + QC],
                                            qsc4[:, 2 * QC:3 * QC],
                                            ALU.add))
                    st["dve_done"][i] = dve.n
                DVE(t10)

            def emit_tail_v(g, tc_abs):
                i = st["i"]
                sl = i % 2

                def v1(v, i=i, sl=sl, pd=st["pst_done"]):
                    v.wait_ge(pe.h, st["pe_done"][i])
                    dve.inc(v.tensor_copy(ptil[:, sl * QC:sl * QC + QC],
                                          psq[sl][:, 0:QC]))
                DVE(v1)
                dv1 = dve.n
                nb = QC // P
                pe_tr = []

                def v2(t, i=i, sl=sl, g=g, tc_abs=tc_abs, dv1=dv1,
                       pd=st["pst_done"]):
                    for j in range(nb):
                        t.wait_ge(dve.h, dv1 + j)
                        if j == 0 and pd:
                            t.wait_ge(dve.h, max(dv1, pd))
                        pe.inc(t.transpose(pst[:],
                                           ptil[:, sl * QC + j * P:sl * QC + (j + 1) * P],
                                           ident[:]))
                        pe_tr.append(pe.n)
                PE(v2)

                def v3(v, i=i, g=g, tc_abs=tc_abs):
                    for j in range(nb):
                        v.wait_ge(pe.h, pe_tr[j])
                        o = VB_OFF + g * EXT + tc_abs + j * P
                        dve.inc(v.tensor_copy(kvb[:, o:o + P], pst[:]))
                    st["pst_done"] = dve.n
                    st["dve_done"][i] = dve.n
                DVE(v3)

            wload = {"n": 0}

            def load_w(wT, blk):
                li = wload["n"]
                wload["n"] += 1
                wslot = (li % 2) * DB * P
                wsem = s_w[li % 2]

                def fg(g_, wT=wT, blk=blk, wslot=wslot,
                       pm=st["pe_done"].get(st["i"] - 2, pe.n)):
                    if pm:
                        g_.wait_ge(pe.h, pm)
                    wsem.inc(g_.dma_start(
                        out=wbuf[:, wslot:wslot + DB * P].rearrange("p (db c) -> p db c", db=DB),
                        in_=wT.ap().rearrange("(db p) c -> p db c", p=P)
                            [:, :, blk * P:(blk + 1) * P]), 16)
                POOL(fg)
                return wslot, wsem, wsem.n

            for ch in range(NCH):
                base = ch * FTC
                for tc in range(NTC):
                    col0 = base + tc * QC
                    emit_ln_stats(res.ap(), col0, pR0, pR1, pn0, pn1, C_EPSLN)
                    emit_ln_apply(res.ap(), col0, pR0, pR1, lnxt, FTC, tc * QC)
                    st["ln_dve"] = dve.n

                for g in range(KVB):
                    wslot, wsem, wmark = load_w(wkT, g)
                    for tc in range(NTC):
                        emit_proj(wslot, wsem, wmark, tc * QC)
                        emit_tail_qk(KNW, base + tc * QC, kvb, g * EXT + base + tc * QC)
                        st["i"] += 1
                    wslot, wsem, wmark = load_w(wvT, g)
                    for tc in range(NTC):
                        emit_proj(wslot, wsem, wmark, tc * QC)
                        emit_tail_v(g, base + tc * QC)
                        st["i"] += 1
                for h in range(H):
                    cols = [tc for tc in range(NTC) if base + tc * QC >= HALO]
                    if not cols:
                        continue
                    wslot, wsem, wmark = load_w(wqT, h)
                    for tc in cols:
                        emit_proj(wslot, wsem, wmark, tc * QC)
                        emit_tail_qk(QNW, base + tc * QC, bigB,
                                     h * NT + (base + tc * QC - HALO))
                        st["i"] += 1

    def phase_flash_b():
        WBK = W // P
        for h in range(H):
            g = h // RH
            for qi in range(NT // FQ2):
                q0 = HALO + qi * FQ2
                kb_lo = (q0 - W + 1) // P
                kb_hi = (q0 + FQ2 - 1) // P
                kbs = list(range(kb_lo, kb_hi + 1))
                nk = len(kbs)
                qoff = h * NT + qi * FQ2

                geo = []
                for kb in kbs:
                    qlo = max(q0, P * kb)
                    qhi = min(q0 + FQ2, P * kb + W + P)
                    subs = []
                    a = qlo
                    while a < qhi:
                        bnd = q0 + ((a - q0) // 512 + 1) * 512
                        b_ = min(qhi, bnd)
                        subs.append((a - q0, b_ - a))
                        a = b_
                    diag = (P * kb >= q0)
                    far = (qhi == P * kb + W + P)
                    halo_k = (kb * P < HALO)
                    geo.append((kb, qlo, qhi, subs, diag, far, halo_k))

                with nc.psum_tensor([P, FQ2], F32) as psO, \
                     nc.psum_tensor([1, FQ2], F32) as psD:
                  with nc.psum_tensor([P, FQ2], F32) as psS0, \
                       nc.psum_tensor([P, FQ2], F32) as psS1:
                    psS = [psS0, psS1]
                    pe0, act0, dve0 = pe.n, act.n, dve.n
                    pe_after_scores = []
                    dve_after_masks = []
                    pe_c = pe0
                    dve_c = dve0
                    for ki, (kb, qlo, qhi, subs, diag, far, halo_k) in enumerate(geo):
                        pe_c += len(subs)
                        pe_after_scores.append(pe_c)
                        dve_c += ((1 if diag else 0) + (1 if far else 0)
                                  + (1 if halo_k else 0))
                        dve_after_masks.append(dve_c)
                        if ki >= 1:
                            pe_c += 2 * len(geo[ki - 1][3])
                    pe_c += 2 * len(geo[-1][3])

                    started_O = set()
                    started_D = set()

                    def fp(t, h=h, g=g, q0=q0):
                        if dve0:
                            t.wait_ge(dve.h, dve0)
                        for ki, (kb, qlo, qhi, subs, diag, far, halo_k) in enumerate(geo):
                            if ki >= 2:
                                t.wait_ge(act.h, act0 + ki - 1)
                            for off, n in subs:
                                pe.inc(t.matmul(
                                    psS[ki % 2][:, off:off + n],
                                    kvb[:, g * EXT + kb * P:g * EXT + (kb + 1) * P],
                                    bigB[:, qoff + off:qoff + off + n],
                                    start=True, stop=True))
                            if ki >= 1:
                                t.wait_ge(act.h, act0 + ki)
                                kbp, qlop, qhip, subsp, _, _, _ = geo[ki - 1]
                                pl = ((ki - 1) % 2) * FQ2
                                for off, n in subsp:
                                    st = (off // 512) not in started_O
                                    started_O.add(off // 512)
                                    pe.inc(t.matmul(
                                        psO[:, off:off + n],
                                        kvb[:, VB_OFF + g * EXT + kbp * P:
                                             VB_OFF + g * EXT + (kbp + 1) * P],
                                        ptil[:, pl + off:pl + off + n],
                                        start=st, stop=False))
                                for off, n in subsp:
                                    st = (off // 512) not in started_D
                                    started_D.add(off // 512)
                                    pe.inc(t.matmul(
                                        psD[0:1, off:off + n], ones_bf,
                                        ptil[:, pl + off:pl + off + n],
                                        start=st, stop=False))
                        t.wait_ge(act.h, act0 + len(geo))
                        kbp, qlop, qhip, subsp, _, _, _ = geo[-1]
                        pl = ((len(geo) - 1) % 2) * FQ2
                        for off, n in subsp:
                            st = (off // 512) not in started_O
                            started_O.add(off // 512)
                            pe.inc(t.matmul(
                                psO[:, off:off + n],
                                kvb[:, VB_OFF + g * EXT + kbp * P:
                                     VB_OFF + g * EXT + (kbp + 1) * P],
                                ptil[:, pl + off:pl + off + n],
                                start=st, stop=True))
                        for off, n in subsp:
                            st = (off // 512) not in started_D
                            started_D.add(off // 512)
                            pe.inc(t.matmul(
                                psD[0:1, off:off + n], ones_bf,
                                ptil[:, pl + off:pl + off + n],
                                start=st, stop=True))
                    PE(fp)

                    def fm(v, q0=q0):
                        for ki, (kb, qlo, qhi, subs, diag, far, halo_k) in enumerate(geo):
                            if not (diag or far or halo_k):
                                continue
                            v.wait_ge(pe.h, pe_after_scores[ki])
                            sl = psS[ki % 2]
                            if diag:
                                o = P * kb - q0
                                dve.inc(v.tensor_tensor(
                                    sl[:, o:o + P], sl[:, o:o + P],
                                    mask2[:, P:2 * P], ALU.add))
                            if far:
                                o = qhi - P - q0
                                dve.inc(v.tensor_tensor(
                                    sl[:, o:o + P], sl[:, o:o + P],
                                    mask2[:, 0:P], ALU.add))
                            if halo_k:
                                if diag or far:
                                    v.wait_ge(dve.h, dve.n)
                                o = qlo - q0
                                dve.inc(v.tensor_scalar(
                                    sl[:, o:qhi - q0], sl[:, o:qhi - q0],
                                    small[:, HALOR:HALOR + 1], None, ALU.add))
                    DVE(fm)

                    def fa(s, q0=q0):
                        for ki, (kb, qlo, qhi, subs, diag, far, halo_k) in enumerate(geo):
                            s.wait_ge(pe.h, pe_after_scores[ki])
                            s.wait_ge(dve.h, dve_after_masks[ki])
                            pl = (ki % 2) * FQ2
                            o = qlo - q0
                            act.inc(s.activation(
                                ptil[:, pl + o:pl + (qhi - q0)],
                                psS[ki % 2][:, o:qhi - q0], AF.Exp,
                                bias=small[:, C_NSHIFT:C_NSHIFT + 1]))
                    ACT(fa)
                    act_all = act.n

                  with nc.psum_tensor([P, FQ2], F32) as psR:
                    def fv(v):
                        v.wait_ge(pe.h, pe.n)
                        dve.inc(v.reciprocal(out=rows[0:1, R_DEN:R_DEN + FQ2],
                                             in_=psD[0:1, 0:FQ2]))
                        v.wait_ge(dve.h, dve.n)
                        dve.inc(v.tensor_copy(rowsb[0:1, RB_DEN:RB_DEN + FQ2],
                                              rows[0:1, R_DEN:R_DEN + FQ2]))
                    DVE(fv)

                    def fp2(t):
                        t.wait_ge(dve.h, dve.n)
                        for ss in range(FQ2 // QC):
                            pe.inc(t.matmul(
                                psR[:, ss * QC:(ss + 1) * QC], ones_row_bf,
                                rowsb[0:1, RB_DEN + ss * QC:RB_DEN + (ss + 1) * QC],
                                start=True, stop=True))
                    PE(fp2)

                    def fv2(v, qoff=qoff):
                        v.wait_ge(pe.h, pe.n)
                        dve.inc(v.tensor_copy(rtil[:, 0:FQ2], psR[:, 0:FQ2]))
                        v.wait_ge(dve.h, dve.n)
                        dve.inc(v.tensor_tensor(bigB[:, qoff:qoff + FQ2],
                                                psO[:, 0:FQ2], rtil[:, 0:FQ2],
                                                ALU.mult))
                    DVE(fv2)

    phase_setup()
    join_all()
    phase_kv()
    join_all()
    for slab in range(NSLAB):
        phase_attn_a_slab(slab)
    phase_ffn_gate_setup()
    join_light()
    phase_ffn(res.ap(), res.ap(), 0, EXT, 0)
    join_light()
    phase_ropetab()
    join_light()
    phase_attn_b_proj()
    join_light()
    phase_flash_b()
    join_light()
    emit_oproj(woT.ap(), kvb, 0, bigB, NT, 0, HALO, NT, None,
               res.ap(), res.ap(), 0)
    join_light()
    phase_ffn(res.ap(), outT.ap(), HALO, NT, HALO)

    def ffinal(s):
        s.wait_ge(s_resw[0].h, s_resw[0].n)
        s.wait_ge(s_resw[1].h, s_resw[1].n)
    SP(ffinal)

    ctx.close()
    return nc



def _rope_tables(cfg, ext0):
    pos = ext0 + np.arange(cfg.EXT, dtype=np.float64)
    inv = 1.0 / (10000.0 ** (np.arange(0, P, 2, dtype=np.float64) / P))
    ang = inv[:, None] * pos[None, :]
    cos = np.cos(ang).astype(np.float32)
    sin = np.sin(ang).astype(np.float32)
    cos2 = np.concatenate([cos, cos], axis=0)
    sin2 = np.concatenate([sin, sin], axis=0)
    return np.concatenate([cos2, sin2], axis=1)


def make_core_inputs(cfg, inp, core):
    bsz = 4
    bidx, rank = divmod(core, bsz)
    t0 = rank * cfg.NT
    x = inp["L0_attn_a_x"]
    xT = np.ascontiguousarray(np.transpose(x[bidx]))
    ext0 = t0 - cfg.HALO
    if ext0 < 0:
        pad = np.zeros((cfg.D, -ext0), np.float32)
        xTe = np.concatenate([pad, xT[:, 0:t0 + cfg.NT]], axis=1)
    else:
        xTe = xT[:, ext0:t0 + cfg.NT]
    halo_valid = 0.0 if ext0 >= 0 else -30000.0
    d = {
        "xTb": xT,
        "xTe": np.ascontiguousarray(xTe),
        "Wq": inp["L0_attn_a_Wq"],
        "Wk": inp["L0_attn_a_Wk"],
        "Wv": inp["L0_attn_a_Wv"],
        "Wo": inp["L0_attn_a_Wo"],
        "bqT": np.ascontiguousarray(inp["L0_attn_a_bq"].reshape(cfg.H, P).T),
        "bk": inp["L0_attn_a_bk"].reshape(P, 1),
        "bv": inp["L0_attn_a_bv"].reshape(P, 1),
        "boT": np.ascontiguousarray(inp["L0_attn_a_bo"].reshape(cfg.DB, P).T),
        "WgT": np.ascontiguousarray(inp["L0_ffn_Wg"].T),
        "W0aT": np.ascontiguousarray(inp["L0_ffn_W0a"].T),
        "W0bT": np.ascontiguousarray(inp["L0_ffn_W0b"].T),
        "W1aT": np.ascontiguousarray(inp["L0_ffn_W1a"].T),
        "W1bT": np.ascontiguousarray(inp["L0_ffn_W1b"].T),
        "b0aT": np.ascontiguousarray(inp["L0_ffn_b0a"].reshape(cfg.FB, P).T),
        "b0bT": np.ascontiguousarray(inp["L0_ffn_b0b"].reshape(cfg.DB, P).T),
        "b1aT": np.ascontiguousarray(inp["L0_ffn_b1a"].reshape(cfg.FB, P).T),
        "b1bT": np.ascontiguousarray(inp["L0_ffn_b1b"].reshape(cfg.DB, P).T),
        "wqT": np.ascontiguousarray(inp["L0_attn_b_wq"].T),
        "wkT": np.ascontiguousarray(inp["L0_attn_b_wk"].T),
        "wvT": np.ascontiguousarray(inp["L0_attn_b_wv"].T),
        "woT": np.ascontiguousarray(inp["L0_attn_b_wo"].T),
        "qnw": inp["L0_attn_b_q_norm_w"].reshape(P, 1),
        "knw": inp["L0_attn_b_k_norm_w"].reshape(P, 1),
        "pos0": np.array([[float(ext0)]], np.float32),
        "ropeT": _rope_tables(cfg, ext0),
        "halob": np.full((P, 1), halo_valid, np.float32),
    }
    return {k: np.ascontiguousarray(np.asarray(v, np.float32)) for k, v in d.items()}


def kernel(**inputs):
    cfg = Cfg()
    nc = build_program(cfg)
    in_maps = [make_core_inputs(cfg, inputs, c) for c in range(8)]
    r = run_bass_kernel_spmd(nc, in_maps, core_ids=list(range(8)))
    outs = []
    for b in range(2):
        cores = [r.results[b * 4 + i]["outT"].T for i in range(4)]
        outs.append(np.concatenate(cores, axis=0))
    return np.stack(outs, axis=0).astype(np.float32)


# version 22
# speedup vs baseline: 1.0556x; 1.0556x over previous
import math
from contextlib import ExitStack

import numpy as np

import concourse.bass as bass
import concourse.mybir as mybir
from concourse.bass_utils import run_bass_kernel_spmd

F32 = mybir.dt.float32
BF16 = mybir.dt.bfloat16
F32R = mybir.dt.float32r
AF = mybir.ActivationFunctionType
ALU = mybir.AluOpType
AX = mybir.AxisListType

P = 128
CSHIFT = 12.0


class Cfg:
    def __init__(self, D=2048, S=8192, NT=2048, HALO=1024, H=16, KVB=4, F=4096,
                 QC=512, ACH=1024, FTC=1024):
        self.D, self.S, self.NT, self.HALO = D, S, NT, HALO
        self.H, self.KVB, self.F = H, KVB, F
        self.QC, self.ACH, self.FTC = QC, ACH, FTC
        self.DB = D // P
        self.EXT = NT + HALO
        self.FB = F // P
        self.W = HALO
        self.RH = H // KVB
        assert self.EXT % ACH == 0 and ACH % QC == 0 and S % QC == 0
        assert self.EXT % FTC == 0 and FTC % QC == 0 and NT % FTC == 0
        assert HALO % P == 0 and NT % P == 0


class Sem:
    __slots__ = ("h", "n")

    def __init__(self, nc, name):
        self.h = nc.alloc_semaphore(name=name)
        self.n = 0

    def inc(self, instr, k=1):
        instr.then_inc(self.h, k)
        self.n += k
        return self.n


def build_program(cfg: Cfg):
    nc = bass.Bass("TRN2", target_bir_lowering=False, debug=False)
    ctx = ExitStack()
    D, DB, S, NT, HALO, EXT = cfg.D, cfg.DB, cfg.S, cfg.NT, cfg.HALO, cfg.EXT
    H, KVB, F, FB, RH = cfg.H, cfg.KVB, cfg.F, cfg.FB, cfg.RH
    QC, ACH, FTC, W = cfg.QC, cfg.ACH, cfg.FTC, cfg.W
    SC = S // QC
    NSLAB = EXT // ACH
    KB = S // P
    WB = W // P

    dram = {}

    def din(name, shape, dtype=F32):
        dram[name] = nc.dram_tensor(name, list(shape), dtype, kind="ExternalInput")
        return dram[name]

    xTb = din("xTb", (D, S))
    xTe = din("xTe", (D, EXT))
    Wq = din("Wq", (D, H * P))
    Wk = din("Wk", (D, P))
    Wv = din("Wv", (D, P))
    Wo = din("Wo", (H * P, D))
    bqT = din("bqT", (P, H))
    bk_ = din("bk", (P, 1))
    bv_ = din("bv", (P, 1))
    boT = din("boT", (P, DB))
    WgT = din("WgT", (D, 2))
    W0aT = din("W0aT", (D, F))
    W0bT = din("W0bT", (F, D))
    W1aT = din("W1aT", (D, F))
    W1bT = din("W1bT", (F, D))
    b0aT = din("b0aT", (P, FB))
    b0bT = din("b0bT", (P, DB))
    b1aT = din("b1aT", (P, FB))
    b1bT = din("b1bT", (P, DB))
    wqT = din("wqT", (D, H * P))
    wkT = din("wkT", (D, KVB * P))
    wvT = din("wvT", (D, KVB * P))
    woT = din("woT", (H * P, D))
    qnw_ = din("qnw", (P, 1))
    knw_ = din("knw", (P, 1))
    pos0_ = din("pos0", (1, 1))
    ropeT_ = din("ropeT", (P, 2 * (NT + HALO)))
    halob_ = din("halob", (P, 1))

    res = nc.dram_tensor("res", [D, EXT], F32)
    outT = nc.dram_tensor("outT", [D, NT], F32, kind="ExternalOutput")

    sbufs = {}

    def sb(name, shape, dtype):
        sbufs[name] = ctx.enter_context(nc.sbuf_tensor(name, list(shape), dtype))
        return sbufs[name]

    lnxt = sb("lnxt", (P, DB * max(ACH, FTC)), BF16)
    WQCH = max(1, DB // 4) * P
    bigB = sb("bigB", (P, max(H * ACH + 2 * H * WQCH, FB * FTC,
                              H * NT, 2 * H * WQCH)), BF16)
    kvb = sb("kvb", (P, max(2 * S, 2 * KVB * EXT, DB * FTC + FB * 256)), BF16)
    wbuf = sb("wbuf", (P, 2 * DB * P), BF16)
    xt2 = sb("xt2", (P, 2 * QC), F32)
    xb2 = sb("xb2", (P, 2 * QC), BF16)
    sqb = sb("sqb", (P, 2 * QC), BF16)
    sq2 = sb("sq2", (P, 2 * QC), F32)
    rest = sb("rest", (P, 2 * QC), F32)
    ptil = sb("ptil", (P, max(4 * QC, 2 * ACH, 2 * min(1024, NT))), BF16)
    rtil = sb("rtil", (P, max(2 * QC, ACH, min(1024, NT))), BF16)
    mtil = sb("mtil", (P, FTC), BF16)
    qsc = sb("qsc", (P, 3 * QC), F32)
    ropet = sb("ropet", (P, 2 * EXT), BF16)
    rtc = sb("rtc", (P, P), BF16)
    rows = sb("rows", (1, 2112), F32)
    rowsb = sb("rowsb", (1, 1024), BF16)
    small = sb("small", (P, 288), F32)
    smb = sb("smb", (P, 160), BF16)
    mask2 = sb("mask2", (P, 2 * P), BF16)
    ident = sb("ident", (P, P), BF16)

    ONES = 0
    BQ = 1
    BK = BQ + H
    BV = BK + 1
    BO = BV + 1
    B0A = BO + DB
    B0B = B0A + FB
    B1A = B0B + DB
    B1B = B1A + FB
    QNW = B1B + DB
    KNW = QNW + 1
    HALOB = KNW + 1
    WBARK = HALOB + 1
    WBARV = WBARK + 1
    WGD = WBARV + 1
    ONE_ROW_P0 = WGD + DB
    C_NSHIFT = ONE_ROW_P0 + P
    C_EPSLN = C_NSHIFT + 1
    C_EPSRMS = C_EPSLN + 1
    C_EPSRMS128 = C_EPSRMS + 1
    C_LNHALF128 = C_EPSRMS128 + 1
    HALOR = C_LNHALF128 + 1
    assert HALOR + 1 <= 288

    R_MEAN, R_RSTD, R_T0, R_T1 = 0, 512, 1024, 1536
    R_DEN = 1024
    R_MISC = 2048
    RB_RSTD, RB_MB, RB_DEN, RB_MASK = 0, 512, 0, 0


    pe = Sem(nc, "pe")
    act = Sem(nc, "act")
    dve = Sem(nc, "dve")
    pool = Sem(nc, "pool")
    s_bias = Sem(nc, "s_bias")
    s_wkv = Sem(nc, "s_wkv")
    s_x = [Sem(nc, "s_x0"), Sem(nc, "s_x1")]
    s_xb = [Sem(nc, "s_xb0"), Sem(nc, "s_xb1")]
    s_resw = [Sem(nc, "s_resw0"), Sem(nc, "s_resw1")]
    s_w = [Sem(nc, "s_w0"), Sem(nc, "s_w1")]
    s_wo = [Sem(nc, "s_wo0"), Sem(nc, "s_wo1")]
    s_dbg = Sem(nc, "s_dbg")
    all_dma_sems = [s_bias, s_wkv, s_x[0], s_x[1], s_xb[0], s_xb[1],
                    s_resw[0], s_resw[1], s_w[0], s_w[1], s_wo[0], s_wo[1],
                    s_dbg]

    block = ctx.enter_context(nc.Block())

    ones_col = small[:, ONES:ONES + 1]
    ones_row = small[0:1, ONE_ROW_P0:ONE_ROW_P0 + P]
    ones_bf = smb[:, 0:1]
    ones128_bf = smb[:, 1:2]
    ones_row_bf = smb[0:1, 8:8 + P]
    WGDB = 140
    scale_a = 1.0 / math.sqrt(P)


    def PE(f): block.tensor(f)
    def ACT(f): block.scalar(f)
    def DVE(f): block.vector(f)
    def POOL(f): block.gpsimd(f)
    def SP(f): block.sync(f)

    _pin_rot = [0]

    def act_set_pin(s):
        c = R_MISC + 8 + (_pin_rot[0] % 8)
        _pin_rot[0] += 1
        s.activation(rows[0:1, c:c + 1],
                     small[0:1, C_EPSRMS:C_EPSRMS + 1], AF.Ln)

    def join_all():
        marks = [(s, s.n) for s in ([pe, act, dve, pool] + all_dma_sems)]

        def mk():
            def f(e):
                for s, n in marks:
                    if n:
                        e.wait_ge(s.h, n)
            return f
        PE(mk()); ACT(mk()); DVE(mk()); POOL(mk()); SP(mk())

    def join_light():
        marks = [(s, s.n) for s in (pe, act, dve, pool, s_resw[0], s_resw[1],
                                    s_x[0], s_x[1], s_xb[0], s_xb[1])]

        def mk():
            def f(e):
                for s, n in marks:
                    if n:
                        e.wait_ge(s.h, n)
            return f
        PE(mk()); ACT(mk()); DVE(mk()); POOL(mk()); SP(mk())

    def phase_setup():
        def f(g):
            pool.inc(g.memset(small[:, ONES:ONES + 1], 1.0))
            pool.inc(g.memset(small[0:1, ONE_ROW_P0:ONE_ROW_P0 + P], 1.0))
            pool.inc(g.memset(smb[:, 0:1], 1.0))
            pool.inc(g.memset(smb[:, 1:2], 1.0 / 128.0))
            pool.inc(g.memset(smb[0:1, 8:8 + P], 1.0))
            pool.inc(g.memset(small[:, C_NSHIFT:C_NSHIFT + 1], -CSHIFT))
            pool.inc(g.memset(small[:, C_EPSLN:C_EPSLN + 1], 1e-5))
            pool.inc(g.memset(small[:, C_EPSRMS:C_EPSRMS + 1], 1e-6))
            pool.inc(g.memset(small[:, C_EPSRMS128:C_EPSRMS128 + 1], 128e-6))
            pool.inc(g.memset(small[:, C_LNHALF128:C_LNHALF128 + 1],
                              0.5 * math.log(128.0)))
            pool.inc(g.memset(ident[:], 0.0))
            pool.inc(g.memset(mask2[:, 0:P], 0.0))
            pool.inc(g.memset(mask2[:, P:2 * P], 0.0))
            g.wait_ge(pool.h, pool.n)
            pool.inc(g.affine_select(out=ident[:], in_=ident[:],
                                     compare_op=ALU.not_equal, fill=1.0,
                                     base=0, pattern=[[-1, P]], channel_multiplier=1))
            pool.inc(g.affine_select(out=mask2[:, 0:P], in_=mask2[:, 0:P],
                                     compare_op=ALU.is_ge, fill=-30000.0,
                                     base=-1, pattern=[[-1, P]], channel_multiplier=1))
            pool.inc(g.affine_select(out=mask2[:, P:2 * P], in_=mask2[:, P:2 * P],
                                     compare_op=ALU.is_ge, fill=-30000.0,
                                     base=0, pattern=[[1, P]], channel_multiplier=-1))
        POOL(f)

        def fs(s):
            s_bias.inc(s.dma_start(out=small[:, BQ:BQ + H], in_=bqT.ap()), 16)
            s_bias.inc(s.dma_start(out=small[:, BK:BK + 1], in_=bk_.ap()), 16)
            s_bias.inc(s.dma_start(out=small[:, BV:BV + 1], in_=bv_.ap()), 16)
            s_bias.inc(s.dma_start(out=small[:, BO:BO + DB], in_=boT.ap()), 16)
            s_bias.inc(s.dma_start(out=small[:, B0A:B0A + FB], in_=b0aT.ap()), 16)
            s_bias.inc(s.dma_start(out=small[:, B0B:B0B + DB], in_=b0bT.ap()), 16)
            s_bias.inc(s.dma_start(out=small[:, B1A:B1A + FB], in_=b1aT.ap()), 16)
            s_bias.inc(s.dma_start(out=small[:, B1B:B1B + DB], in_=b1bT.ap()), 16)
            s_bias.inc(s.dma_start(out=small[:, QNW:QNW + 1], in_=qnw_.ap()), 16)
            s_bias.inc(s.dma_start(out=small[:, KNW:KNW + 1], in_=knw_.ap()), 16)
            s_bias.inc(s.dma_start(out=small[:, HALOB:HALOB + 1], in_=halob_.ap()), 16)
            s_bias.inc(s.dma_start(out=rows[0:1, R_MISC:R_MISC + 1], in_=pos0_.ap()), 16)
        SP(fs)

        def fv(v):
            v.wait_ge(s_bias.h, s_bias.n)
            dve.inc(v.tensor_scalar_mul(small[:, BQ:BQ + H], small[:, BQ:BQ + H], scale_a))
            v.wait_ge(dve.h, dve.n)
            dve.inc(v.tensor_scalar_mul(small[:, QNW:QNW + 1], small[:, QNW:QNW + 1], scale_a))
            v.wait_ge(dve.h, dve.n)
            dve.inc(v.tensor_copy(small[:, HALOR:HALOR + 1], small[:, HALOB:HALOB + 1]))
            v.wait_ge(dve.h, dve.n)
            dve.inc(v.tensor_scalar_add(small[:, HALOB:HALOB + 1], small[:, HALOB:HALOB + 1], -CSHIFT))
        DVE(fv)

    def emit_ln_stats(src, col0, psA, psB, ps_sx, ps_sx2, eps_col, gate=False, ps_g=None):
        act0, pe0, dve0 = act.n, pe.n, dve.n
        per_db = 3 if gate else 2
        xw = []

        def fs(g):
            for db in range(DB):
                if db >= 2:
                    g.wait_ge(dve.h, dve0 + db - 1)
                    g.wait_ge(pe.h, pe0 + per_db * (db - 1))
                else:
                    if act0:
                        g.wait_ge(act.h, act0)
                    if pe0:
                        g.wait_ge(pe.h, pe0)
                    if dve0:
                        g.wait_ge(dve.h, dve0)
                xw.append(s_xb[db % 2].inc(g.dma_start(
                    out=xb2[:, (db % 2) * QC:(db % 2) * QC + QC],
                    in_=src[db * P:(db + 1) * P, col0:col0 + QC]), 16))
        POOL(fs)

        def fa(v):
            for db in range(DB):
                v.wait_ge(s_xb[db % 2].h, xw[db])
                if db >= 2:
                    v.wait_ge(pe.h, pe0 + per_db * (db - 1))
                dve.inc(v.tensor_tensor(sqb[:, (db % 2) * QC:(db % 2) * QC + QC],
                                        xb2[:, (db % 2) * QC:(db % 2) * QC + QC],
                                        xb2[:, (db % 2) * QC:(db % 2) * QC + QC],
                                        ALU.mult))
        DVE(fa)

        def fp(t):
            for db in range(DB):
                sl = (db % 2) * QC
                t.wait_ge(s_xb[db % 2].h, xw[db])
                pe.inc(t.matmul(ps_sx[0:1, 0:QC], ones_bf, xb2[:, sl:sl + QC],
                                start=(db == 0), stop=(db == DB - 1)))
                if gate:
                    pe.inc(t.matmul(ps_g[0:1, 0:QC], smb[:, WGDB + db:WGDB + db + 1],
                                    xb2[:, sl:sl + QC],
                                    start=(db == 0), stop=(db == DB - 1)))
                t.wait_ge(dve.h, dve0 + db + 1)
                pe.inc(t.matmul(ps_sx2[0:1, 0:QC], ones_bf, sqb[:, sl:sl + QC],
                                start=(db == 0), stop=(db == DB - 1)))
        PE(fp)

        def fv(v):
            v.wait_ge(pe.h, pe.n)
            dve.inc(v.tensor_scalar_mul(rows[0:1, R_MEAN:R_MEAN + QC],
                                        ps_sx[0:1, 0:QC], 1.0 / D))
            if gate:
                v.wait_ge(dve.h, dve.n)
                dve.inc(v.tensor_copy(rows[0:1, R_T1:R_T1 + QC], ps_g[0:1, 0:QC]))
            v.wait_ge(dve.h, dve.n)
            dve.inc(v.tensor_scalar_mul(rows[0:1, R_T0:R_T0 + QC],
                                        ps_sx2[0:1, 0:QC], 1.0 / D))
            v.wait_ge(dve.h, dve.n)
            dve.inc(v.tensor_tensor(rows[0:1, R_RSTD:R_RSTD + QC],
                                    rows[0:1, R_MEAN:R_MEAN + QC],
                                    rows[0:1, R_MEAN:R_MEAN + QC], ALU.mult))
            v.wait_ge(dve.h, dve.n)
            dve.inc(v.scalar_tensor_tensor(rows[0:1, R_T0:R_T0 + QC],
                                           rows[0:1, R_T0:R_T0 + QC],
                                           small[0:1, eps_col:eps_col + 1],
                                           rows[0:1, R_RSTD:R_RSTD + QC],
                                           ALU.add, ALU.subtract))
        DVE(fv)

        def fa2(s):
            s.wait_ge(dve.h, dve.n)
            act.inc(s.activation(rows[0:1, R_T0:R_T0 + QC],
                                 rows[0:1, R_T0:R_T0 + QC], AF.Ln))
            s.wait_ge(act.h, act.n)
            act.inc(s.activation(rows[0:1, R_RSTD:R_RSTD + QC],
                                 rows[0:1, R_T0:R_T0 + QC], AF.Exp, scale=-0.5))
            s.wait_ge(act.h, act.n)
            act.inc(s.activation(rowsb[0:1, RB_RSTD:RB_RSTD + QC],
                                 rows[0:1, R_T0:R_T0 + QC], AF.Exp, scale=-0.5))
        ACT(fa2)

        def fv2(v):
            v.wait_ge(act.h, act.n)
            dve.inc(v.tensor_tensor(rowsb[0:1, RB_MB:RB_MB + QC],
                                    rows[0:1, R_MEAN:R_MEAN + QC],
                                    rows[0:1, R_RSTD:R_RSTD + QC], ALU.mult))
        DVE(fv2)

        def fp2(t):
            t.wait_ge(dve.h, dve.n)
            pe.inc(t.matmul(psA[:, 0:QC], ones_row_bf,
                            rowsb[0:1, RB_RSTD:RB_RSTD + QC],
                            start=True, stop=True))
            pe.inc(t.matmul(psB[:, 0:QC], ones_row_bf,
                            rowsb[0:1, RB_MB:RB_MB + QC],
                            start=True, stop=True))
        PE(fp2)

    def emit_ln_apply(src, col0, psA, psB, dst, span, dcol):
        dve0, pe0a, act0a = dve.n, pe.n, act.n
        xw = []

        def fs(s):
            for db in range(DB):
                if db >= 2:
                    s.wait_ge(dve.h, dve0 + 2 * (db - 1))
                else:
                    if pe0a:
                        s.wait_ge(pe.h, pe0a)
                    if act0a:
                        s.wait_ge(act.h, act0a)
                xw.append(s_x[db % 2].inc(s.dma_start(
                    out=xt2[:, (db % 2) * QC:(db % 2) * QC + QC],
                    in_=src[db * P:(db + 1) * P, col0:col0 + QC]), 16))
        SP(fs)

        def fv(v):
            if pe0a:
                v.wait_ge(pe.h, pe0a)
            for db in range(DB):
                sl = (db % 2) * QC
                v.wait_ge(s_x[db % 2].h, xw[db])
                dve.inc(v.tensor_tensor(sq2[:, sl:sl + QC], xt2[:, sl:sl + QC],
                                        psA[:, 0:QC], ALU.mult))
                v.wait_ge(dve.h, dve.n)
                dve.inc(v.tensor_tensor(dst[:, db * span + dcol:db * span + dcol + QC],
                                        sq2[:, sl:sl + QC], psB[:, 0:QC], ALU.subtract))
        DVE(fv)

    KOFF, VOFF = 0, S

    def phase_kv():
        def fs(g):
            s_wkv.inc(g.dma_start(
                out=wbuf[:, 0:DB * P].rearrange("p (db c) -> p db c", db=DB),
                in_=Wk.ap().rearrange("(db p) c -> p db c", p=P)), 16)
            s_wkv.inc(g.dma_start(
                out=wbuf[:, DB * P:2 * DB * P].rearrange("p (db c) -> p db c", db=DB),
                in_=Wv.ap().rearrange("(db p) c -> p db c", p=P)), 16)
        POOL(fs)

        with nc.psum_tensor([P, 2], F32) as psw:
            def fp(t):
                t.wait_ge(s_wkv.h, s_wkv.n)
                for db in range(DB):
                    pe.inc(t.matmul(psw[:, 0:1], wbuf[:, db * P:(db + 1) * P],
                                    ones_bf, start=(db == 0), stop=(db == DB - 1)))
                for db in range(DB):
                    pe.inc(t.matmul(psw[:, 1:2], wbuf[:, (DB + db) * P:(DB + db + 1) * P],
                                    ones_bf, start=(db == 0), stop=(db == DB - 1)))
            PE(fp)

            def fv(v):
                v.wait_ge(pe.h, pe.n)
                dve.inc(v.tensor_copy(small[:, WBARK:WBARK + 1], psw[:, 0:1]))
                v.wait_ge(dve.h, dve.n)
                dve.inc(v.tensor_copy(small[:, WBARV:WBARV + 1], psw[:, 1:2]))
            DVE(fv)

        for c in range(SC):
            col0 = c * QC
            with nc.psum_tensor([P, QC], F32) as psA, \
                 nc.psum_tensor([P, QC], F32) as psB, \
                 nc.psum_tensor([1, QC], F32) as ps_sx, \
                 nc.psum_tensor([1, QC], F32) as ps_sx2, \
                 nc.psum_tensor([P, QC], F32) as psK, \
                 nc.psum_tensor([P, QC], F32) as psV:
                emit_ln_stats(xTb.ap(), col0, psA, psB, ps_sx, ps_sx2, C_EPSLN)

                pe1 = pe.n
                xw2 = []
                act1k = act.n

                def fs2(g, col0=col0):
                    for db in range(DB):
                        if db >= 2:
                            g.wait_ge(pe.h, pe1 + 2 * (db - 1))
                        else:
                            g.wait_ge(pe.h, pe1)
                            g.wait_ge(act.h, act1k)
                        xw2.append(s_xb[db % 2].inc(g.dma_start(
                            out=xb2[:, (db % 2) * QC:(db % 2) * QC + QC],
                            in_=xTb.ap()[db * P:(db + 1) * P, col0:col0 + QC]), 16))
                POOL(fs2)

                def fp2(t):
                    for db in range(DB):
                        xa = xb2[:, (db % 2) * QC:(db % 2) * QC + QC]
                        t.wait_ge(s_xb[db % 2].h, xw2[db])
                        pe.inc(t.matmul(psK[:, 0:QC],
                                        wbuf[:, db * P:(db + 1) * P],
                                        xa, start=(db == 0), stop=(db == DB - 1)))
                        pe.inc(t.matmul(psV[:, 0:QC],
                                        wbuf[:, (DB + db) * P:(DB + db + 1) * P],
                                        xa, start=(db == 0), stop=(db == DB - 1)))
                PE(fp2)

                def fv2(v, col0=col0):
                    v.wait_ge(pe.h, pe.n)
                    dve.inc(v.tensor_copy(xt2[:, 0:QC], psA[:, 0:QC]))
                    v.wait_ge(dve.h, dve.n)
                    dve.inc(v.tensor_scalar(sq2[:, 0:QC], psB[:, 0:QC],
                                            small[:, WBARK:WBARK + 1], None, ALU.mult))
                    v.wait_ge(dve.h, dve.n)
                    dve.inc(v.tensor_scalar(sq2[:, QC:2 * QC], psB[:, 0:QC],
                                            small[:, WBARV:WBARV + 1], None, ALU.mult))
                    v.wait_ge(dve.h, dve.n)
                    dve.inc(v.tensor_tensor(qsc[:, 0:QC], psK[:, 0:QC],
                                            xt2[:, 0:QC], ALU.mult))
                    v.wait_ge(dve.h, dve.n)
                    dve.inc(v.tensor_tensor(qsc[:, QC:2 * QC], psV[:, 0:QC],
                                            xt2[:, 0:QC], ALU.mult))
                    v.wait_ge(dve.h, dve.n)
                    dve.inc(v.scalar_tensor_tensor(
                        kvb[:, KOFF + col0:KOFF + col0 + QC],
                        qsc[:, 0:QC], small[:, BK:BK + 1], sq2[:, 0:QC],
                        ALU.add, ALU.subtract))
                    v.wait_ge(dve.h, dve.n)
                    dve.inc(v.scalar_tensor_tensor(
                        ptil[:, 0:QC],
                        qsc[:, QC:2 * QC], small[:, BV:BV + 1], sq2[:, QC:2 * QC],
                        ALU.add, ALU.subtract))
                DVE(fv2)

                nblk = QC // P
                with nc.psum_tensor([P, P], BF16) as pst:
                    pe2 = pe.n
                    dve2 = dve.n

                    def ft(t, col0=col0):
                        for i in range(nblk):
                            t.wait_ge(dve.h, dve2 + i)
                            pe.inc(t.transpose(pst[:], ptil[:, i * P:(i + 1) * P], ident[:]))
                    PE(ft)

                    def fv3(v, col0=col0):
                        for i in range(nblk):
                            v.wait_ge(pe.h, pe2 + i + 1)
                            dve.inc(v.tensor_copy(
                                kvb[:, VOFF + col0 + i * P:VOFF + col0 + (i + 1) * P],
                                pst[:]))
                    DVE(fv3)

    def emit_oproj(wdram, wreg, wreg_off, av, av_span, av_off, tok0, ntok,
                   bias_off, src, dst, dst_off, wpre_mark=None):
        DQ = max(1, DB // 4)
        nq = (DB + DQ - 1) // DQ
        nch = ntok // QC
        ncols = DQ * P
        wn = [0, 0]
        pe_done = {}
        dve_done = {}
        tiles = [(q, cc, db) for q in range(nq) for cc in range(nch)
                 for db in range(q * DQ, min(DB, (q + 1) * DQ))]
        qtiles = {}
        for i, (q, cc, db) in enumerate(tiles):
            qtiles.setdefault(q, []).append(i)

        pending_w = [None]

        def flush_w():
            if pending_w[0] is not None:
                SP(pending_w[0])
                pending_w[0] = None

        with nc.psum_tensor([P, QC], F32) as po0, \
             nc.psum_tensor([P, QC], F32) as po1:
            pos = [po0, po1]
            for q in range(nq):
                def fg(g, q=q,
                       pm=(wpre_mark if wpre_mark is not None else pe.n)):
                    if q >= 2:
                        g.wait_ge(pe.h, pe_done[qtiles[q - 2][-1]])
                    elif pm:
                        g.wait_ge(pe.h, pm)
                    s_wo[q % 2].inc(g.dma_start(
                        out=wreg[:, wreg_off + (q % 2) * H * ncols:
                                 wreg_off + (q % 2) * H * ncols + H * ncols]
                            .rearrange("p (hb c) -> p hb c", hb=H),
                        in_=wdram.rearrange("(hb p) c -> p hb c", p=P)
                            [:, :, q * ncols:q * ncols + min(ncols, D - q * ncols)]), 16)
                    wn[q % 2] = s_wo[q % 2].n
                POOL(fg)

                for i in qtiles[q]:
                    _, cc, db = tiles[i]
                    col0 = tok0 + cc * QC
                    sl = i % 2
                    woff2 = wreg_off + (q % 2) * H * ncols

                    def fp(t, q=q, cc=cc, db=db, sl=sl, woff2=woff2, i=i,
                           wmark=wn[q % 2], dm0=dve.n):
                        t.wait_ge(s_wo[q % 2].h, wmark)
                        if i >= 2:
                            t.wait_ge(dve.h, dve_done[i - 2])
                        elif dm0:
                            t.wait_ge(dve.h, dm0)
                        for hb in range(H):
                            pe.inc(t.matmul(
                                pos[sl][:, 0:QC],
                                wreg[:, woff2 + hb * ncols + (db - q * DQ) * P:
                                     woff2 + hb * ncols + (db - q * DQ) * P + P],
                                av[:, hb * av_span + av_off + cc * QC:
                                   hb * av_span + av_off + (cc + 1) * QC],
                                start=(hb == 0), stop=(hb == H - 1)))
                        pe_done[i] = pe.n
                    PE(fp)

                    def fs(g, db=db, col0=col0, sl=sl,
                           dm=dve_done.get(i - 2, dve.n)):
                        if dm:
                            g.wait_ge(dve.h, dm)
                        s_x[sl].inc(g.dma_start(
                            out=xt2[:, sl * QC:sl * QC + QC],
                            in_=src[db * P:(db + 1) * P, col0:col0 + QC]), 16)
                    SP(fs)
                    flush_w()
                    xn = s_x[sl].n

                    def fv(v, db=db, sl=sl, i=i):
                        v.wait_ge(pe.h, pe_done[i])
                        v.wait_ge(s_x[sl].h, xn)
                        v.wait_ge(s_resw[sl].h, s_resw[sl].n)
                        if bias_off is not None:
                            dve.inc(v.scalar_tensor_tensor(
                                rest[:, sl * QC:sl * QC + QC], pos[sl][:, 0:QC],
                                small[:, bias_off + db:bias_off + db + 1],
                                xt2[:, sl * QC:sl * QC + QC], ALU.add, ALU.add))
                        else:
                            dve.inc(v.tensor_tensor(
                                rest[:, sl * QC:sl * QC + QC], pos[sl][:, 0:QC],
                                xt2[:, sl * QC:sl * QC + QC], ALU.add))
                        dve_done[i] = dve.n
                    DVE(fv)

                    def fs2(g, db=db, col0=col0, sl=sl, i=i,
                            dd=None):
                        g.wait_ge(dve.h, dve_done[i])
                        s_resw[sl].inc(g.dma_start(
                            out=dst[db * P:(db + 1) * P,
                                    col0 - dst_off:col0 - dst_off + QC],
                            in_=rest[:, sl * QC:sl * QC + QC]), 16)
                    pending_w[0] = fs2
        flush_w()

    def phase_attn_a_slab(slab):
        t0 = slab * ACH
        nch = ACH // QC
        for cc in range(nch):
            col0 = t0 + cc * QC
            with nc.psum_tensor([P, QC], F32) as psA, \
                 nc.psum_tensor([P, QC], F32) as psB, \
                 nc.psum_tensor([1, QC], F32) as ps_sx, \
                 nc.psum_tensor([1, QC], F32) as ps_sx2:
                emit_ln_stats(xTe.ap(), col0, psA, psB, ps_sx, ps_sx2, C_EPSLN)
                emit_ln_apply(xTe.ap(), col0, psA, psB, lnxt, ACH, cc * QC)

        for h in range(H):
            wslot = (h % 2) * DB * P
            pe_before = pe.n

            def fg(g, h=h, wslot=wslot):
                if h >= 2:
                    g.wait_ge(pe.h, pe_before)
                s_w[h % 2].inc(g.dma_start(
                    out=wbuf[:, wslot:wslot + DB * P].rearrange("p (db c) -> p db c", db=DB),
                    in_=Wq.ap().rearrange("(db p) c -> p db c", p=P)[:, :, h * P:(h + 1) * P]), 16)
            POOL(fg)

            with nc.psum_tensor([P, QC], F32) as psq0, \
                 nc.psum_tensor([P, QC], F32) as psq1:
                psqs = [psq0, psq1]
                wsem, wn = s_w[h % 2], s_w[h % 2].n
                pe1 = pe.n
                dve1 = dve.n

                def fp(t, h=h, wslot=wslot):
                    t.wait_ge(wsem.h, wn)
                    for cc in range(nch):
                        if cc >= 2:
                            t.wait_ge(dve.h, dve1 + cc - 1)
                        elif dve1:
                            t.wait_ge(dve.h, dve1)
                        for db in range(DB):
                            pe.inc(t.matmul(
                                psqs[cc % 2][:, 0:QC],
                                wbuf[:, wslot + db * P:wslot + (db + 1) * P],
                                lnxt[:, db * ACH + cc * QC:db * ACH + cc * QC + QC],
                                start=(db == 0), stop=(db == DB - 1)))
                PE(fp)

                def fa(v, h=h):
                    for cc in range(nch):
                        v.wait_ge(pe.h, pe1 + DB * (cc + 1))
                        dve.inc(v.tensor_scalar(
                            bigB[:, h * ACH + cc * QC:h * ACH + cc * QC + QC],
                            psqs[cc % 2][:, 0:QC], scale_a,
                            small[:, BQ + h:BQ + h + 1], ALU.mult, ALU.add))
                DVE(fa)

        preflash_pe = pe.n
        FQ = QC
        for pair in range(H // 2):
            hA, hB = 2 * pair, 2 * pair + 1
            for qi in range(ACH // FQ):
                qoffA = hA * ACH + qi * FQ
                qoffB = hB * ACH + qi * FQ
                with nc.psum_tensor([P, FQ], F32) as psOA, \
                     nc.psum_tensor([P, FQ], F32) as psOB, \
                     nc.psum_tensor([1, FQ], F32) as psDA, \
                     nc.psum_tensor([1, FQ], F32) as psDB, \
                     nc.psum_tensor([P, FQ], F32) as psSA, \
                     nc.psum_tensor([P, FQ], F32) as psSB:
                    pe0, act0, dve0 = pe.n, act.n, dve.n

                    def fp(t):
                        if dve0:
                            t.wait_ge(dve.h, dve0)
                        for kb in range(KB):
                            if kb >= 1:
                                t.wait_ge(act.h, act0 + 2 * (kb - 1) + 1)
                            pe.inc(t.matmul(psSA[:, 0:FQ],
                                            kvb[:, KOFF + kb * P:KOFF + (kb + 1) * P],
                                            bigB[:, qoffA:qoffA + FQ],
                                            start=True, stop=True))
                            if kb >= 1:
                                t.wait_ge(act.h, act0 + 2 * (kb - 1) + 2)
                            pe.inc(t.matmul(psSB[:, 0:FQ],
                                            kvb[:, KOFF + kb * P:KOFF + (kb + 1) * P],
                                            bigB[:, qoffB:qoffB + FQ],
                                            start=True, stop=True))
                            if kb >= 1:
                                for (pD, pO, sl) in ((psDA, psOA, ((kb - 1) % 2)),
                                                     (psDB, psOB, 2 + ((kb - 1) % 2))):
                                    pe.inc(t.matmul(pO[:, 0:FQ],
                                                    kvb[:, VOFF + (kb - 1) * P:VOFF + kb * P],
                                                    ptil[:, sl * FQ:(sl + 1) * FQ],
                                                    start=(kb == 1), stop=False))
                                    pe.inc(t.matmul(pD[0:1, 0:FQ], ones_bf,
                                                    ptil[:, sl * FQ:(sl + 1) * FQ],
                                                    start=(kb == 1), stop=False))
                        t.wait_ge(act.h, act0 + 2 * KB)
                        for (pD, pO, sl) in ((psDA, psOA, ((KB - 1) % 2)),
                                             (psDB, psOB, 2 + ((KB - 1) % 2))):
                            pe.inc(t.matmul(pO[:, 0:FQ],
                                            kvb[:, VOFF + (KB - 1) * P:VOFF + KB * P],
                                            ptil[:, sl * FQ:(sl + 1) * FQ],
                                            start=False, stop=True))
                            pe.inc(t.matmul(pD[0:1, 0:FQ], ones_bf,
                                            ptil[:, sl * FQ:(sl + 1) * FQ],
                                            start=False, stop=True))
                    PE(fp)
                    pe_all = pe.n

                    def fa(s):
                        act_set_pin(s)
                        for kb in range(KB):
                            s.wait_ge(pe.h, pe0 + (1 if kb == 0 else 6 * kb - 3))
                            act.inc(s.activation(
                                ptil[:, (kb % 2) * FQ:(kb % 2) * FQ + FQ],
                                psSA[:, 0:FQ], AF.Exp,
                                bias=small[:, C_NSHIFT:C_NSHIFT + 1]))
                            s.wait_ge(pe.h, pe0 + (2 if kb == 0 else 6 * kb - 2))
                            act.inc(s.activation(
                                ptil[:, (2 + kb % 2) * FQ:(2 + kb % 2) * FQ + FQ],
                                psSB[:, 0:FQ], AF.Exp,
                                bias=small[:, C_NSHIFT:C_NSHIFT + 1]))
                    ACT(fa)

                    with nc.psum_tensor([P, FQ], F32) as psR:
                        for (pD, pO, qoff, rbo) in ((psDA, psOA, qoffA, 0),
                                                    (psDB, psOB, qoffB, FQ)):
                            def ft(v, pD=pD, rbo=rbo):
                                v.wait_ge(pe.h, pe_all)
                                dve.inc(v.reciprocal(
                                    out=rows[0:1, R_DEN + rbo:R_DEN + rbo + FQ],
                                    in_=pD[0:1, 0:FQ]))
                                v.wait_ge(dve.h, dve.n)
                                dve.inc(v.tensor_copy(
                                    rowsb[0:1, RB_DEN + rbo:RB_DEN + rbo + FQ],
                                    rows[0:1, R_DEN + rbo:R_DEN + rbo + FQ]))
                            DVE(ft)

                            def fpr(t, rbo=rbo):
                                t.wait_ge(dve.h, dve.n)
                                pe.inc(t.matmul(psR[:, 0:FQ], ones_row_bf,
                                                rowsb[0:1, RB_DEN + rbo:RB_DEN + rbo + FQ],
                                                start=True, stop=True))
                            PE(fpr)

                            def fvt(v, pO=pO, qoff=qoff, rbo=rbo):
                                v.wait_ge(pe.h, pe.n)
                                dve.inc(v.tensor_copy(rtil[:, rbo:rbo + FQ], psR[:, 0:FQ]))
                                v.wait_ge(dve.h, dve.n)
                                dve.inc(v.tensor_tensor(bigB[:, qoff:qoff + FQ],
                                                        pO[:, 0:FQ], rtil[:, rbo:rbo + FQ],
                                                        ALU.mult))
                            DVE(fvt)

        emit_oproj(Wo.ap(), bigB, H * ACH, bigB, ACH, 0, t0, ACH, BO,
                   xTe.ap(), res.ap(), 0, wpre_mark=preflash_pe)
        join_light()

    W0B_CH = P
    OUT0_OFF = 0
    WB_OFF = DB * FTC

    def phase_ffn_gate_setup():
        def fs(g):
            s_wkv.inc(g.dma_start(
                out=wbuf[:, 0:DB * 2].rearrange("p (db e) -> p db e", e=2),
                in_=WgT.ap().rearrange("(db p) e -> p db e", p=P)), 16)
        POOL(fs)

        def fv(v):
            v.wait_ge(s_wkv.h, s_wkv.n)
            for db in range(DB):
                dve.inc(v.tensor_tensor(
                    smb[:, WGDB + db:WGDB + db + 1],
                    wbuf[:, 2 * db:2 * db + 1],
                    wbuf[:, 2 * db + 1:2 * db + 2],
                    ALU.subtract))
        DVE(fv)

        with nc.psum_tensor([1, 1], F32) as psb:
            def fp(t):
                t.wait_ge(dve.h, dve.n)
                for db in range(DB):
                    pe.inc(t.matmul(psb[0:1, 0:1], smb[:, WGDB + db:WGDB + db + 1],
                                    ones_bf, start=(db == 0), stop=(db == DB - 1)))
            PE(fp)

            def fv2(v):
                v.wait_ge(pe.h, pe.n)
                dve.inc(v.tensor_copy(rows[0:1, R_MISC + 4:R_MISC + 5], psb[0:1, 0:1]))
            DVE(fv2)

    def phase_ffn(src, dst, tok0, ntok, dst_off):
        NCH = ntok // FTC
        NTC = FTC // QC
        wb_free = [pe.n]
        for ch in range(NCH):
            base = tok0 + ch * FTC
            for tc in range(NTC):
                col0 = base + tc * QC
                with nc.psum_tensor([P, QC], F32) as psA, \
                     nc.psum_tensor([P, QC], F32) as psB, \
                     nc.psum_tensor([1, QC], F32) as ps_sx, \
                     nc.psum_tensor([1, QC], F32) as ps_sx2, \
                     nc.psum_tensor([1, QC], F32) as ps_g:
                    emit_ln_stats(src, col0, psA, psB, ps_sx, ps_sx2, C_EPSLN,
                                  gate=True, ps_g=ps_g)
                    def fvg(v, tc=tc):
                        v.wait_ge(pe.h, pe.n)
                        v.wait_ge(dve.h, dve.n)
                        dve.inc(v.tensor_single_scalar(
                            rows[0:1, R_T0:R_T0 + QC], rows[0:1, R_MEAN:R_MEAN + QC],
                            rows[0:1, R_MISC + 4:R_MISC + 5], ALU.mult))
                        v.wait_ge(dve.h, dve.n)
                        dve.inc(v.tensor_tensor(rows[0:1, R_T1:R_T1 + QC],
                                                rows[0:1, R_T1:R_T1 + QC],
                                                rows[0:1, R_T0:R_T0 + QC],
                                                ALU.subtract))
                        v.wait_ge(dve.h, dve.n)
                        dve.inc(v.tensor_single_scalar(
                            rowsb[0:1, RB_MASK:RB_MASK + QC], rows[0:1, R_T1:R_T1 + QC],
                            0.0, ALU.is_ge))
                    DVE(fvg)

                    with nc.psum_tensor([P, QC], F32) as psM:
                        def fpm(t, tc=tc):
                            t.wait_ge(dve.h, dve.n)
                            pe.inc(t.matmul(psM[:, 0:QC], ones_row_bf,
                                            rowsb[0:1, RB_MASK:RB_MASK + QC],
                                            start=True, stop=True))
                        PE(fpm)

                        def fvm(v, tc=tc):
                            v.wait_ge(pe.h, pe.n)
                            dve.inc(v.tensor_copy(mtil[:, tc * QC:(tc + 1) * QC],
                                                  psM[:, 0:QC]))
                        DVE(fvm)

                    emit_ln_apply(src, col0, psA, psB, lnxt, FTC, tc * QC)

            for e in range(2):
                WaT = W0aT if e == 0 else W1aT
                WbT = W0bT if e == 0 else W1bT
                BA = B0A if e == 0 else B1A
                BB = B0B if e == 0 else B1B
                with nc.psum_tensor([P, QC], F32) as ph00, \
                     nc.psum_tensor([P, QC], F32) as ph01, \
                     nc.psum_tensor([P, QC], F32) as ph10, \
                     nc.psum_tensor([P, QC], F32) as ph11:
                    psH = [[ph00, ph01], [ph10, ph11]]
                    pe0, act0 = pe.n, act.n

                    def fgw(g, e=e, WaT=WaT):
                        for fb in range(FB):
                            if fb >= 2:
                                g.wait_ge(pe.h, pe0 + DB * NTC * (fb - 1))
                            else:
                                g.wait_ge(pe.h, pe0)
                            s_w[fb % 2].inc(g.dma_start(
                                out=wbuf[:, (fb % 2) * DB * P:(fb % 2 + 1) * DB * P]
                                    .rearrange("p (db c) -> p db c", db=DB),
                                in_=WaT.ap().rearrange("(db p) c -> p db c", p=P)
                                    [:, :, fb * P:(fb + 1) * P]), 16)
                    POOL(fgw)

                    wn0 = [s_w[0].n - (FB + 1) // 2 * 16, s_w[1].n - FB // 2 * 16]

                    def fp1(t, e=e, dm=dve.n):
                        if dm:
                            t.wait_ge(dve.h, dm)
                        for fb in range(FB):
                            t.wait_ge(s_w[fb % 2].h, wn0[fb % 2] + (fb // 2 + 1) * 16)
                            if fb >= 2:
                                t.wait_ge(act.h, act0 + NTC * (fb - 1))
                            elif act0:
                                t.wait_ge(act.h, act0)
                            for db in range(DB):
                                for tc in range(NTC):
                                    pe.inc(t.matmul(
                                        psH[fb % 2][tc][:, 0:QC],
                                        wbuf[:, (fb % 2) * DB * P + db * P:
                                             (fb % 2) * DB * P + (db + 1) * P],
                                        lnxt[:, db * FTC + tc * QC:db * FTC + (tc + 1) * QC],
                                        start=(db == 0), stop=(db == DB - 1)))
                    PE(fp1)

                    AFUN = AF.Gelu if e == 0 else AF.Relu

                    def fa1(s, e=e, BA=BA, AFUN=AFUN):
                        for fb in range(FB):
                            s.wait_ge(pe.h, pe0 + DB * NTC * fb + DB * NTC)
                            for tc in range(NTC):
                                act.inc(s.activation(
                                    bigB[:, fb * FTC + tc * QC:fb * FTC + (tc + 1) * QC],
                                    psH[fb % 2][tc][:, 0:QC], AFUN,
                                    bias=small[:, BA + fb:BA + fb + 1]))
                    ACT(fa1)

                ndq = D // W0B_CH
                wn2 = [0, 0]
                pe2_done = {}
                dve2_done = {}
                pend2 = [None]

                def flush2():
                    if pend2[0] is not None:
                        SP(pend2[0])
                        pend2[0] = None

                with nc.psum_tensor([P, QC], F32) as q20, \
                     nc.psum_tensor([P, QC], F32) as q21:
                    pq2 = [q20, q21]
                    mm2_pre = wb_free[0]
                    for dq in range(ndq):
                        def fgw2(g, e=e, WbT=WbT, dq=dq, pm=mm2_pre):
                            if dq >= 2:
                                g.wait_ge(pe.h, pe2_done[(dq - 2, NTC - 1)])
                            elif pm:
                                g.wait_ge(pe.h, pm)
                            s_wo[dq % 2].inc(g.dma_start(
                                out=kvb[:, WB_OFF + (dq % 2) * FB * W0B_CH:
                                     WB_OFF + (dq % 2 + 1) * FB * W0B_CH]
                                    .rearrange("p (fb c) -> p fb c", fb=FB),
                                in_=WbT.ap().rearrange("(fb p) c -> p fb c", p=P)
                                    [:, :, dq * W0B_CH:(dq + 1) * W0B_CH]), 16)
                            wn2[dq % 2] = s_wo[dq % 2].n
                        POOL(fgw2)

                        db = dq
                        woff2 = WB_OFF + (dq % 2) * FB * W0B_CH
                        for tc in range(NTC):
                            col0 = base + tc * QC
                            i2 = dq * NTC + tc
                            sl = i2 % 2

                            def fp2(t, db=db, tc=tc, sl=sl, i2=i2, dq=dq,
                                    woff2=woff2, am=act.n, wmark=wn2[dq % 2]):
                                t.wait_ge(s_wo[dq % 2].h, wmark)
                                t.wait_ge(act.h, am)
                                if i2 >= 2:
                                    t.wait_ge(dve.h, dve2_done[i2 - 2])
                                elif dve.n:
                                    t.wait_ge(dve.h, dve.n)
                                for fb in range(FB):
                                    pe.inc(t.matmul(
                                        pq2[sl][:, 0:QC],
                                        kvb[:, woff2 + fb * W0B_CH:
                                             woff2 + fb * W0B_CH + P],
                                        bigB[:, fb * FTC + tc * QC:fb * FTC + (tc + 1) * QC],
                                        start=(fb == 0), stop=(fb == FB - 1)))
                                pe2_done[(dq, tc)] = pe.n
                            PE(fp2)
                            pe2_done_i = None

                            if e == 0:
                                def fv20(v, db=db, tc=tc, BB=BB, sl=sl, i2=i2, dq=dq):
                                    v.wait_ge(pe.h, pe2_done[(dq, tc)])
                                    dve.inc(v.tensor_scalar(
                                        kvb[:, OUT0_OFF + db * FTC + tc * QC:
                                             OUT0_OFF + db * FTC + (tc + 1) * QC],
                                        pq2[sl][:, 0:QC],
                                        small[:, BB + db:BB + db + 1], None,
                                        ALU.add))
                                    dve2_done[i2] = dve.n
                                DVE(fv20)
                            else:
                                def fsx(g, db=db, col0=col0, sl=sl,
                                        dm=dve2_done.get(i2 - 2, dve.n)):
                                    if dm:
                                        g.wait_ge(dve.h, dm)
                                    s_x[sl].inc(g.dma_start(
                                        out=xt2[:, sl * QC:sl * QC + QC],
                                        in_=src[db * P:(db + 1) * P, col0:col0 + QC]), 16)
                                SP(fsx)
                                flush2()
                                xn = s_x[sl].n

                                def fv21(v, db=db, tc=tc, BB=BB, sl=sl, i2=i2,
                                         dq=dq, xn=xn):
                                    v.wait_ge(pe.h, pe2_done[(dq, tc)])
                                    v.wait_ge(s_x[sl].h, xn)
                                    v.wait_ge(s_resw[sl].h, s_resw[sl].n)
                                    dve.inc(v.scalar_tensor_tensor(
                                        sq2[:, sl * QC:sl * QC + QC], pq2[sl][:, 0:QC],
                                        small[:, BB + db:BB + db + 1],
                                        kvb[:, OUT0_OFF + db * FTC + tc * QC:
                                             OUT0_OFF + db * FTC + (tc + 1) * QC],
                                        ALU.add, ALU.subtract))
                                    v.wait_ge(dve.h, dve.n)
                                    dve.inc(v.tensor_tensor(
                                        qsc[:, sl * QC:sl * QC + QC],
                                        sq2[:, sl * QC:sl * QC + QC],
                                        mtil[:, tc * QC:(tc + 1) * QC], ALU.mult))
                                    v.wait_ge(dve.h, dve.n)
                                    dve.inc(v.scalar_tensor_tensor(
                                        sq2[:, sl * QC:sl * QC + QC], pq2[sl][:, 0:QC],
                                        small[:, BB + db:BB + db + 1],
                                        qsc[:, sl * QC:sl * QC + QC],
                                        ALU.add, ALU.subtract))
                                    v.wait_ge(dve.h, dve.n)
                                    dve.inc(v.tensor_tensor(
                                        rest[:, sl * QC:sl * QC + QC],
                                        sq2[:, sl * QC:sl * QC + QC],
                                        xt2[:, sl * QC:sl * QC + QC], ALU.add))
                                    dve2_done[i2] = dve.n
                                DVE(fv21)

                                def fso(g, db=db, col0=col0, dst=dst, sl=sl, i2=i2,
                                        dst_off=dst_off):
                                    g.wait_ge(dve.h, dve2_done[i2])
                                    s_resw[sl].inc(g.dma_start(
                                        out=dst[db * P:(db + 1) * P,
                                                col0 - dst_off:col0 - dst_off + QC],
                                        in_=rest[:, sl * QC:sl * QC + QC]), 16)
                                pend2[0] = fso

                    flush2()
                    wb_free[0] = pe.n

    EB = EXT // P
    NQB2 = NT // 1024 if NT % 1024 == 0 else NT // min(NT, 1024)
    FQ2 = min(1024, NT)
    VB_OFF = KVB * EXT

    def phase_ropetab():
        def fg(g):
            pool.inc(g.memset(sq2[:, 0:P], 0.0))
            g.wait_ge(pool.h, pool.n)
            pool.inc(g.affine_select(out=sq2[:, 0:P], in_=sq2[:, 0:P],
                                     compare_op=ALU.not_equal, fill=1.0,
                                     base=-(P // 2), pattern=[[1, P]],
                                     channel_multiplier=-1))
            g.wait_ge(pool.h, pool.n)
            pool.inc(g.affine_select(out=sq2[:, 0:P], in_=sq2[:, 0:P],
                                     compare_op=ALU.not_equal, fill=-1.0,
                                     base=(P // 2), pattern=[[1, P]],
                                     channel_multiplier=-1))
            g.wait_ge(pool.h, pool.n)
            s_wkv.inc(g.dma_start(out=ropet[:, 0:2 * EXT], in_=ropeT_.ap()), 16)
        POOL(fg)

        def fv0(v):
            v.wait_ge(pool.h, pool.n)
            dve.inc(v.tensor_copy(rtc[:, 0:P], sq2[:, 0:P]))
            v.wait_ge(s_wkv.h, s_wkv.n)
        DVE(fv0)

    def phase_attn_b_proj():
        NCH = EXT // FTC
        NTC = FTC // QC
        qsc4 = qsc

        with nc.psum_tensor([P, QC], F32) as pq0, \
             nc.psum_tensor([P, QC], F32) as pq1, \
             nc.psum_tensor([1, QC], F32) as pn0, \
             nc.psum_tensor([1, QC], F32) as pn1, \
             nc.psum_tensor([P, QC], F32) as pR0, \
             nc.psum_tensor([P, QC], F32) as pR1, \
             nc.psum_tensor([P, QC], F32) as psrot, \
             nc.psum_tensor([P, P], BF16) as pst:
            psq = [pq0, pq1]
            psn = [pn0, pn1]
            psR = [pR0, pR1]
            st = {"i": 0, "dve_done": {}, "pe_done": {}, "rot_done": 0,
                  "pst_done": dve.n, "ln_dve": 0}

            def emit_proj(wslot, wsem, wmark, tc_col):
                i = st["i"]
                sl = i % 2

                def fp(t, i=i, sl=sl, wslot=wslot, tc_col=tc_col, wmark=wmark,
                       dm=max(st["dve_done"].get(i - 2, dve.n), st["ln_dve"])):
                    t.wait_ge(wsem.h, wmark)
                    if dm:
                        t.wait_ge(dve.h, dm)
                    for db in range(DB):
                        pe.inc(t.matmul(
                            psq[sl][:, 0:QC],
                            wbuf[:, wslot + db * P:wslot + (db + 1) * P],
                            lnxt[:, db * FTC + tc_col:db * FTC + tc_col + QC],
                            start=(db == 0), stop=(db == DB - 1)))
                    st["pe_done"][i] = pe.n
                PE(fp)

            def emit_tail_qk(gain_off, tc_abs, dst, dcol):
                i = st["i"]
                sl = i % 2

                def t1(v, i=i, sl=sl):
                    v.wait_ge(pe.h, st["pe_done"][i])
                    dve.inc(v.tensor_copy(qsc4[:, sl * QC:sl * QC + QC],
                                          psq[sl][:, 0:QC]))
                    v.wait_ge(dve.h, dve.n)
                    dve.inc(v.tensor_tensor(sqb[:, sl * QC:sl * QC + QC],
                                            qsc4[:, sl * QC:sl * QC + QC],
                                            qsc4[:, sl * QC:sl * QC + QC],
                                            ALU.mult))
                DVE(t1)
                a1 = dve.n

                def t2(t, i=i, sl=sl):
                    t.wait_ge(dve.h, a1)
                    pe.inc(t.matmul(psn[sl][0:1, 0:QC], ones128_bf,
                                    sqb[:, sl * QC:sl * QC + QC],
                                    start=True, stop=True))
                PE(t2)
                p2 = pe.n

                rT = R_T0 + sl * 512
                rB = sl * 512

                def t45(s, i=i, sl=sl, rT=rT, rB=rB):
                    s.wait_ge(pe.h, p2)
                    s.wait_ge(act.h, act.n)
                    act.inc(s.activation(rows[0:1, rT:rT + QC],
                                         psn[sl][0:1, 0:QC], AF.Ln,
                                         bias=small[0:1, C_EPSRMS:C_EPSRMS + 1]))
                    s.wait_ge(act.h, act.n)
                    act.inc(s.activation(rowsb[0:1, rB:rB + QC],
                                         rows[0:1, rT:rT + QC], AF.Exp,
                                         scale=-0.5))
                ACT(t45)
                a5 = act.n

                def t6(t, i=i, sl=sl, rB=rB):
                    t.wait_ge(act.h, a5)
                    pe.inc(t.matmul(psR[sl][:, 0:QC], ones_row_bf,
                                    rowsb[0:1, rB:rB + QC], start=True, stop=True))
                PE(t6)
                p6 = pe.n

                def t78(v, i=i, sl=sl, gain_off=gain_off):
                    v.wait_ge(pe.h, p6)
                    dve.inc(v.tensor_scalar(qsc4[:, 2 * QC:3 * QC],
                                            qsc4[:, sl * QC:sl * QC + QC],
                                            small[:, gain_off:gain_off + 1], None,
                                            ALU.mult))
                    v.wait_ge(dve.h, dve.n)
                    dve.inc(v.tensor_tensor(ptil[:, sl * QC:sl * QC + QC],
                                            qsc4[:, 2 * QC:3 * QC],
                                            psR[sl][:, 0:QC], ALU.mult))
                DVE(t78)
                d8 = dve.n

                def t9(t, i=i, sl=sl, rd=st["rot_done"]):
                    t.wait_ge(dve.h, d8)
                    if rd:
                        t.wait_ge(dve.h, rd)
                    pe.inc(t.matmul(psrot[:, 0:QC], rtc[:, 0:P],
                                    ptil[:, sl * QC:sl * QC + QC],
                                    start=True, stop=True))
                PE(t9)
                p9 = pe.n

                def t10(v, i=i, sl=sl, tc_abs=tc_abs, dst=dst, dcol=dcol):
                    v.wait_ge(pe.h, p9)
                    dve.inc(v.tensor_tensor(qsc4[:, sl * QC:sl * QC + QC],
                                            ptil[:, sl * QC:sl * QC + QC],
                                            ropet[:, tc_abs:tc_abs + QC], ALU.mult))
                    v.wait_ge(dve.h, dve.n)
                    dve.inc(v.tensor_tensor(qsc4[:, 2 * QC:3 * QC],
                                            psrot[:, 0:QC],
                                            ropet[:, EXT + tc_abs:EXT + tc_abs + QC],
                                            ALU.mult))
                    st["rot_done"] = dve.n
                    v.wait_ge(dve.h, dve.n)
                    dve.inc(v.tensor_tensor(dst[:, dcol:dcol + QC],
                                            qsc4[:, sl * QC:sl * QC + QC],
                                            qsc4[:, 2 * QC:3 * QC],
                                            ALU.add))
                    st["dve_done"][i] = dve.n
                DVE(t10)

            def emit_tail_v(g, tc_abs):
                i = st["i"]
                sl = i % 2

                def v1(v, i=i, sl=sl, pd=st["pst_done"]):
                    v.wait_ge(pe.h, st["pe_done"][i])
                    dve.inc(v.tensor_copy(ptil[:, sl * QC:sl * QC + QC],
                                          psq[sl][:, 0:QC]))
                DVE(v1)
                dv1 = dve.n
                nb = QC // P
                pe_tr = []

                def v2(t, i=i, sl=sl, g=g, tc_abs=tc_abs, dv1=dv1,
                       pd=st["pst_done"]):
                    for j in range(nb):
                        t.wait_ge(dve.h, dv1 + j)
                        if j == 0 and pd:
                            t.wait_ge(dve.h, max(dv1, pd))
                        pe.inc(t.transpose(pst[:],
                                           ptil[:, sl * QC + j * P:sl * QC + (j + 1) * P],
                                           ident[:]))
                        pe_tr.append(pe.n)
                PE(v2)

                def v3(v, i=i, g=g, tc_abs=tc_abs):
                    for j in range(nb):
                        v.wait_ge(pe.h, pe_tr[j])
                        o = VB_OFF + g * EXT + tc_abs + j * P
                        dve.inc(v.tensor_copy(kvb[:, o:o + P], pst[:]))
                    st["pst_done"] = dve.n
                    st["dve_done"][i] = dve.n
                DVE(v3)

            wload = {"n": 0}

            def load_w(wT, blk):
                li = wload["n"]
                wload["n"] += 1
                wslot = (li % 2) * DB * P
                wsem = s_w[li % 2]

                def fg(g_, wT=wT, blk=blk, wslot=wslot,
                       pm=st["pe_done"].get(st["i"] - 2, pe.n)):
                    if pm:
                        g_.wait_ge(pe.h, pm)
                    wsem.inc(g_.dma_start(
                        out=wbuf[:, wslot:wslot + DB * P].rearrange("p (db c) -> p db c", db=DB),
                        in_=wT.ap().rearrange("(db p) c -> p db c", p=P)
                            [:, :, blk * P:(blk + 1) * P]), 16)
                POOL(fg)
                return wslot, wsem, wsem.n

            for ch in range(NCH):
                base = ch * FTC
                for tc in range(NTC):
                    col0 = base + tc * QC
                    emit_ln_stats(res.ap(), col0, pR0, pR1, pn0, pn1, C_EPSLN)
                    emit_ln_apply(res.ap(), col0, pR0, pR1, lnxt, FTC, tc * QC)
                    st["ln_dve"] = dve.n

                for g in range(KVB):
                    wslot, wsem, wmark = load_w(wkT, g)
                    for tc in range(NTC):
                        emit_proj(wslot, wsem, wmark, tc * QC)
                        emit_tail_qk(KNW, base + tc * QC, kvb, g * EXT + base + tc * QC)
                        st["i"] += 1
                    wslot, wsem, wmark = load_w(wvT, g)
                    for tc in range(NTC):
                        emit_proj(wslot, wsem, wmark, tc * QC)
                        emit_tail_v(g, base + tc * QC)
                        st["i"] += 1
                for h in range(H):
                    cols = [tc for tc in range(NTC) if base + tc * QC >= HALO]
                    if not cols:
                        continue
                    wslot, wsem, wmark = load_w(wqT, h)
                    for tc in cols:
                        emit_proj(wslot, wsem, wmark, tc * QC)
                        emit_tail_qk(QNW, base + tc * QC, bigB,
                                     h * NT + (base + tc * QC - HALO))
                        st["i"] += 1

    def phase_flash_b():
        WBK = W // P
        for h in range(H):
            g = h // RH
            for qi in range(NT // FQ2):
                q0 = HALO + qi * FQ2
                kb_lo = (q0 - W + 1) // P
                kb_hi = (q0 + FQ2 - 1) // P
                kbs = list(range(kb_lo, kb_hi + 1))
                nk = len(kbs)
                qoff = h * NT + qi * FQ2

                geo = []
                for kb in kbs:
                    qlo = max(q0, P * kb)
                    qhi = min(q0 + FQ2, P * kb + W + P)
                    subs = []
                    a = qlo
                    while a < qhi:
                        bnd = q0 + ((a - q0) // 512 + 1) * 512
                        b_ = min(qhi, bnd)
                        subs.append((a - q0, b_ - a))
                        a = b_
                    diag = (P * kb >= q0)
                    far = (qhi == P * kb + W + P)
                    halo_k = (kb * P < HALO)
                    geo.append((kb, qlo, qhi, subs, diag, far, halo_k))

                with nc.psum_tensor([P, FQ2], F32) as psO, \
                     nc.psum_tensor([1, FQ2], F32) as psD:
                  with nc.psum_tensor([P, FQ2], F32) as psS0, \
                       nc.psum_tensor([P, FQ2], F32) as psS1:
                    psS = [psS0, psS1]
                    pe0, act0, dve0 = pe.n, act.n, dve.n
                    pe_after_scores = []
                    dve_after_masks = []
                    pe_c = pe0
                    dve_c = dve0
                    for ki, (kb, qlo, qhi, subs, diag, far, halo_k) in enumerate(geo):
                        pe_c += len(subs)
                        pe_after_scores.append(pe_c)
                        dve_c += ((1 if diag else 0) + (1 if far else 0)
                                  + (1 if halo_k else 0))
                        dve_after_masks.append(dve_c)
                        if ki >= 1:
                            pe_c += 2 * len(geo[ki - 1][3])
                    pe_c += 2 * len(geo[-1][3])

                    started_O = set()
                    started_D = set()

                    def fp(t, h=h, g=g, q0=q0):
                        if dve0:
                            t.wait_ge(dve.h, dve0)
                        for ki, (kb, qlo, qhi, subs, diag, far, halo_k) in enumerate(geo):
                            if ki >= 2:
                                t.wait_ge(act.h, act0 + ki - 1)
                            for off, n in subs:
                                pe.inc(t.matmul(
                                    psS[ki % 2][:, off:off + n],
                                    kvb[:, g * EXT + kb * P:g * EXT + (kb + 1) * P],
                                    bigB[:, qoff + off:qoff + off + n],
                                    start=True, stop=True))
                            if ki >= 1:
                                t.wait_ge(act.h, act0 + ki)
                                kbp, qlop, qhip, subsp, _, _, _ = geo[ki - 1]
                                pl = ((ki - 1) % 2) * FQ2
                                for off, n in subsp:
                                    st = (off // 512) not in started_O
                                    started_O.add(off // 512)
                                    pe.inc(t.matmul(
                                        psO[:, off:off + n],
                                        kvb[:, VB_OFF + g * EXT + kbp * P:
                                             VB_OFF + g * EXT + (kbp + 1) * P],
                                        ptil[:, pl + off:pl + off + n],
                                        start=st, stop=False))
                                for off, n in subsp:
                                    st = (off // 512) not in started_D
                                    started_D.add(off // 512)
                                    pe.inc(t.matmul(
                                        psD[0:1, off:off + n], ones_bf,
                                        ptil[:, pl + off:pl + off + n],
                                        start=st, stop=False))
                        t.wait_ge(act.h, act0 + len(geo))
                        kbp, qlop, qhip, subsp, _, _, _ = geo[-1]
                        pl = ((len(geo) - 1) % 2) * FQ2
                        for off, n in subsp:
                            st = (off // 512) not in started_O
                            started_O.add(off // 512)
                            pe.inc(t.matmul(
                                psO[:, off:off + n],
                                kvb[:, VB_OFF + g * EXT + kbp * P:
                                     VB_OFF + g * EXT + (kbp + 1) * P],
                                ptil[:, pl + off:pl + off + n],
                                start=st, stop=True))
                        for off, n in subsp:
                            st = (off // 512) not in started_D
                            started_D.add(off // 512)
                            pe.inc(t.matmul(
                                psD[0:1, off:off + n], ones_bf,
                                ptil[:, pl + off:pl + off + n],
                                start=st, stop=True))
                    PE(fp)

                    def fm(v, q0=q0):
                        for ki, (kb, qlo, qhi, subs, diag, far, halo_k) in enumerate(geo):
                            if not (diag or far or halo_k):
                                continue
                            v.wait_ge(pe.h, pe_after_scores[ki])
                            sl = psS[ki % 2]
                            if diag:
                                o = P * kb - q0
                                dve.inc(v.tensor_tensor(
                                    sl[:, o:o + P], sl[:, o:o + P],
                                    mask2[:, P:2 * P], ALU.add))
                            if far:
                                o = qhi - P - q0
                                dve.inc(v.tensor_tensor(
                                    sl[:, o:o + P], sl[:, o:o + P],
                                    mask2[:, 0:P], ALU.add))
                            if halo_k:
                                if diag or far:
                                    v.wait_ge(dve.h, dve.n)
                                o = qlo - q0
                                dve.inc(v.tensor_scalar(
                                    sl[:, o:qhi - q0], sl[:, o:qhi - q0],
                                    small[:, HALOR:HALOR + 1], None, ALU.add))
                    DVE(fm)

                    def fa(s, q0=q0):
                        act_set_pin(s)
                        for ki, (kb, qlo, qhi, subs, diag, far, halo_k) in enumerate(geo):
                            s.wait_ge(pe.h, pe_after_scores[ki])
                            s.wait_ge(dve.h, dve_after_masks[ki])
                            pl = (ki % 2) * FQ2
                            o = qlo - q0
                            act.inc(s.activation(
                                ptil[:, pl + o:pl + (qhi - q0)],
                                psS[ki % 2][:, o:qhi - q0], AF.Exp,
                                bias=small[:, C_NSHIFT:C_NSHIFT + 1]))
                    ACT(fa)
                    act_all = act.n

                  with nc.psum_tensor([P, FQ2], F32) as psR:
                    def fv(v):
                        v.wait_ge(pe.h, pe.n)
                        dve.inc(v.reciprocal(out=rows[0:1, R_DEN:R_DEN + FQ2],
                                             in_=psD[0:1, 0:FQ2]))
                        v.wait_ge(dve.h, dve.n)
                        dve.inc(v.tensor_copy(rowsb[0:1, RB_DEN:RB_DEN + FQ2],
                                              rows[0:1, R_DEN:R_DEN + FQ2]))
                    DVE(fv)

                    def fp2(t):
                        t.wait_ge(dve.h, dve.n)
                        for ss in range(FQ2 // QC):
                            pe.inc(t.matmul(
                                psR[:, ss * QC:(ss + 1) * QC], ones_row_bf,
                                rowsb[0:1, RB_DEN + ss * QC:RB_DEN + (ss + 1) * QC],
                                start=True, stop=True))
                    PE(fp2)

                    def fv2(v, qoff=qoff):
                        v.wait_ge(pe.h, pe.n)
                        dve.inc(v.tensor_copy(rtil[:, 0:FQ2], psR[:, 0:FQ2]))
                        v.wait_ge(dve.h, dve.n)
                        dve.inc(v.tensor_tensor(bigB[:, qoff:qoff + FQ2],
                                                psO[:, 0:FQ2], rtil[:, 0:FQ2],
                                                ALU.mult))
                    DVE(fv2)

    phase_setup()
    join_all()
    phase_kv()
    join_all()
    for slab in range(NSLAB):
        phase_attn_a_slab(slab)
    phase_ffn_gate_setup()
    join_light()
    phase_ffn(res.ap(), res.ap(), 0, EXT, 0)
    join_light()
    phase_ropetab()
    join_light()
    phase_attn_b_proj()
    join_light()
    phase_flash_b()
    join_light()
    emit_oproj(woT.ap(), kvb, 0, bigB, NT, 0, HALO, NT, None,
               res.ap(), res.ap(), 0)
    join_light()
    phase_ffn(res.ap(), outT.ap(), HALO, NT, HALO)

    def ffinal(s):
        s.wait_ge(s_resw[0].h, s_resw[0].n)
        s.wait_ge(s_resw[1].h, s_resw[1].n)
    SP(ffinal)

    ctx.close()
    return nc



def _rope_tables(cfg, ext0):
    pos = ext0 + np.arange(cfg.EXT, dtype=np.float64)
    inv = 1.0 / (10000.0 ** (np.arange(0, P, 2, dtype=np.float64) / P))
    ang = inv[:, None] * pos[None, :]
    cos = np.cos(ang).astype(np.float32)
    sin = np.sin(ang).astype(np.float32)
    cos2 = np.concatenate([cos, cos], axis=0)
    sin2 = np.concatenate([sin, sin], axis=0)
    return np.concatenate([cos2, sin2], axis=1)


def make_core_inputs(cfg, inp, core):
    bsz = 4
    bidx, rank = divmod(core, bsz)
    t0 = rank * cfg.NT
    x = inp["L0_attn_a_x"]
    xT = np.ascontiguousarray(np.transpose(x[bidx]))
    ext0 = t0 - cfg.HALO
    if ext0 < 0:
        pad = np.zeros((cfg.D, -ext0), np.float32)
        xTe = np.concatenate([pad, xT[:, 0:t0 + cfg.NT]], axis=1)
    else:
        xTe = xT[:, ext0:t0 + cfg.NT]
    halo_valid = 0.0 if ext0 >= 0 else -30000.0
    d = {
        "xTb": xT,
        "xTe": np.ascontiguousarray(xTe),
        "Wq": inp["L0_attn_a_Wq"],
        "Wk": inp["L0_attn_a_Wk"],
        "Wv": inp["L0_attn_a_Wv"],
        "Wo": inp["L0_attn_a_Wo"],
        "bqT": np.ascontiguousarray(inp["L0_attn_a_bq"].reshape(cfg.H, P).T),
        "bk": inp["L0_attn_a_bk"].reshape(P, 1),
        "bv": inp["L0_attn_a_bv"].reshape(P, 1),
        "boT": np.ascontiguousarray(inp["L0_attn_a_bo"].reshape(cfg.DB, P).T),
        "WgT": np.ascontiguousarray(inp["L0_ffn_Wg"].T),
        "W0aT": np.ascontiguousarray(inp["L0_ffn_W0a"].T),
        "W0bT": np.ascontiguousarray(inp["L0_ffn_W0b"].T),
        "W1aT": np.ascontiguousarray(inp["L0_ffn_W1a"].T),
        "W1bT": np.ascontiguousarray(inp["L0_ffn_W1b"].T),
        "b0aT": np.ascontiguousarray(inp["L0_ffn_b0a"].reshape(cfg.FB, P).T),
        "b0bT": np.ascontiguousarray(inp["L0_ffn_b0b"].reshape(cfg.DB, P).T),
        "b1aT": np.ascontiguousarray(inp["L0_ffn_b1a"].reshape(cfg.FB, P).T),
        "b1bT": np.ascontiguousarray(inp["L0_ffn_b1b"].reshape(cfg.DB, P).T),
        "wqT": np.ascontiguousarray(inp["L0_attn_b_wq"].T),
        "wkT": np.ascontiguousarray(inp["L0_attn_b_wk"].T),
        "wvT": np.ascontiguousarray(inp["L0_attn_b_wv"].T),
        "woT": np.ascontiguousarray(inp["L0_attn_b_wo"].T),
        "qnw": inp["L0_attn_b_q_norm_w"].reshape(P, 1),
        "knw": inp["L0_attn_b_k_norm_w"].reshape(P, 1),
        "pos0": np.array([[float(ext0)]], np.float32),
        "ropeT": _rope_tables(cfg, ext0),
        "halob": np.full((P, 1), halo_valid, np.float32),
    }
    return {k: np.ascontiguousarray(np.asarray(v, np.float32)) for k, v in d.items()}


def kernel(**inputs):
    cfg = Cfg()
    nc = build_program(cfg)
    in_maps = [make_core_inputs(cfg, inputs, c) for c in range(8)]
    r = run_bass_kernel_spmd(nc, in_maps, core_ids=list(range(8)))
    outs = []
    for b in range(2):
        cores = [r.results[b * 4 + i]["outT"].T for i in range(4)]
        outs.append(np.concatenate(cores, axis=0))
    return np.stack(outs, axis=0).astype(np.float32)


# version 23
# speedup vs baseline: 1.0945x; 1.0369x over previous
import math
from contextlib import ExitStack

import numpy as np

import concourse.bass as bass
import concourse.mybir as mybir
from concourse.bass_utils import run_bass_kernel_spmd

F32 = mybir.dt.float32
BF16 = mybir.dt.bfloat16
F32R = mybir.dt.float32r
AF = mybir.ActivationFunctionType
ALU = mybir.AluOpType
AX = mybir.AxisListType

P = 128
CSHIFT = 12.0


class Cfg:
    def __init__(self, D=2048, S=8192, NT=2048, HALO=1024, H=16, KVB=4, F=4096,
                 QC=512, ACH=1024, FTC=1024):
        self.D, self.S, self.NT, self.HALO = D, S, NT, HALO
        self.H, self.KVB, self.F = H, KVB, F
        self.QC, self.ACH, self.FTC = QC, ACH, FTC
        self.DB = D // P
        self.EXT = NT + HALO
        self.FB = F // P
        self.W = HALO
        self.RH = H // KVB
        assert self.EXT % ACH == 0 and ACH % QC == 0 and S % QC == 0
        assert self.EXT % FTC == 0 and FTC % QC == 0 and NT % FTC == 0
        assert HALO % P == 0 and NT % P == 0


class Sem:
    __slots__ = ("h", "n")

    def __init__(self, nc, name):
        self.h = nc.alloc_semaphore(name=name)
        self.n = 0

    def inc(self, instr, k=1):
        instr.then_inc(self.h, k)
        self.n += k
        return self.n


def build_program(cfg: Cfg):
    nc = bass.Bass("TRN2", target_bir_lowering=False, debug=False)
    ctx = ExitStack()
    D, DB, S, NT, HALO, EXT = cfg.D, cfg.DB, cfg.S, cfg.NT, cfg.HALO, cfg.EXT
    H, KVB, F, FB, RH = cfg.H, cfg.KVB, cfg.F, cfg.FB, cfg.RH
    QC, ACH, FTC, W = cfg.QC, cfg.ACH, cfg.FTC, cfg.W
    SC = S // QC
    NSLAB = EXT // ACH
    KB = S // P
    WB = W // P

    dram = {}

    def din(name, shape, dtype=F32):
        dram[name] = nc.dram_tensor(name, list(shape), dtype, kind="ExternalInput")
        return dram[name]

    xTb = din("xTb", (D, S))
    xTe = din("xTe", (D, EXT))
    Wq = din("Wq", (D, H * P))
    Wk = din("Wk", (D, P))
    Wv = din("Wv", (D, P))
    Wo = din("Wo", (H * P, D))
    bqT = din("bqT", (P, H))
    bk_ = din("bk", (P, 1))
    bv_ = din("bv", (P, 1))
    boT = din("boT", (P, DB))
    WgT = din("WgT", (D, 2))
    W0aT = din("W0aT", (D, F))
    W0bT = din("W0bT", (F, D))
    W1aT = din("W1aT", (D, F))
    W1bT = din("W1bT", (F, D))
    b0aT = din("b0aT", (P, FB))
    b0bT = din("b0bT", (P, DB))
    b1aT = din("b1aT", (P, FB))
    b1bT = din("b1bT", (P, DB))
    wqT = din("wqT", (D, H * P))
    wkT = din("wkT", (D, KVB * P))
    wvT = din("wvT", (D, KVB * P))
    woT = din("woT", (H * P, D))
    qnw_ = din("qnw", (P, 1))
    knw_ = din("knw", (P, 1))
    pos0_ = din("pos0", (1, 1))
    ropeT_ = din("ropeT", (P, 2 * (NT + HALO)))
    halob_ = din("halob", (P, 1))

    res = nc.dram_tensor("res", [D, EXT], F32)
    outT = nc.dram_tensor("outT", [D, NT], F32, kind="ExternalOutput")

    sbufs = {}

    def sb(name, shape, dtype):
        sbufs[name] = ctx.enter_context(nc.sbuf_tensor(name, list(shape), dtype))
        return sbufs[name]

    lnxt = sb("lnxt", (P, DB * max(ACH, FTC)), BF16)
    WQCH = max(1, DB // 4) * P
    bigB = sb("bigB", (P, max(H * ACH + 2 * H * WQCH, FB * FTC,
                              H * NT, 2 * H * WQCH)), BF16)
    kvb = sb("kvb", (P, max(2 * S, 2 * KVB * EXT, DB * FTC + FB * 256)), BF16)
    wbuf = sb("wbuf", (P, 2 * DB * P), BF16)
    xt2 = sb("xt2", (P, 2 * QC), F32)
    xb2 = sb("xb2", (P, 2 * QC), BF16)
    sqb = sb("sqb", (P, 2 * QC), BF16)
    sq2 = sb("sq2", (P, 2 * QC), F32)
    rest = sb("rest", (P, 2 * QC), F32)
    ptil = sb("ptil", (P, max(4 * QC, 2 * ACH, 2 * min(1024, NT))), BF16)
    rtil = sb("rtil", (P, max(2 * QC, ACH, min(1024, NT))), BF16)
    mtil = sb("mtil", (P, FTC), BF16)
    qsc = sb("qsc", (P, 3 * QC), F32)
    ropet = sb("ropet", (P, 2 * EXT), BF16)
    rtc = sb("rtc", (P, P), BF16)
    rows = sb("rows", (1, 2112), F32)
    rowsb = sb("rowsb", (1, 1024), BF16)
    small = sb("small", (P, 288), F32)
    smb = sb("smb", (P, 160), BF16)
    mask2 = sb("mask2", (P, 2 * P), BF16)
    ident = sb("ident", (P, P), BF16)

    ONES = 0
    BQ = 1
    BK = BQ + H
    BV = BK + 1
    BO = BV + 1
    B0A = BO + DB
    B0B = B0A + FB
    B1A = B0B + DB
    B1B = B1A + FB
    QNW = B1B + DB
    KNW = QNW + 1
    HALOB = KNW + 1
    WBARK = HALOB + 1
    WBARV = WBARK + 1
    WGD = WBARV + 1
    ONE_ROW_P0 = WGD + DB
    C_NSHIFT = ONE_ROW_P0 + P
    C_EPSLN = C_NSHIFT + 1
    C_EPSRMS = C_EPSLN + 1
    C_EPSRMS128 = C_EPSRMS + 1
    C_LNHALF128 = C_EPSRMS128 + 1
    HALOR = C_LNHALF128 + 1
    assert HALOR + 1 <= 288

    R_MEAN, R_RSTD, R_T0, R_T1 = 0, 512, 1024, 1536
    R_DEN = 1024
    R_MISC = 2048
    RB_RSTD, RB_MB, RB_DEN, RB_MASK = 0, 512, 0, 0


    pe = Sem(nc, "pe")
    act = Sem(nc, "act")
    dve = Sem(nc, "dve")
    pool = Sem(nc, "pool")
    s_bias = Sem(nc, "s_bias")
    s_wkv = Sem(nc, "s_wkv")
    s_x = [Sem(nc, "s_x0"), Sem(nc, "s_x1")]
    s_xb = [Sem(nc, "s_xb0"), Sem(nc, "s_xb1")]
    s_resw = [Sem(nc, "s_resw0"), Sem(nc, "s_resw1")]
    s_w = [Sem(nc, "s_w0"), Sem(nc, "s_w1")]
    s_wo = [Sem(nc, "s_wo0"), Sem(nc, "s_wo1")]
    s_dbg = Sem(nc, "s_dbg")
    all_dma_sems = [s_bias, s_wkv, s_x[0], s_x[1], s_xb[0], s_xb[1],
                    s_resw[0], s_resw[1], s_w[0], s_w[1], s_wo[0], s_wo[1],
                    s_dbg]

    block = ctx.enter_context(nc.Block())

    ones_col = small[:, ONES:ONES + 1]
    ones_row = small[0:1, ONE_ROW_P0:ONE_ROW_P0 + P]
    ones_bf = smb[:, 0:1]
    ones128_bf = smb[:, 1:2]
    ones_row_bf = smb[0:1, 8:8 + P]
    WGDB = 140
    scale_a = 1.0 / math.sqrt(P)


    def PE(f): block.tensor(f)
    def ACT(f): block.scalar(f)
    def DVE(f): block.vector(f)
    def POOL(f): block.gpsimd(f)
    def SP(f): block.sync(f)

    _pin_rot = [0]

    def act_set_pin(s):
        c = R_MISC + 8 + (_pin_rot[0] % 8)
        _pin_rot[0] += 1
        s.activation(rows[0:1, c:c + 1],
                     small[0:1, C_EPSRMS:C_EPSRMS + 1], AF.Ln)

    def join_all():
        marks = [(s, s.n) for s in ([pe, act, dve, pool] + all_dma_sems)]

        def mk():
            def f(e):
                for s, n in marks:
                    if n:
                        e.wait_ge(s.h, n)
            return f
        PE(mk()); ACT(mk()); DVE(mk()); POOL(mk()); SP(mk())

    def join_light():
        marks = [(s, s.n) for s in (pe, act, dve, pool, s_resw[0], s_resw[1],
                                    s_x[0], s_x[1], s_xb[0], s_xb[1])]

        def mk():
            def f(e):
                for s, n in marks:
                    if n:
                        e.wait_ge(s.h, n)
            return f
        PE(mk()); ACT(mk()); DVE(mk()); POOL(mk()); SP(mk())

    def phase_setup():
        def f(g):
            pool.inc(g.memset(small[:, ONES:ONES + 1], 1.0))
            pool.inc(g.memset(small[0:1, ONE_ROW_P0:ONE_ROW_P0 + P], 1.0))
            pool.inc(g.memset(smb[:, 0:1], 1.0))
            pool.inc(g.memset(smb[:, 1:2], 1.0 / 128.0))
            pool.inc(g.memset(smb[0:1, 8:8 + P], 1.0))
            pool.inc(g.memset(small[:, C_NSHIFT:C_NSHIFT + 1], -CSHIFT))
            pool.inc(g.memset(small[:, C_EPSLN:C_EPSLN + 1], 1e-5))
            pool.inc(g.memset(small[:, C_EPSRMS:C_EPSRMS + 1], 1e-6))
            pool.inc(g.memset(small[:, C_EPSRMS128:C_EPSRMS128 + 1], 128e-6))
            pool.inc(g.memset(small[:, C_LNHALF128:C_LNHALF128 + 1],
                              0.5 * math.log(128.0)))
            pool.inc(g.memset(ident[:], 0.0))
            pool.inc(g.memset(mask2[:, 0:P], 0.0))
            pool.inc(g.memset(mask2[:, P:2 * P], 0.0))
            g.wait_ge(pool.h, pool.n)
            pool.inc(g.affine_select(out=ident[:], in_=ident[:],
                                     compare_op=ALU.not_equal, fill=1.0,
                                     base=0, pattern=[[-1, P]], channel_multiplier=1))
            pool.inc(g.affine_select(out=mask2[:, 0:P], in_=mask2[:, 0:P],
                                     compare_op=ALU.is_ge, fill=-30000.0,
                                     base=-1, pattern=[[-1, P]], channel_multiplier=1))
            pool.inc(g.affine_select(out=mask2[:, P:2 * P], in_=mask2[:, P:2 * P],
                                     compare_op=ALU.is_ge, fill=-30000.0,
                                     base=0, pattern=[[1, P]], channel_multiplier=-1))
        POOL(f)

        def fs(s):
            s_bias.inc(s.dma_start(out=small[:, BQ:BQ + H], in_=bqT.ap()), 16)
            s_bias.inc(s.dma_start(out=small[:, BK:BK + 1], in_=bk_.ap()), 16)
            s_bias.inc(s.dma_start(out=small[:, BV:BV + 1], in_=bv_.ap()), 16)
            s_bias.inc(s.dma_start(out=small[:, BO:BO + DB], in_=boT.ap()), 16)
            s_bias.inc(s.dma_start(out=small[:, B0A:B0A + FB], in_=b0aT.ap()), 16)
            s_bias.inc(s.dma_start(out=small[:, B0B:B0B + DB], in_=b0bT.ap()), 16)
            s_bias.inc(s.dma_start(out=small[:, B1A:B1A + FB], in_=b1aT.ap()), 16)
            s_bias.inc(s.dma_start(out=small[:, B1B:B1B + DB], in_=b1bT.ap()), 16)
            s_bias.inc(s.dma_start(out=small[:, QNW:QNW + 1], in_=qnw_.ap()), 16)
            s_bias.inc(s.dma_start(out=small[:, KNW:KNW + 1], in_=knw_.ap()), 16)
            s_bias.inc(s.dma_start(out=small[:, HALOB:HALOB + 1], in_=halob_.ap()), 16)
            s_bias.inc(s.dma_start(out=rows[0:1, R_MISC:R_MISC + 1], in_=pos0_.ap()), 16)
        SP(fs)

        def fv(v):
            v.wait_ge(s_bias.h, s_bias.n)
            dve.inc(v.tensor_scalar_mul(small[:, BQ:BQ + H], small[:, BQ:BQ + H], scale_a))
            v.wait_ge(dve.h, dve.n)
            dve.inc(v.tensor_scalar_mul(small[:, QNW:QNW + 1], small[:, QNW:QNW + 1], scale_a))
            v.wait_ge(dve.h, dve.n)
            dve.inc(v.tensor_copy(small[:, HALOR:HALOR + 1], small[:, HALOB:HALOB + 1]))
            v.wait_ge(dve.h, dve.n)
            dve.inc(v.tensor_scalar_add(small[:, HALOB:HALOB + 1], small[:, HALOB:HALOB + 1], -CSHIFT))
        DVE(fv)

    def emit_ln_stats(src, col0, psA, psB, ps_sx, ps_sx2, eps_col, gate=False, ps_g=None):
        act0, pe0, dve0 = act.n, pe.n, dve.n
        per_db = 3 if gate else 2
        xw = []

        def fs(g):
            for db in range(DB):
                if db >= 2:
                    g.wait_ge(dve.h, dve0 + db - 1)
                    g.wait_ge(pe.h, pe0 + per_db * (db - 1))
                else:
                    if act0:
                        g.wait_ge(act.h, act0)
                    if pe0:
                        g.wait_ge(pe.h, pe0)
                    if dve0:
                        g.wait_ge(dve.h, dve0)
                xw.append(s_xb[db % 2].inc(g.dma_start(
                    out=xb2[:, (db % 2) * QC:(db % 2) * QC + QC],
                    in_=src[db * P:(db + 1) * P, col0:col0 + QC]), 16))
        POOL(fs)

        def fa(v):
            for db in range(DB):
                v.wait_ge(s_xb[db % 2].h, xw[db])
                if db >= 2:
                    v.wait_ge(pe.h, pe0 + per_db * (db - 1))
                dve.inc(v.tensor_tensor(sqb[:, (db % 2) * QC:(db % 2) * QC + QC],
                                        xb2[:, (db % 2) * QC:(db % 2) * QC + QC],
                                        xb2[:, (db % 2) * QC:(db % 2) * QC + QC],
                                        ALU.mult))
        DVE(fa)

        def fp(t):
            for db in range(DB):
                sl = (db % 2) * QC
                t.wait_ge(s_xb[db % 2].h, xw[db])
                pe.inc(t.matmul(ps_sx[0:1, 0:QC], ones_bf, xb2[:, sl:sl + QC],
                                start=(db == 0), stop=(db == DB - 1)))
                if gate:
                    pe.inc(t.matmul(ps_g[0:1, 0:QC], smb[:, WGDB + db:WGDB + db + 1],
                                    xb2[:, sl:sl + QC],
                                    start=(db == 0), stop=(db == DB - 1)))
                t.wait_ge(dve.h, dve0 + db + 1)
                pe.inc(t.matmul(ps_sx2[0:1, 0:QC], ones_bf, sqb[:, sl:sl + QC],
                                start=(db == 0), stop=(db == DB - 1)))
        PE(fp)

        def fv(v):
            v.wait_ge(pe.h, pe.n)
            dve.inc(v.tensor_scalar_mul(rows[0:1, R_MEAN:R_MEAN + QC],
                                        ps_sx[0:1, 0:QC], 1.0 / D))
            if gate:
                v.wait_ge(dve.h, dve.n)
                dve.inc(v.tensor_copy(rows[0:1, R_T1:R_T1 + QC], ps_g[0:1, 0:QC]))
            v.wait_ge(dve.h, dve.n)
            dve.inc(v.tensor_scalar_mul(rows[0:1, R_T0:R_T0 + QC],
                                        ps_sx2[0:1, 0:QC], 1.0 / D))
            v.wait_ge(dve.h, dve.n)
            dve.inc(v.tensor_tensor(rows[0:1, R_RSTD:R_RSTD + QC],
                                    rows[0:1, R_MEAN:R_MEAN + QC],
                                    rows[0:1, R_MEAN:R_MEAN + QC], ALU.mult))
            v.wait_ge(dve.h, dve.n)
            dve.inc(v.scalar_tensor_tensor(rows[0:1, R_T0:R_T0 + QC],
                                           rows[0:1, R_T0:R_T0 + QC],
                                           small[0:1, eps_col:eps_col + 1],
                                           rows[0:1, R_RSTD:R_RSTD + QC],
                                           ALU.add, ALU.subtract))
        DVE(fv)

        def fa2(s):
            s.wait_ge(dve.h, dve.n)
            act.inc(s.activation(rows[0:1, R_T0:R_T0 + QC],
                                 rows[0:1, R_T0:R_T0 + QC], AF.Ln))
            s.wait_ge(act.h, act.n)
            act.inc(s.activation(rows[0:1, R_RSTD:R_RSTD + QC],
                                 rows[0:1, R_T0:R_T0 + QC], AF.Exp, scale=-0.5))
            s.wait_ge(act.h, act.n)
            act.inc(s.activation(rowsb[0:1, RB_RSTD:RB_RSTD + QC],
                                 rows[0:1, R_T0:R_T0 + QC], AF.Exp, scale=-0.5))
        ACT(fa2)

        def fv2(v):
            v.wait_ge(act.h, act.n)
            dve.inc(v.tensor_tensor(rowsb[0:1, RB_MB:RB_MB + QC],
                                    rows[0:1, R_MEAN:R_MEAN + QC],
                                    rows[0:1, R_RSTD:R_RSTD + QC], ALU.mult))
        DVE(fv2)

        def fp2(t):
            t.wait_ge(dve.h, dve.n)
            pe.inc(t.matmul(psA[:, 0:QC], ones_row_bf,
                            rowsb[0:1, RB_RSTD:RB_RSTD + QC],
                            start=True, stop=True))
            pe.inc(t.matmul(psB[:, 0:QC], ones_row_bf,
                            rowsb[0:1, RB_MB:RB_MB + QC],
                            start=True, stop=True))
        PE(fp2)

    def emit_ln_apply(src, col0, psA, psB, dst, span, dcol):
        dve0, pe0a, act0a = dve.n, pe.n, act.n
        xw = []

        def fs(s):
            for db in range(DB):
                if db >= 2:
                    s.wait_ge(dve.h, dve0 + 2 * (db - 1))
                else:
                    if pe0a:
                        s.wait_ge(pe.h, pe0a)
                    if act0a:
                        s.wait_ge(act.h, act0a)
                xw.append(s_x[db % 2].inc(s.dma_start(
                    out=xt2[:, (db % 2) * QC:(db % 2) * QC + QC],
                    in_=src[db * P:(db + 1) * P, col0:col0 + QC]), 16))
        SP(fs)

        def fv(v):
            if pe0a:
                v.wait_ge(pe.h, pe0a)
            for db in range(DB):
                sl = (db % 2) * QC
                v.wait_ge(s_x[db % 2].h, xw[db])
                dve.inc(v.tensor_tensor(sq2[:, sl:sl + QC], xt2[:, sl:sl + QC],
                                        psA[:, 0:QC], ALU.mult))
                v.wait_ge(dve.h, dve.n)
                dve.inc(v.tensor_tensor(dst[:, db * span + dcol:db * span + dcol + QC],
                                        sq2[:, sl:sl + QC], psB[:, 0:QC], ALU.subtract))
        DVE(fv)

    KOFF, VOFF = 0, S

    def phase_kv():
        def fs(g):
            s_wkv.inc(g.dma_start(
                out=wbuf[:, 0:DB * P].rearrange("p (db c) -> p db c", db=DB),
                in_=Wk.ap().rearrange("(db p) c -> p db c", p=P)), 16)
            s_wkv.inc(g.dma_start(
                out=wbuf[:, DB * P:2 * DB * P].rearrange("p (db c) -> p db c", db=DB),
                in_=Wv.ap().rearrange("(db p) c -> p db c", p=P)), 16)
        POOL(fs)

        with nc.psum_tensor([P, 2], F32) as psw:
            def fp(t):
                t.wait_ge(s_wkv.h, s_wkv.n)
                for db in range(DB):
                    pe.inc(t.matmul(psw[:, 0:1], wbuf[:, db * P:(db + 1) * P],
                                    ones_bf, start=(db == 0), stop=(db == DB - 1)))
                for db in range(DB):
                    pe.inc(t.matmul(psw[:, 1:2], wbuf[:, (DB + db) * P:(DB + db + 1) * P],
                                    ones_bf, start=(db == 0), stop=(db == DB - 1)))
            PE(fp)

            def fv(v):
                v.wait_ge(pe.h, pe.n)
                dve.inc(v.tensor_copy(small[:, WBARK:WBARK + 1], psw[:, 0:1]))
                v.wait_ge(dve.h, dve.n)
                dve.inc(v.tensor_copy(small[:, WBARV:WBARV + 1], psw[:, 1:2]))
            DVE(fv)

        for c in range(SC):
            col0 = c * QC
            with nc.psum_tensor([P, QC], F32) as psA, \
                 nc.psum_tensor([P, QC], F32) as psB, \
                 nc.psum_tensor([1, QC], F32) as ps_sx, \
                 nc.psum_tensor([1, QC], F32) as ps_sx2, \
                 nc.psum_tensor([P, QC], F32) as psK, \
                 nc.psum_tensor([P, QC], F32) as psV:
                emit_ln_stats(xTb.ap(), col0, psA, psB, ps_sx, ps_sx2, C_EPSLN)

                pe1 = pe.n
                xw2 = []
                act1k = act.n

                def fs2(g, col0=col0):
                    for db in range(DB):
                        if db >= 2:
                            g.wait_ge(pe.h, pe1 + 2 * (db - 1))
                        else:
                            g.wait_ge(pe.h, pe1)
                            g.wait_ge(act.h, act1k)
                        xw2.append(s_xb[db % 2].inc(g.dma_start(
                            out=xb2[:, (db % 2) * QC:(db % 2) * QC + QC],
                            in_=xTb.ap()[db * P:(db + 1) * P, col0:col0 + QC]), 16))
                POOL(fs2)

                def fp2(t):
                    for db in range(DB):
                        xa = xb2[:, (db % 2) * QC:(db % 2) * QC + QC]
                        t.wait_ge(s_xb[db % 2].h, xw2[db])
                        pe.inc(t.matmul(psK[:, 0:QC],
                                        wbuf[:, db * P:(db + 1) * P],
                                        xa, start=(db == 0), stop=(db == DB - 1)))
                        pe.inc(t.matmul(psV[:, 0:QC],
                                        wbuf[:, (DB + db) * P:(DB + db + 1) * P],
                                        xa, start=(db == 0), stop=(db == DB - 1)))
                PE(fp2)

                def fv2(v, col0=col0):
                    v.wait_ge(pe.h, pe.n)
                    dve.inc(v.tensor_copy(xt2[:, 0:QC], psA[:, 0:QC]))
                    v.wait_ge(dve.h, dve.n)
                    dve.inc(v.tensor_scalar(sq2[:, 0:QC], psB[:, 0:QC],
                                            small[:, WBARK:WBARK + 1], None, ALU.mult))
                    v.wait_ge(dve.h, dve.n)
                    dve.inc(v.tensor_scalar(sq2[:, QC:2 * QC], psB[:, 0:QC],
                                            small[:, WBARV:WBARV + 1], None, ALU.mult))
                    v.wait_ge(dve.h, dve.n)
                    dve.inc(v.tensor_tensor(qsc[:, 0:QC], psK[:, 0:QC],
                                            xt2[:, 0:QC], ALU.mult))
                    v.wait_ge(dve.h, dve.n)
                    dve.inc(v.tensor_tensor(qsc[:, QC:2 * QC], psV[:, 0:QC],
                                            xt2[:, 0:QC], ALU.mult))
                    v.wait_ge(dve.h, dve.n)
                    dve.inc(v.scalar_tensor_tensor(
                        kvb[:, KOFF + col0:KOFF + col0 + QC],
                        qsc[:, 0:QC], small[:, BK:BK + 1], sq2[:, 0:QC],
                        ALU.add, ALU.subtract))
                    v.wait_ge(dve.h, dve.n)
                    dve.inc(v.scalar_tensor_tensor(
                        ptil[:, 0:QC],
                        qsc[:, QC:2 * QC], small[:, BV:BV + 1], sq2[:, QC:2 * QC],
                        ALU.add, ALU.subtract))
                DVE(fv2)

                nblk = QC // P
                with nc.psum_tensor([P, P], BF16) as pst:
                    pe2 = pe.n
                    dve2 = dve.n

                    def ft(t, col0=col0):
                        for i in range(nblk):
                            t.wait_ge(dve.h, dve2 + i)
                            pe.inc(t.transpose(pst[:], ptil[:, i * P:(i + 1) * P], ident[:]))
                    PE(ft)

                    def fv3(v, col0=col0):
                        for i in range(nblk):
                            v.wait_ge(pe.h, pe2 + i + 1)
                            dve.inc(v.tensor_copy(
                                kvb[:, VOFF + col0 + i * P:VOFF + col0 + (i + 1) * P],
                                pst[:]))
                    DVE(fv3)

    def emit_oproj(wdram, wreg, wreg_off, av, av_span, av_off, tok0, ntok,
                   bias_off, src, dst, dst_off, wpre_mark=None):
        DQ = max(1, DB // 4)
        nq = (DB + DQ - 1) // DQ
        nch = ntok // QC
        ncols = DQ * P
        wn = [0, 0]
        pe_done = {}
        dve_done = {}
        tiles = [(q, cc, db) for q in range(nq) for cc in range(nch)
                 for db in range(q * DQ, min(DB, (q + 1) * DQ))]
        qtiles = {}
        for i, (q, cc, db) in enumerate(tiles):
            qtiles.setdefault(q, []).append(i)

        pending_w = [None]

        def flush_w():
            if pending_w[0] is not None:
                SP(pending_w[0])
                pending_w[0] = None

        with nc.psum_tensor([P, QC], F32) as po0, \
             nc.psum_tensor([P, QC], F32) as po1:
            pos = [po0, po1]
            for q in range(nq):
                def fg(g, q=q,
                       pm=(wpre_mark if wpre_mark is not None else pe.n)):
                    if q >= 2:
                        g.wait_ge(pe.h, pe_done[qtiles[q - 2][-1]])
                    elif pm:
                        g.wait_ge(pe.h, pm)
                    s_wo[q % 2].inc(g.dma_start(
                        out=wreg[:, wreg_off + (q % 2) * H * ncols:
                                 wreg_off + (q % 2) * H * ncols + H * ncols]
                            .rearrange("p (hb c) -> p hb c", hb=H),
                        in_=wdram.rearrange("(hb p) c -> p hb c", p=P)
                            [:, :, q * ncols:q * ncols + min(ncols, D - q * ncols)]), 16)
                    wn[q % 2] = s_wo[q % 2].n
                POOL(fg)

                for i in qtiles[q]:
                    _, cc, db = tiles[i]
                    col0 = tok0 + cc * QC
                    sl = i % 2
                    woff2 = wreg_off + (q % 2) * H * ncols

                    def fp(t, q=q, cc=cc, db=db, sl=sl, woff2=woff2, i=i,
                           wmark=wn[q % 2], dm0=dve.n):
                        t.wait_ge(s_wo[q % 2].h, wmark)
                        if i >= 2:
                            t.wait_ge(dve.h, dve_done[i - 2])
                        elif dm0:
                            t.wait_ge(dve.h, dm0)
                        for hb in range(H):
                            pe.inc(t.matmul(
                                pos[sl][:, 0:QC],
                                wreg[:, woff2 + hb * ncols + (db - q * DQ) * P:
                                     woff2 + hb * ncols + (db - q * DQ) * P + P],
                                av[:, hb * av_span + av_off + cc * QC:
                                   hb * av_span + av_off + (cc + 1) * QC],
                                start=(hb == 0), stop=(hb == H - 1)))
                        pe_done[i] = pe.n
                    PE(fp)

                    def fs(g, db=db, col0=col0, sl=sl,
                           dm=dve_done.get(i - 2, dve.n)):
                        if dm:
                            g.wait_ge(dve.h, dm)
                        s_x[sl].inc(g.dma_start(
                            out=xt2[:, sl * QC:sl * QC + QC],
                            in_=src[db * P:(db + 1) * P, col0:col0 + QC]), 16)
                    SP(fs)
                    flush_w()
                    xn = s_x[sl].n

                    def fv(v, db=db, sl=sl, i=i):
                        v.wait_ge(pe.h, pe_done[i])
                        v.wait_ge(s_x[sl].h, xn)
                        v.wait_ge(s_resw[sl].h, s_resw[sl].n)
                        if bias_off is not None:
                            dve.inc(v.scalar_tensor_tensor(
                                rest[:, sl * QC:sl * QC + QC], pos[sl][:, 0:QC],
                                small[:, bias_off + db:bias_off + db + 1],
                                xt2[:, sl * QC:sl * QC + QC], ALU.add, ALU.add))
                        else:
                            dve.inc(v.tensor_tensor(
                                rest[:, sl * QC:sl * QC + QC], pos[sl][:, 0:QC],
                                xt2[:, sl * QC:sl * QC + QC], ALU.add))
                        dve_done[i] = dve.n
                    DVE(fv)

                    def fs2(g, db=db, col0=col0, sl=sl, i=i,
                            dd=None):
                        g.wait_ge(dve.h, dve_done[i])
                        s_resw[sl].inc(g.dma_start(
                            out=dst[db * P:(db + 1) * P,
                                    col0 - dst_off:col0 - dst_off + QC],
                            in_=rest[:, sl * QC:sl * QC + QC]), 16)
                    pending_w[0] = fs2
        flush_w()

    def phase_attn_a_slab(slab):
        t0 = slab * ACH
        nch = ACH // QC
        for cc in range(nch):
            col0 = t0 + cc * QC
            with nc.psum_tensor([P, QC], F32) as psA, \
                 nc.psum_tensor([P, QC], F32) as psB, \
                 nc.psum_tensor([1, QC], F32) as ps_sx, \
                 nc.psum_tensor([1, QC], F32) as ps_sx2:
                emit_ln_stats(xTe.ap(), col0, psA, psB, ps_sx, ps_sx2, C_EPSLN)
                emit_ln_apply(xTe.ap(), col0, psA, psB, lnxt, ACH, cc * QC)

        for h in range(H):
            wslot = (h % 2) * DB * P
            pe_before = pe.n

            def fg(g, h=h, wslot=wslot):
                if h >= 2:
                    g.wait_ge(pe.h, pe_before)
                s_w[h % 2].inc(g.dma_start(
                    out=wbuf[:, wslot:wslot + DB * P].rearrange("p (db c) -> p db c", db=DB),
                    in_=Wq.ap().rearrange("(db p) c -> p db c", p=P)[:, :, h * P:(h + 1) * P]), 16)
            POOL(fg)

            with nc.psum_tensor([P, QC], F32) as psq0, \
                 nc.psum_tensor([P, QC], F32) as psq1:
                psqs = [psq0, psq1]
                wsem, wn = s_w[h % 2], s_w[h % 2].n
                pe1 = pe.n
                dve1 = dve.n

                def fp(t, h=h, wslot=wslot):
                    t.wait_ge(wsem.h, wn)
                    for cc in range(nch):
                        if cc >= 2:
                            t.wait_ge(dve.h, dve1 + cc - 1)
                        elif dve1:
                            t.wait_ge(dve.h, dve1)
                        for db in range(DB):
                            pe.inc(t.matmul(
                                psqs[cc % 2][:, 0:QC],
                                wbuf[:, wslot + db * P:wslot + (db + 1) * P],
                                lnxt[:, db * ACH + cc * QC:db * ACH + cc * QC + QC],
                                start=(db == 0), stop=(db == DB - 1)))
                PE(fp)

                def fa(v, h=h):
                    for cc in range(nch):
                        v.wait_ge(pe.h, pe1 + DB * (cc + 1))
                        dve.inc(v.tensor_scalar(
                            bigB[:, h * ACH + cc * QC:h * ACH + cc * QC + QC],
                            psqs[cc % 2][:, 0:QC], scale_a,
                            small[:, BQ + h:BQ + h + 1], ALU.mult, ALU.add))
                DVE(fa)

        preflash_pe = pe.n
        FQ = QC
        for pair in range(H // 2):
            hA, hB = 2 * pair, 2 * pair + 1
            for qi in range(ACH // FQ):
                qoffA = hA * ACH + qi * FQ
                qoffB = hB * ACH + qi * FQ
                with nc.psum_tensor([P, FQ], F32) as psOA, \
                     nc.psum_tensor([P, FQ], F32) as psOB, \
                     nc.psum_tensor([1, FQ], F32) as psDA, \
                     nc.psum_tensor([1, FQ], F32) as psDB, \
                     nc.psum_tensor([P, FQ], F32) as psSA, \
                     nc.psum_tensor([P, FQ], F32) as psSB:
                    pe0, act0, dve0 = pe.n, act.n, dve.n

                    def fp(t):
                        if dve0:
                            t.wait_ge(dve.h, dve0)
                        for kb in range(KB):
                            if kb >= 1:
                                t.wait_ge(act.h, act0 + 2 * (kb - 1) + 1)
                            pe.inc(t.matmul(psSA[:, 0:FQ],
                                            kvb[:, KOFF + kb * P:KOFF + (kb + 1) * P],
                                            bigB[:, qoffA:qoffA + FQ],
                                            start=True, stop=True))
                            if kb >= 1:
                                t.wait_ge(act.h, act0 + 2 * (kb - 1) + 2)
                            pe.inc(t.matmul(psSB[:, 0:FQ],
                                            kvb[:, KOFF + kb * P:KOFF + (kb + 1) * P],
                                            bigB[:, qoffB:qoffB + FQ],
                                            start=True, stop=True))
                            if kb >= 1:
                                for (pD, pO, sl) in ((psDA, psOA, ((kb - 1) % 2)),
                                                     (psDB, psOB, 2 + ((kb - 1) % 2))):
                                    pe.inc(t.matmul(pO[:, 0:FQ],
                                                    kvb[:, VOFF + (kb - 1) * P:VOFF + kb * P],
                                                    ptil[:, sl * FQ:(sl + 1) * FQ],
                                                    start=(kb == 1), stop=False))
                                    pe.inc(t.matmul(pD[0:1, 0:FQ], ones_bf,
                                                    ptil[:, sl * FQ:(sl + 1) * FQ],
                                                    start=(kb == 1), stop=False))
                        t.wait_ge(act.h, act0 + 2 * KB)
                        for (pD, pO, sl) in ((psDA, psOA, ((KB - 1) % 2)),
                                             (psDB, psOB, 2 + ((KB - 1) % 2))):
                            pe.inc(t.matmul(pO[:, 0:FQ],
                                            kvb[:, VOFF + (KB - 1) * P:VOFF + KB * P],
                                            ptil[:, sl * FQ:(sl + 1) * FQ],
                                            start=False, stop=True))
                            pe.inc(t.matmul(pD[0:1, 0:FQ], ones_bf,
                                            ptil[:, sl * FQ:(sl + 1) * FQ],
                                            start=False, stop=True))
                    PE(fp)
                    pe_all = pe.n

                    def fa(s):
                        act_set_pin(s)
                        for kb in range(KB):
                            s.wait_ge(pe.h, pe0 + (1 if kb == 0 else 6 * kb - 3))
                            act.inc(s.activation(
                                ptil[:, (kb % 2) * FQ:(kb % 2) * FQ + FQ],
                                psSA[:, 0:FQ], AF.Exp,
                                bias=small[:, C_NSHIFT:C_NSHIFT + 1]))
                            s.wait_ge(pe.h, pe0 + (2 if kb == 0 else 6 * kb - 2))
                            act.inc(s.activation(
                                ptil[:, (2 + kb % 2) * FQ:(2 + kb % 2) * FQ + FQ],
                                psSB[:, 0:FQ], AF.Exp,
                                bias=small[:, C_NSHIFT:C_NSHIFT + 1]))
                    ACT(fa)

                    with nc.psum_tensor([P, FQ], F32) as psR:
                        for (pD, pO, qoff, rbo) in ((psDA, psOA, qoffA, 0),
                                                    (psDB, psOB, qoffB, FQ)):
                            def ft(s, pD=pD, rbo=rbo):
                                s.wait_ge(pe.h, pe_all)
                                act.inc(s.activation(rows[0:1, R_DEN + rbo:R_DEN + rbo + FQ],
                                                     pD[0:1, 0:FQ], AF.Ln))
                                s.wait_ge(act.h, act.n)
                                act.inc(s.activation(rowsb[0:1, RB_DEN + rbo:RB_DEN + rbo + FQ],
                                                     rows[0:1, R_DEN + rbo:R_DEN + rbo + FQ],
                                                     AF.Exp, scale=-1.0))
                            ACT(ft)

                            def fpr(t, rbo=rbo):
                                t.wait_ge(act.h, act.n)
                                if dve.n:
                                    t.wait_ge(dve.h, dve.n)
                                pe.inc(t.matmul(psR[:, 0:FQ], ones_row_bf,
                                                rowsb[0:1, RB_DEN + rbo:RB_DEN + rbo + FQ],
                                                start=True, stop=True))
                            PE(fpr)

                            def fvt(v, pO=pO, qoff=qoff, rbo=rbo):
                                v.wait_ge(pe.h, pe.n)
                                dve.inc(v.tensor_copy(rtil[:, rbo:rbo + FQ], psR[:, 0:FQ]))
                                v.wait_ge(dve.h, dve.n)
                                dve.inc(v.tensor_tensor(bigB[:, qoff:qoff + FQ],
                                                        pO[:, 0:FQ], rtil[:, rbo:rbo + FQ],
                                                        ALU.mult))
                            DVE(fvt)

        emit_oproj(Wo.ap(), bigB, H * ACH, bigB, ACH, 0, t0, ACH, BO,
                   xTe.ap(), res.ap(), 0, wpre_mark=preflash_pe)
        join_light()

    W0B_CH = P
    OUT0_OFF = 0
    WB_OFF = DB * FTC

    def phase_ffn_gate_setup():
        def fs(g):
            s_wkv.inc(g.dma_start(
                out=wbuf[:, 0:DB * 2].rearrange("p (db e) -> p db e", e=2),
                in_=WgT.ap().rearrange("(db p) e -> p db e", p=P)), 16)
        POOL(fs)

        def fv(v):
            v.wait_ge(s_wkv.h, s_wkv.n)
            for db in range(DB):
                dve.inc(v.tensor_tensor(
                    smb[:, WGDB + db:WGDB + db + 1],
                    wbuf[:, 2 * db:2 * db + 1],
                    wbuf[:, 2 * db + 1:2 * db + 2],
                    ALU.subtract))
        DVE(fv)

        with nc.psum_tensor([1, 1], F32) as psb:
            def fp(t):
                t.wait_ge(dve.h, dve.n)
                for db in range(DB):
                    pe.inc(t.matmul(psb[0:1, 0:1], smb[:, WGDB + db:WGDB + db + 1],
                                    ones_bf, start=(db == 0), stop=(db == DB - 1)))
            PE(fp)

            def fv2(v):
                v.wait_ge(pe.h, pe.n)
                dve.inc(v.tensor_copy(rows[0:1, R_MISC + 4:R_MISC + 5], psb[0:1, 0:1]))
            DVE(fv2)

    def phase_ffn(src, dst, tok0, ntok, dst_off):
        NCH = ntok // FTC
        NTC = FTC // QC
        wb_free = [pe.n]
        for ch in range(NCH):
            base = tok0 + ch * FTC
            for tc in range(NTC):
                col0 = base + tc * QC
                with nc.psum_tensor([P, QC], F32) as psA, \
                     nc.psum_tensor([P, QC], F32) as psB, \
                     nc.psum_tensor([1, QC], F32) as ps_sx, \
                     nc.psum_tensor([1, QC], F32) as ps_sx2, \
                     nc.psum_tensor([1, QC], F32) as ps_g:
                    emit_ln_stats(src, col0, psA, psB, ps_sx, ps_sx2, C_EPSLN,
                                  gate=True, ps_g=ps_g)
                    def fvg(v, tc=tc):
                        v.wait_ge(pe.h, pe.n)
                        v.wait_ge(dve.h, dve.n)
                        dve.inc(v.tensor_single_scalar(
                            rows[0:1, R_T0:R_T0 + QC], rows[0:1, R_MEAN:R_MEAN + QC],
                            rows[0:1, R_MISC + 4:R_MISC + 5], ALU.mult))
                        v.wait_ge(dve.h, dve.n)
                        dve.inc(v.tensor_tensor(rows[0:1, R_T1:R_T1 + QC],
                                                rows[0:1, R_T1:R_T1 + QC],
                                                rows[0:1, R_T0:R_T0 + QC],
                                                ALU.subtract))
                        v.wait_ge(dve.h, dve.n)
                        dve.inc(v.tensor_single_scalar(
                            rowsb[0:1, RB_MASK:RB_MASK + QC], rows[0:1, R_T1:R_T1 + QC],
                            0.0, ALU.is_ge))
                    DVE(fvg)

                    with nc.psum_tensor([P, QC], F32) as psM:
                        def fpm(t, tc=tc):
                            t.wait_ge(dve.h, dve.n)
                            pe.inc(t.matmul(psM[:, 0:QC], ones_row_bf,
                                            rowsb[0:1, RB_MASK:RB_MASK + QC],
                                            start=True, stop=True))
                        PE(fpm)

                        def fvm(v, tc=tc):
                            v.wait_ge(pe.h, pe.n)
                            dve.inc(v.tensor_copy(mtil[:, tc * QC:(tc + 1) * QC],
                                                  psM[:, 0:QC]))
                        DVE(fvm)

                    emit_ln_apply(src, col0, psA, psB, lnxt, FTC, tc * QC)

            for e in range(2):
                WaT = W0aT if e == 0 else W1aT
                WbT = W0bT if e == 0 else W1bT
                BA = B0A if e == 0 else B1A
                BB = B0B if e == 0 else B1B
                with nc.psum_tensor([P, QC], F32) as ph00, \
                     nc.psum_tensor([P, QC], F32) as ph01, \
                     nc.psum_tensor([P, QC], F32) as ph10, \
                     nc.psum_tensor([P, QC], F32) as ph11:
                    psH = [[ph00, ph01], [ph10, ph11]]
                    pe0, act0 = pe.n, act.n

                    def fgw(g, e=e, WaT=WaT):
                        for fb in range(FB):
                            if fb >= 2:
                                g.wait_ge(pe.h, pe0 + DB * NTC * (fb - 1))
                            else:
                                g.wait_ge(pe.h, pe0)
                            s_w[fb % 2].inc(g.dma_start(
                                out=wbuf[:, (fb % 2) * DB * P:(fb % 2 + 1) * DB * P]
                                    .rearrange("p (db c) -> p db c", db=DB),
                                in_=WaT.ap().rearrange("(db p) c -> p db c", p=P)
                                    [:, :, fb * P:(fb + 1) * P]), 16)
                    POOL(fgw)

                    wn0 = [s_w[0].n - (FB + 1) // 2 * 16, s_w[1].n - FB // 2 * 16]

                    def fp1(t, e=e, dm=dve.n):
                        if dm:
                            t.wait_ge(dve.h, dm)
                        for fb in range(FB):
                            t.wait_ge(s_w[fb % 2].h, wn0[fb % 2] + (fb // 2 + 1) * 16)
                            if fb >= 2:
                                t.wait_ge(act.h, act0 + NTC * (fb - 1))
                            elif act0:
                                t.wait_ge(act.h, act0)
                            for db in range(DB):
                                for tc in range(NTC):
                                    pe.inc(t.matmul(
                                        psH[fb % 2][tc][:, 0:QC],
                                        wbuf[:, (fb % 2) * DB * P + db * P:
                                             (fb % 2) * DB * P + (db + 1) * P],
                                        lnxt[:, db * FTC + tc * QC:db * FTC + (tc + 1) * QC],
                                        start=(db == 0), stop=(db == DB - 1)))
                    PE(fp1)

                    AFUN = AF.Gelu if e == 0 else AF.Relu

                    def fa1(s, e=e, BA=BA, AFUN=AFUN):
                        for fb in range(FB):
                            s.wait_ge(pe.h, pe0 + DB * NTC * fb + DB * NTC)
                            for tc in range(NTC):
                                act.inc(s.activation(
                                    bigB[:, fb * FTC + tc * QC:fb * FTC + (tc + 1) * QC],
                                    psH[fb % 2][tc][:, 0:QC], AFUN,
                                    bias=small[:, BA + fb:BA + fb + 1]))
                    ACT(fa1)

                ndq = D // W0B_CH
                wn2 = [0, 0]
                pe2_done = {}
                dve2_done = {}
                pend2 = [None]

                def flush2():
                    if pend2[0] is not None:
                        SP(pend2[0])
                        pend2[0] = None

                with nc.psum_tensor([P, QC], F32) as q20, \
                     nc.psum_tensor([P, QC], F32) as q21:
                    pq2 = [q20, q21]
                    mm2_pre = wb_free[0]
                    for dq in range(ndq):
                        def fgw2(g, e=e, WbT=WbT, dq=dq, pm=mm2_pre):
                            if dq >= 2:
                                g.wait_ge(pe.h, pe2_done[(dq - 2, NTC - 1)])
                            elif pm:
                                g.wait_ge(pe.h, pm)
                            s_wo[dq % 2].inc(g.dma_start(
                                out=kvb[:, WB_OFF + (dq % 2) * FB * W0B_CH:
                                     WB_OFF + (dq % 2 + 1) * FB * W0B_CH]
                                    .rearrange("p (fb c) -> p fb c", fb=FB),
                                in_=WbT.ap().rearrange("(fb p) c -> p fb c", p=P)
                                    [:, :, dq * W0B_CH:(dq + 1) * W0B_CH]), 16)
                            wn2[dq % 2] = s_wo[dq % 2].n
                        POOL(fgw2)

                        db = dq
                        woff2 = WB_OFF + (dq % 2) * FB * W0B_CH
                        for tc in range(NTC):
                            col0 = base + tc * QC
                            i2 = dq * NTC + tc
                            sl = i2 % 2

                            def fp2(t, db=db, tc=tc, sl=sl, i2=i2, dq=dq,
                                    woff2=woff2, am=act.n, wmark=wn2[dq % 2]):
                                t.wait_ge(s_wo[dq % 2].h, wmark)
                                t.wait_ge(act.h, am)
                                if i2 >= 2:
                                    t.wait_ge(dve.h, dve2_done[i2 - 2])
                                elif dve.n:
                                    t.wait_ge(dve.h, dve.n)
                                for fb in range(FB):
                                    pe.inc(t.matmul(
                                        pq2[sl][:, 0:QC],
                                        kvb[:, woff2 + fb * W0B_CH:
                                             woff2 + fb * W0B_CH + P],
                                        bigB[:, fb * FTC + tc * QC:fb * FTC + (tc + 1) * QC],
                                        start=(fb == 0), stop=(fb == FB - 1)))
                                pe2_done[(dq, tc)] = pe.n
                            PE(fp2)
                            pe2_done_i = None

                            if e == 0:
                                def fv20(v, db=db, tc=tc, BB=BB, sl=sl, i2=i2, dq=dq):
                                    v.wait_ge(pe.h, pe2_done[(dq, tc)])
                                    dve.inc(v.tensor_scalar(
                                        kvb[:, OUT0_OFF + db * FTC + tc * QC:
                                             OUT0_OFF + db * FTC + (tc + 1) * QC],
                                        pq2[sl][:, 0:QC],
                                        small[:, BB + db:BB + db + 1], None,
                                        ALU.add))
                                    dve2_done[i2] = dve.n
                                DVE(fv20)
                            else:
                                def fsx(g, db=db, col0=col0, sl=sl,
                                        dm=dve2_done.get(i2 - 2, dve.n)):
                                    if dm:
                                        g.wait_ge(dve.h, dm)
                                    s_x[sl].inc(g.dma_start(
                                        out=xt2[:, sl * QC:sl * QC + QC],
                                        in_=src[db * P:(db + 1) * P, col0:col0 + QC]), 16)
                                SP(fsx)
                                flush2()
                                xn = s_x[sl].n

                                def fv21(v, db=db, tc=tc, BB=BB, sl=sl, i2=i2,
                                         dq=dq, xn=xn):
                                    v.wait_ge(pe.h, pe2_done[(dq, tc)])
                                    v.wait_ge(s_x[sl].h, xn)
                                    v.wait_ge(s_resw[sl].h, s_resw[sl].n)
                                    dve.inc(v.scalar_tensor_tensor(
                                        sq2[:, sl * QC:sl * QC + QC], pq2[sl][:, 0:QC],
                                        small[:, BB + db:BB + db + 1],
                                        kvb[:, OUT0_OFF + db * FTC + tc * QC:
                                             OUT0_OFF + db * FTC + (tc + 1) * QC],
                                        ALU.add, ALU.subtract))
                                    v.wait_ge(dve.h, dve.n)
                                    dve.inc(v.tensor_tensor(
                                        qsc[:, sl * QC:sl * QC + QC],
                                        sq2[:, sl * QC:sl * QC + QC],
                                        mtil[:, tc * QC:(tc + 1) * QC], ALU.mult))
                                    v.wait_ge(dve.h, dve.n)
                                    dve.inc(v.scalar_tensor_tensor(
                                        sq2[:, sl * QC:sl * QC + QC], pq2[sl][:, 0:QC],
                                        small[:, BB + db:BB + db + 1],
                                        qsc[:, sl * QC:sl * QC + QC],
                                        ALU.add, ALU.subtract))
                                    v.wait_ge(dve.h, dve.n)
                                    dve.inc(v.tensor_tensor(
                                        rest[:, sl * QC:sl * QC + QC],
                                        sq2[:, sl * QC:sl * QC + QC],
                                        xt2[:, sl * QC:sl * QC + QC], ALU.add))
                                    dve2_done[i2] = dve.n
                                DVE(fv21)

                                def fso(g, db=db, col0=col0, dst=dst, sl=sl, i2=i2,
                                        dst_off=dst_off):
                                    g.wait_ge(dve.h, dve2_done[i2])
                                    s_resw[sl].inc(g.dma_start(
                                        out=dst[db * P:(db + 1) * P,
                                                col0 - dst_off:col0 - dst_off + QC],
                                        in_=rest[:, sl * QC:sl * QC + QC]), 16)
                                pend2[0] = fso

                    flush2()
                    wb_free[0] = pe.n

    EB = EXT // P
    NQB2 = NT // 1024 if NT % 1024 == 0 else NT // min(NT, 1024)
    FQ2 = min(1024, NT)
    VB_OFF = KVB * EXT

    def phase_ropetab():
        def fg(g):
            pool.inc(g.memset(sq2[:, 0:P], 0.0))
            g.wait_ge(pool.h, pool.n)
            pool.inc(g.affine_select(out=sq2[:, 0:P], in_=sq2[:, 0:P],
                                     compare_op=ALU.not_equal, fill=1.0,
                                     base=-(P // 2), pattern=[[1, P]],
                                     channel_multiplier=-1))
            g.wait_ge(pool.h, pool.n)
            pool.inc(g.affine_select(out=sq2[:, 0:P], in_=sq2[:, 0:P],
                                     compare_op=ALU.not_equal, fill=-1.0,
                                     base=(P // 2), pattern=[[1, P]],
                                     channel_multiplier=-1))
            g.wait_ge(pool.h, pool.n)
            s_wkv.inc(g.dma_start(out=ropet[:, 0:2 * EXT], in_=ropeT_.ap()), 16)
        POOL(fg)

        def fv0(v):
            v.wait_ge(pool.h, pool.n)
            dve.inc(v.tensor_copy(rtc[:, 0:P], sq2[:, 0:P]))
            v.wait_ge(s_wkv.h, s_wkv.n)
        DVE(fv0)

    def phase_attn_b_proj():
        NCH = EXT // FTC
        NTC = FTC // QC
        qsc4 = qsc

        with nc.psum_tensor([P, QC], F32) as pq0, \
             nc.psum_tensor([P, QC], F32) as pq1, \
             nc.psum_tensor([1, QC], F32) as pn0, \
             nc.psum_tensor([1, QC], F32) as pn1, \
             nc.psum_tensor([P, QC], F32) as pR0, \
             nc.psum_tensor([P, QC], F32) as pR1, \
             nc.psum_tensor([P, QC], F32) as psrot, \
             nc.psum_tensor([P, P], BF16) as pst:
            psq = [pq0, pq1]
            psn = [pn0, pn1]
            psR = [pR0, pR1]
            st = {"i": 0, "dve_done": {}, "pe_done": {}, "rot_done": 0,
                  "pst_done": dve.n, "ln_dve": 0}

            def emit_proj(wslot, wsem, wmark, tc_col):
                i = st["i"]
                sl = i % 2

                def fp(t, i=i, sl=sl, wslot=wslot, tc_col=tc_col, wmark=wmark,
                       dm=max(st["dve_done"].get(i - 2, dve.n), st["ln_dve"])):
                    t.wait_ge(wsem.h, wmark)
                    if dm:
                        t.wait_ge(dve.h, dm)
                    for db in range(DB):
                        pe.inc(t.matmul(
                            psq[sl][:, 0:QC],
                            wbuf[:, wslot + db * P:wslot + (db + 1) * P],
                            lnxt[:, db * FTC + tc_col:db * FTC + tc_col + QC],
                            start=(db == 0), stop=(db == DB - 1)))
                    st["pe_done"][i] = pe.n
                PE(fp)

            def emit_tail_qk(gain_off, tc_abs, dst, dcol):
                i = st["i"]
                sl = i % 2

                def t1(v, i=i, sl=sl):
                    v.wait_ge(pe.h, st["pe_done"][i])
                    dve.inc(v.tensor_copy(qsc4[:, sl * QC:sl * QC + QC],
                                          psq[sl][:, 0:QC]))
                    v.wait_ge(dve.h, dve.n)
                    dve.inc(v.tensor_tensor(sqb[:, sl * QC:sl * QC + QC],
                                            qsc4[:, sl * QC:sl * QC + QC],
                                            qsc4[:, sl * QC:sl * QC + QC],
                                            ALU.mult))
                DVE(t1)
                a1 = dve.n

                def t2(t, i=i, sl=sl):
                    t.wait_ge(dve.h, a1)
                    pe.inc(t.matmul(psn[sl][0:1, 0:QC], ones128_bf,
                                    sqb[:, sl * QC:sl * QC + QC],
                                    start=True, stop=True))
                PE(t2)
                p2 = pe.n

                rT = R_T0 + sl * 512
                rB = sl * 512

                def t45(s, i=i, sl=sl, rT=rT, rB=rB):
                    s.wait_ge(pe.h, p2)
                    s.wait_ge(act.h, act.n)
                    act.inc(s.activation(rows[0:1, rT:rT + QC],
                                         psn[sl][0:1, 0:QC], AF.Ln,
                                         bias=small[0:1, C_EPSRMS:C_EPSRMS + 1]))
                    s.wait_ge(act.h, act.n)
                    act.inc(s.activation(rowsb[0:1, rB:rB + QC],
                                         rows[0:1, rT:rT + QC], AF.Exp,
                                         scale=-0.5))
                ACT(t45)
                a5 = act.n

                def t6(t, i=i, sl=sl, rB=rB):
                    t.wait_ge(act.h, a5)
                    pe.inc(t.matmul(psR[sl][:, 0:QC], ones_row_bf,
                                    rowsb[0:1, rB:rB + QC], start=True, stop=True))
                PE(t6)
                p6 = pe.n

                def t78(v, i=i, sl=sl, gain_off=gain_off):
                    v.wait_ge(pe.h, p6)
                    dve.inc(v.tensor_scalar(qsc4[:, 2 * QC:3 * QC],
                                            qsc4[:, sl * QC:sl * QC + QC],
                                            small[:, gain_off:gain_off + 1], None,
                                            ALU.mult))
                    v.wait_ge(dve.h, dve.n)
                    dve.inc(v.tensor_tensor(ptil[:, sl * QC:sl * QC + QC],
                                            qsc4[:, 2 * QC:3 * QC],
                                            psR[sl][:, 0:QC], ALU.mult))
                DVE(t78)
                d8 = dve.n

                def t9(t, i=i, sl=sl, rd=st["rot_done"]):
                    t.wait_ge(dve.h, d8)
                    if rd:
                        t.wait_ge(dve.h, rd)
                    pe.inc(t.matmul(psrot[:, 0:QC], rtc[:, 0:P],
                                    ptil[:, sl * QC:sl * QC + QC],
                                    start=True, stop=True))
                PE(t9)
                p9 = pe.n

                def t10(v, i=i, sl=sl, tc_abs=tc_abs, dst=dst, dcol=dcol):
                    v.wait_ge(pe.h, p9)
                    dve.inc(v.tensor_tensor(qsc4[:, sl * QC:sl * QC + QC],
                                            ptil[:, sl * QC:sl * QC + QC],
                                            ropet[:, tc_abs:tc_abs + QC], ALU.mult))
                    v.wait_ge(dve.h, dve.n)
                    dve.inc(v.tensor_tensor(qsc4[:, 2 * QC:3 * QC],
                                            psrot[:, 0:QC],
                                            ropet[:, EXT + tc_abs:EXT + tc_abs + QC],
                                            ALU.mult))
                    st["rot_done"] = dve.n
                    v.wait_ge(dve.h, dve.n)
                    dve.inc(v.tensor_tensor(dst[:, dcol:dcol + QC],
                                            qsc4[:, sl * QC:sl * QC + QC],
                                            qsc4[:, 2 * QC:3 * QC],
                                            ALU.add))
                    st["dve_done"][i] = dve.n
                DVE(t10)

            def emit_tail_v(g, tc_abs):
                i = st["i"]
                sl = i % 2

                def v1(v, i=i, sl=sl, pd=st["pst_done"]):
                    v.wait_ge(pe.h, st["pe_done"][i])
                    dve.inc(v.tensor_copy(ptil[:, sl * QC:sl * QC + QC],
                                          psq[sl][:, 0:QC]))
                DVE(v1)
                dv1 = dve.n
                nb = QC // P
                pe_tr = []

                def v2(t, i=i, sl=sl, g=g, tc_abs=tc_abs, dv1=dv1,
                       pd=st["pst_done"]):
                    for j in range(nb):
                        t.wait_ge(dve.h, dv1 + j)
                        if j == 0 and pd:
                            t.wait_ge(dve.h, max(dv1, pd))
                        pe.inc(t.transpose(pst[:],
                                           ptil[:, sl * QC + j * P:sl * QC + (j + 1) * P],
                                           ident[:]))
                        pe_tr.append(pe.n)
                PE(v2)

                def v3(v, i=i, g=g, tc_abs=tc_abs):
                    for j in range(nb):
                        v.wait_ge(pe.h, pe_tr[j])
                        o = VB_OFF + g * EXT + tc_abs + j * P
                        dve.inc(v.tensor_copy(kvb[:, o:o + P], pst[:]))
                    st["pst_done"] = dve.n
                    st["dve_done"][i] = dve.n
                DVE(v3)

            wload = {"n": 0}

            def load_w(wT, blk):
                li = wload["n"]
                wload["n"] += 1
                wslot = (li % 2) * DB * P
                wsem = s_w[li % 2]

                def fg(g_, wT=wT, blk=blk, wslot=wslot,
                       pm=st["pe_done"].get(st["i"] - 2, pe.n)):
                    if pm:
                        g_.wait_ge(pe.h, pm)
                    wsem.inc(g_.dma_start(
                        out=wbuf[:, wslot:wslot + DB * P].rearrange("p (db c) -> p db c", db=DB),
                        in_=wT.ap().rearrange("(db p) c -> p db c", p=P)
                            [:, :, blk * P:(blk + 1) * P]), 16)
                POOL(fg)
                return wslot, wsem, wsem.n

            for ch in range(NCH):
                base = ch * FTC
                for tc in range(NTC):
                    col0 = base + tc * QC
                    emit_ln_stats(res.ap(), col0, pR0, pR1, pn0, pn1, C_EPSLN)
                    emit_ln_apply(res.ap(), col0, pR0, pR1, lnxt, FTC, tc * QC)
                    st["ln_dve"] = dve.n

                for g in range(KVB):
                    wslot, wsem, wmark = load_w(wkT, g)
                    for tc in range(NTC):
                        emit_proj(wslot, wsem, wmark, tc * QC)
                        emit_tail_qk(KNW, base + tc * QC, kvb, g * EXT + base + tc * QC)
                        st["i"] += 1
                    wslot, wsem, wmark = load_w(wvT, g)
                    for tc in range(NTC):
                        emit_proj(wslot, wsem, wmark, tc * QC)
                        emit_tail_v(g, base + tc * QC)
                        st["i"] += 1
                for h in range(H):
                    cols = [tc for tc in range(NTC) if base + tc * QC >= HALO]
                    if not cols:
                        continue
                    wslot, wsem, wmark = load_w(wqT, h)
                    for tc in cols:
                        emit_proj(wslot, wsem, wmark, tc * QC)
                        emit_tail_qk(QNW, base + tc * QC, bigB,
                                     h * NT + (base + tc * QC - HALO))
                        st["i"] += 1

    def phase_flash_b():
        WBK = W // P
        for h in range(H):
            g = h // RH
            for qi in range(NT // FQ2):
                q0 = HALO + qi * FQ2
                kb_lo = (q0 - W + 1) // P
                kb_hi = (q0 + FQ2 - 1) // P
                kbs = list(range(kb_lo, kb_hi + 1))
                nk = len(kbs)
                qoff = h * NT + qi * FQ2

                geo = []
                for kb in kbs:
                    qlo = max(q0, P * kb)
                    qhi = min(q0 + FQ2, P * kb + W + P)
                    subs = []
                    a = qlo
                    while a < qhi:
                        bnd = q0 + ((a - q0) // 512 + 1) * 512
                        b_ = min(qhi, bnd)
                        subs.append((a - q0, b_ - a))
                        a = b_
                    diag = (P * kb >= q0)
                    far = (qhi == P * kb + W + P)
                    halo_k = (kb * P < HALO)
                    geo.append((kb, qlo, qhi, subs, diag, far, halo_k))

                with nc.psum_tensor([P, FQ2], F32) as psO, \
                     nc.psum_tensor([1, FQ2], F32) as psD:
                  with nc.psum_tensor([P, FQ2], F32) as psS0, \
                       nc.psum_tensor([P, FQ2], F32) as psS1:
                    psS = [psS0, psS1]
                    pe0, act0, dve0 = pe.n, act.n, dve.n
                    pe_after_scores = []
                    dve_after_masks = []
                    pe_c = pe0
                    dve_c = dve0
                    for ki, (kb, qlo, qhi, subs, diag, far, halo_k) in enumerate(geo):
                        pe_c += len(subs)
                        pe_after_scores.append(pe_c)
                        dve_c += (1 if diag else 0) + (1 if far else 0)
                        dve_after_masks.append(dve_c)
                        if ki >= 1:
                            pe_c += 2 * len(geo[ki - 1][3])
                    pe_c += 2 * len(geo[-1][3])

                    started_O = set()
                    started_D = set()

                    def fp(t, h=h, g=g, q0=q0):
                        if dve0:
                            t.wait_ge(dve.h, dve0)
                        for ki, (kb, qlo, qhi, subs, diag, far, halo_k) in enumerate(geo):
                            if ki >= 2:
                                t.wait_ge(act.h, act0 + ki - 1)
                            for off, n in subs:
                                pe.inc(t.matmul(
                                    psS[ki % 2][:, off:off + n],
                                    kvb[:, g * EXT + kb * P:g * EXT + (kb + 1) * P],
                                    bigB[:, qoff + off:qoff + off + n],
                                    start=True, stop=True))
                            if ki >= 1:
                                t.wait_ge(act.h, act0 + ki)
                                kbp, qlop, qhip, subsp, _, _, _ = geo[ki - 1]
                                pl = ((ki - 1) % 2) * FQ2
                                for off, n in subsp:
                                    st = (off // 512) not in started_O
                                    started_O.add(off // 512)
                                    pe.inc(t.matmul(
                                        psO[:, off:off + n],
                                        kvb[:, VB_OFF + g * EXT + kbp * P:
                                             VB_OFF + g * EXT + (kbp + 1) * P],
                                        ptil[:, pl + off:pl + off + n],
                                        start=st, stop=False))
                                for off, n in subsp:
                                    st = (off // 512) not in started_D
                                    started_D.add(off // 512)
                                    pe.inc(t.matmul(
                                        psD[0:1, off:off + n], ones_bf,
                                        ptil[:, pl + off:pl + off + n],
                                        start=st, stop=False))
                        t.wait_ge(act.h, act0 + len(geo))
                        kbp, qlop, qhip, subsp, _, _, _ = geo[-1]
                        pl = ((len(geo) - 1) % 2) * FQ2
                        for off, n in subsp:
                            st = (off // 512) not in started_O
                            started_O.add(off // 512)
                            pe.inc(t.matmul(
                                psO[:, off:off + n],
                                kvb[:, VB_OFF + g * EXT + kbp * P:
                                     VB_OFF + g * EXT + (kbp + 1) * P],
                                ptil[:, pl + off:pl + off + n],
                                start=st, stop=True))
                        for off, n in subsp:
                            st = (off // 512) not in started_D
                            started_D.add(off // 512)
                            pe.inc(t.matmul(
                                psD[0:1, off:off + n], ones_bf,
                                ptil[:, pl + off:pl + off + n],
                                start=st, stop=True))
                    PE(fp)

                    def fm(v, q0=q0):
                        for ki, (kb, qlo, qhi, subs, diag, far, halo_k) in enumerate(geo):
                            if not (diag or far):
                                continue
                            v.wait_ge(pe.h, pe_after_scores[ki])
                            sl = psS[ki % 2]
                            if diag:
                                o = P * kb - q0
                                dve.inc(v.tensor_tensor(
                                    sl[:, o:o + P], sl[:, o:o + P],
                                    mask2[:, P:2 * P], ALU.add))
                            if far:
                                o = qhi - P - q0
                                dve.inc(v.tensor_tensor(
                                    sl[:, o:o + P], sl[:, o:o + P],
                                    mask2[:, 0:P], ALU.add))
                    DVE(fm)

                    def fa(s, q0=q0):
                        act_set_pin(s)
                        for ki, (kb, qlo, qhi, subs, diag, far, halo_k) in enumerate(geo):
                            s.wait_ge(pe.h, pe_after_scores[ki])
                            s.wait_ge(dve.h, dve_after_masks[ki])
                            bias = small[:, HALOB:HALOB + 1] if halo_k \
                                else small[:, C_NSHIFT:C_NSHIFT + 1]
                            pl = (ki % 2) * FQ2
                            o = qlo - q0
                            act.inc(s.activation(
                                ptil[:, pl + o:pl + (qhi - q0)],
                                psS[ki % 2][:, o:qhi - q0], AF.Exp, bias=bias))
                    ACT(fa)
                    act_all = act.n

                  with nc.psum_tensor([P, FQ2], F32) as psR:
                    def fv(s):
                        s.wait_ge(pe.h, pe.n)
                        act.inc(s.activation(rows[0:1, R_DEN:R_DEN + FQ2],
                                             psD[0:1, 0:FQ2], AF.Ln))
                        s.wait_ge(act.h, act.n)
                        act.inc(s.activation(rowsb[0:1, RB_DEN:RB_DEN + FQ2],
                                             rows[0:1, R_DEN:R_DEN + FQ2],
                                             AF.Exp, scale=-1.0))
                    ACT(fv)

                    def fp2(t):
                        t.wait_ge(act.h, act.n)
                        for ss in range(FQ2 // QC):
                            pe.inc(t.matmul(
                                psR[:, ss * QC:(ss + 1) * QC], ones_row_bf,
                                rowsb[0:1, RB_DEN + ss * QC:RB_DEN + (ss + 1) * QC],
                                start=True, stop=True))
                    PE(fp2)

                    def fv2(v, qoff=qoff):
                        v.wait_ge(pe.h, pe.n)
                        dve.inc(v.tensor_copy(rtil[:, 0:FQ2], psR[:, 0:FQ2]))
                        v.wait_ge(dve.h, dve.n)
                        dve.inc(v.tensor_tensor(bigB[:, qoff:qoff + FQ2],
                                                psO[:, 0:FQ2], rtil[:, 0:FQ2],
                                                ALU.mult))
                    DVE(fv2)

    phase_setup()
    join_all()
    phase_kv()
    join_all()
    for slab in range(NSLAB):
        phase_attn_a_slab(slab)
    phase_ffn_gate_setup()
    join_light()
    phase_ffn(res.ap(), res.ap(), 0, EXT, 0)
    join_light()
    phase_ropetab()
    join_light()
    phase_attn_b_proj()
    join_light()
    phase_flash_b()
    join_light()
    emit_oproj(woT.ap(), kvb, 0, bigB, NT, 0, HALO, NT, None,
               res.ap(), res.ap(), 0)
    join_light()
    phase_ffn(res.ap(), outT.ap(), HALO, NT, HALO)

    def ffinal(s):
        s.wait_ge(s_resw[0].h, s_resw[0].n)
        s.wait_ge(s_resw[1].h, s_resw[1].n)
    SP(ffinal)

    ctx.close()
    return nc



def _rope_tables(cfg, ext0):
    pos = ext0 + np.arange(cfg.EXT, dtype=np.float64)
    inv = 1.0 / (10000.0 ** (np.arange(0, P, 2, dtype=np.float64) / P))
    ang = inv[:, None] * pos[None, :]
    cos = np.cos(ang).astype(np.float32)
    sin = np.sin(ang).astype(np.float32)
    cos2 = np.concatenate([cos, cos], axis=0)
    sin2 = np.concatenate([sin, sin], axis=0)
    return np.concatenate([cos2, sin2], axis=1)


def make_core_inputs(cfg, inp, core):
    bsz = 4
    bidx, rank = divmod(core, bsz)
    t0 = rank * cfg.NT
    x = inp["L0_attn_a_x"]
    xT = np.ascontiguousarray(np.transpose(x[bidx]))
    ext0 = t0 - cfg.HALO
    if ext0 < 0:
        pad = np.zeros((cfg.D, -ext0), np.float32)
        xTe = np.concatenate([pad, xT[:, 0:t0 + cfg.NT]], axis=1)
    else:
        xTe = xT[:, ext0:t0 + cfg.NT]
    halo_valid = 0.0 if ext0 >= 0 else -30000.0
    d = {
        "xTb": xT,
        "xTe": np.ascontiguousarray(xTe),
        "Wq": inp["L0_attn_a_Wq"],
        "Wk": inp["L0_attn_a_Wk"],
        "Wv": inp["L0_attn_a_Wv"],
        "Wo": inp["L0_attn_a_Wo"],
        "bqT": np.ascontiguousarray(inp["L0_attn_a_bq"].reshape(cfg.H, P).T),
        "bk": inp["L0_attn_a_bk"].reshape(P, 1),
        "bv": inp["L0_attn_a_bv"].reshape(P, 1),
        "boT": np.ascontiguousarray(inp["L0_attn_a_bo"].reshape(cfg.DB, P).T),
        "WgT": np.ascontiguousarray(inp["L0_ffn_Wg"].T),
        "W0aT": np.ascontiguousarray(inp["L0_ffn_W0a"].T),
        "W0bT": np.ascontiguousarray(inp["L0_ffn_W0b"].T),
        "W1aT": np.ascontiguousarray(inp["L0_ffn_W1a"].T),
        "W1bT": np.ascontiguousarray(inp["L0_ffn_W1b"].T),
        "b0aT": np.ascontiguousarray(inp["L0_ffn_b0a"].reshape(cfg.FB, P).T),
        "b0bT": np.ascontiguousarray(inp["L0_ffn_b0b"].reshape(cfg.DB, P).T),
        "b1aT": np.ascontiguousarray(inp["L0_ffn_b1a"].reshape(cfg.FB, P).T),
        "b1bT": np.ascontiguousarray(inp["L0_ffn_b1b"].reshape(cfg.DB, P).T),
        "wqT": np.ascontiguousarray(inp["L0_attn_b_wq"].T),
        "wkT": np.ascontiguousarray(inp["L0_attn_b_wk"].T),
        "wvT": np.ascontiguousarray(inp["L0_attn_b_wv"].T),
        "woT": np.ascontiguousarray(inp["L0_attn_b_wo"].T),
        "qnw": inp["L0_attn_b_q_norm_w"].reshape(P, 1),
        "knw": inp["L0_attn_b_k_norm_w"].reshape(P, 1),
        "pos0": np.array([[float(ext0)]], np.float32),
        "ropeT": _rope_tables(cfg, ext0),
        "halob": np.full((P, 1), halo_valid, np.float32),
    }
    return {k: np.ascontiguousarray(np.asarray(v, np.float32)) for k, v in d.items()}


def kernel(**inputs):
    cfg = Cfg()
    nc = build_program(cfg)
    in_maps = [make_core_inputs(cfg, inputs, c) for c in range(8)]
    r = run_bass_kernel_spmd(nc, in_maps, core_ids=list(range(8)))
    outs = []
    for b in range(2):
        cores = [r.results[b * 4 + i]["outT"].T for i in range(4)]
        outs.append(np.concatenate(cores, axis=0))
    return np.stack(outs, axis=0).astype(np.float32)


# version 24 (final)
# speedup vs baseline: 1.1245x; 1.0274x over previous
import math
from contextlib import ExitStack

import numpy as np

import concourse.bass as bass
import concourse.mybir as mybir
from concourse.bass_utils import run_bass_kernel_spmd

F32 = mybir.dt.float32
BF16 = mybir.dt.bfloat16
F32R = mybir.dt.float32r
AF = mybir.ActivationFunctionType
ALU = mybir.AluOpType
AX = mybir.AxisListType

P = 128
CSHIFT = 12.0


class Cfg:
    def __init__(self, D=2048, S=8192, NT=2048, HALO=1024, H=16, KVB=4, F=4096,
                 QC=512, ACH=1024, FTC=1024):
        self.D, self.S, self.NT, self.HALO = D, S, NT, HALO
        self.H, self.KVB, self.F = H, KVB, F
        self.QC, self.ACH, self.FTC = QC, ACH, FTC
        self.DB = D // P
        self.EXT = NT + HALO
        self.FB = F // P
        self.W = HALO
        self.RH = H // KVB
        assert self.EXT % ACH == 0 and ACH % QC == 0 and S % QC == 0
        assert self.EXT % FTC == 0 and FTC % QC == 0 and NT % FTC == 0
        assert HALO % P == 0 and NT % P == 0


class Sem:
    __slots__ = ("h", "n")

    def __init__(self, nc, name):
        self.h = nc.alloc_semaphore(name=name)
        self.n = 0

    def inc(self, instr, k=1):
        instr.then_inc(self.h, k)
        self.n += k
        return self.n


def build_program(cfg: Cfg):
    nc = bass.Bass("TRN2", target_bir_lowering=False, debug=False)
    ctx = ExitStack()
    D, DB, S, NT, HALO, EXT = cfg.D, cfg.DB, cfg.S, cfg.NT, cfg.HALO, cfg.EXT
    H, KVB, F, FB, RH = cfg.H, cfg.KVB, cfg.F, cfg.FB, cfg.RH
    QC, ACH, FTC, W = cfg.QC, cfg.ACH, cfg.FTC, cfg.W
    SC = S // QC
    NSLAB = EXT // ACH
    KB = S // P
    WB = W // P

    dram = {}

    def din(name, shape, dtype=F32):
        dram[name] = nc.dram_tensor(name, list(shape), dtype, kind="ExternalInput")
        return dram[name]

    xTb = din("xTb", (D, S))
    xTe = din("xTe", (D, EXT))
    Wq = din("Wq", (D, H * P))
    Wk = din("Wk", (D, P))
    Wv = din("Wv", (D, P))
    Wo = din("Wo", (H * P, D))
    bqT = din("bqT", (P, H))
    bk_ = din("bk", (P, 1))
    bv_ = din("bv", (P, 1))
    boT = din("boT", (P, DB))
    WgT = din("WgT", (D, 2))
    W0aT = din("W0aT", (D, F))
    W0bT = din("W0bT", (F, D))
    W1aT = din("W1aT", (D, F))
    W1bT = din("W1bT", (F, D))
    b0aT = din("b0aT", (P, FB))
    b0bT = din("b0bT", (P, DB))
    b1aT = din("b1aT", (P, FB))
    b1bT = din("b1bT", (P, DB))
    wqT = din("wqT", (D, H * P))
    wkT = din("wkT", (D, KVB * P))
    wvT = din("wvT", (D, KVB * P))
    woT = din("woT", (H * P, D))
    qnw_ = din("qnw", (P, 1))
    knw_ = din("knw", (P, 1))
    pos0_ = din("pos0", (1, 1))
    ropeT_ = din("ropeT", (P, 2 * (NT + HALO)))
    halob_ = din("halob", (P, 1))

    res = nc.dram_tensor("res", [D, EXT], F32)
    outT = nc.dram_tensor("outT", [D, NT], F32, kind="ExternalOutput")

    sbufs = {}

    def sb(name, shape, dtype):
        sbufs[name] = ctx.enter_context(nc.sbuf_tensor(name, list(shape), dtype))
        return sbufs[name]

    lnxt = sb("lnxt", (P, DB * max(ACH, FTC)), BF16)
    WQCH = max(1, DB // 4) * P
    bigB = sb("bigB", (P, max(H * ACH + 2 * H * WQCH, FB * FTC,
                              H * NT, 2 * H * WQCH)), BF16)
    kvb = sb("kvb", (P, max(2 * S, 2 * KVB * EXT, DB * FTC + FB * 256)), BF16)
    wbuf = sb("wbuf", (P, 2 * DB * P), BF16)
    xt2 = sb("xt2", (P, 2 * QC), F32)
    xb2 = sb("xb2", (P, 2 * QC), BF16)
    sqb = sb("sqb", (P, 2 * QC), BF16)
    sq2 = sb("sq2", (P, 2 * QC), F32)
    rest = sb("rest", (P, 2 * QC), F32)
    ptil = sb("ptil", (P, max(4 * QC, 2 * ACH, 2 * min(1024, NT))), BF16)
    rtil = sb("rtil", (P, max(2 * QC, ACH, min(1024, NT))), BF16)
    mtil = sb("mtil", (P, FTC), BF16)
    qsc = sb("qsc", (P, 3 * QC), F32)
    ropet = sb("ropet", (P, 2 * EXT), BF16)
    rtc = sb("rtc", (P, P), BF16)
    rows = sb("rows", (1, 2112), F32)
    rowsb = sb("rowsb", (1, 1024), BF16)
    small = sb("small", (P, 288), F32)
    smb = sb("smb", (P, 160), BF16)
    mask2 = sb("mask2", (P, 2 * P), BF16)
    ident = sb("ident", (P, P), BF16)

    ONES = 0
    BQ = 1
    BK = BQ + H
    BV = BK + 1
    BO = BV + 1
    B0A = BO + DB
    B0B = B0A + FB
    B1A = B0B + DB
    B1B = B1A + FB
    QNW = B1B + DB
    KNW = QNW + 1
    HALOB = KNW + 1
    WBARK = HALOB + 1
    WBARV = WBARK + 1
    WGD = WBARV + 1
    ONE_ROW_P0 = WGD + DB
    C_NSHIFT = ONE_ROW_P0 + P
    C_EPSLN = C_NSHIFT + 1
    C_EPSRMS = C_EPSLN + 1
    C_EPSRMS128 = C_EPSRMS + 1
    C_LNHALF128 = C_EPSRMS128 + 1
    HALOR = C_LNHALF128 + 1
    assert HALOR + 1 <= 288

    R_MEAN, R_RSTD, R_T0, R_T1 = 0, 512, 1024, 1536
    R_DEN = 1024
    R_MISC = 2048
    RB_RSTD, RB_MB, RB_DEN, RB_MASK = 0, 512, 0, 0


    pe = Sem(nc, "pe")
    act = Sem(nc, "act")
    dve = Sem(nc, "dve")
    pool = Sem(nc, "pool")
    s_bias = Sem(nc, "s_bias")
    s_wkv = Sem(nc, "s_wkv")
    s_x = [Sem(nc, "s_x0"), Sem(nc, "s_x1")]
    s_xb = [Sem(nc, "s_xb0"), Sem(nc, "s_xb1")]
    s_resw = [Sem(nc, "s_resw0"), Sem(nc, "s_resw1")]
    s_w = [Sem(nc, "s_w0"), Sem(nc, "s_w1")]
    s_wo = [Sem(nc, "s_wo0"), Sem(nc, "s_wo1")]
    s_dbg = Sem(nc, "s_dbg")
    all_dma_sems = [s_bias, s_wkv, s_x[0], s_x[1], s_xb[0], s_xb[1],
                    s_resw[0], s_resw[1], s_w[0], s_w[1], s_wo[0], s_wo[1],
                    s_dbg]

    block = ctx.enter_context(nc.Block())

    ones_col = small[:, ONES:ONES + 1]
    ones_row = small[0:1, ONE_ROW_P0:ONE_ROW_P0 + P]
    ones_bf = smb[:, 0:1]
    ones128_bf = smb[:, 1:2]
    ones_row_bf = smb[0:1, 8:8 + P]
    WGDB = 140
    scale_a = 1.0 / math.sqrt(P)


    def PE(f): block.tensor(f)
    def ACT(f): block.scalar(f)
    def DVE(f): block.vector(f)
    def POOL(f): block.gpsimd(f)
    def SP(f): block.sync(f)

    _pin_rot = [0]

    def act_set_pin(s):
        c = R_MISC + 8 + (_pin_rot[0] % 8)
        _pin_rot[0] += 1
        s.activation(rows[0:1, c:c + 1],
                     small[0:1, C_EPSRMS:C_EPSRMS + 1], AF.Ln)

    def join_all():
        marks = [(s, s.n) for s in ([pe, act, dve, pool] + all_dma_sems)]

        def mk():
            def f(e):
                for s, n in marks:
                    if n:
                        e.wait_ge(s.h, n)
            return f
        PE(mk()); ACT(mk()); DVE(mk()); POOL(mk()); SP(mk())

    def join_light():
        marks = [(s, s.n) for s in (pe, act, dve, pool, s_resw[0], s_resw[1],
                                    s_x[0], s_x[1], s_xb[0], s_xb[1])]

        def mk():
            def f(e):
                for s, n in marks:
                    if n:
                        e.wait_ge(s.h, n)
            return f
        PE(mk()); ACT(mk()); DVE(mk()); POOL(mk()); SP(mk())

    def phase_setup():
        def f(g):
            pool.inc(g.memset(small[:, ONES:ONES + 1], 1.0))
            pool.inc(g.memset(small[0:1, ONE_ROW_P0:ONE_ROW_P0 + P], 1.0))
            pool.inc(g.memset(smb[:, 0:1], 1.0))
            pool.inc(g.memset(smb[:, 1:2], 1.0 / 128.0))
            pool.inc(g.memset(smb[0:1, 8:8 + P], 1.0))
            pool.inc(g.memset(small[:, C_NSHIFT:C_NSHIFT + 1], -CSHIFT))
            pool.inc(g.memset(small[:, C_EPSLN:C_EPSLN + 1], 1e-5))
            pool.inc(g.memset(small[:, C_EPSRMS:C_EPSRMS + 1], 1e-6))
            pool.inc(g.memset(small[:, C_EPSRMS128:C_EPSRMS128 + 1], 128e-6))
            pool.inc(g.memset(small[:, C_LNHALF128:C_LNHALF128 + 1],
                              0.5 * math.log(128.0)))
            pool.inc(g.memset(ident[:], 0.0))
            pool.inc(g.memset(mask2[:, 0:P], 0.0))
            pool.inc(g.memset(mask2[:, P:2 * P], 0.0))
            g.wait_ge(pool.h, pool.n)
            pool.inc(g.affine_select(out=ident[:], in_=ident[:],
                                     compare_op=ALU.not_equal, fill=1.0,
                                     base=0, pattern=[[-1, P]], channel_multiplier=1))
            pool.inc(g.affine_select(out=mask2[:, 0:P], in_=mask2[:, 0:P],
                                     compare_op=ALU.is_ge, fill=-30000.0,
                                     base=-1, pattern=[[-1, P]], channel_multiplier=1))
            pool.inc(g.affine_select(out=mask2[:, P:2 * P], in_=mask2[:, P:2 * P],
                                     compare_op=ALU.is_ge, fill=-30000.0,
                                     base=0, pattern=[[1, P]], channel_multiplier=-1))
        POOL(f)

        def fs(s):
            s_bias.inc(s.dma_start(out=small[:, BQ:BQ + H], in_=bqT.ap()), 16)
            s_bias.inc(s.dma_start(out=small[:, BK:BK + 1], in_=bk_.ap()), 16)
            s_bias.inc(s.dma_start(out=small[:, BV:BV + 1], in_=bv_.ap()), 16)
            s_bias.inc(s.dma_start(out=small[:, BO:BO + DB], in_=boT.ap()), 16)
            s_bias.inc(s.dma_start(out=small[:, B0A:B0A + FB], in_=b0aT.ap()), 16)
            s_bias.inc(s.dma_start(out=small[:, B0B:B0B + DB], in_=b0bT.ap()), 16)
            s_bias.inc(s.dma_start(out=small[:, B1A:B1A + FB], in_=b1aT.ap()), 16)
            s_bias.inc(s.dma_start(out=small[:, B1B:B1B + DB], in_=b1bT.ap()), 16)
            s_bias.inc(s.dma_start(out=small[:, QNW:QNW + 1], in_=qnw_.ap()), 16)
            s_bias.inc(s.dma_start(out=small[:, KNW:KNW + 1], in_=knw_.ap()), 16)
            s_bias.inc(s.dma_start(out=small[:, HALOB:HALOB + 1], in_=halob_.ap()), 16)
            s_bias.inc(s.dma_start(out=rows[0:1, R_MISC:R_MISC + 1], in_=pos0_.ap()), 16)
        SP(fs)

        def fv(v):
            v.wait_ge(s_bias.h, s_bias.n)
            dve.inc(v.tensor_scalar_mul(small[:, BQ:BQ + H], small[:, BQ:BQ + H], scale_a))
            v.wait_ge(dve.h, dve.n)
            dve.inc(v.tensor_scalar_mul(small[:, QNW:QNW + 1], small[:, QNW:QNW + 1], scale_a))
            v.wait_ge(dve.h, dve.n)
            dve.inc(v.tensor_copy(small[:, HALOR:HALOR + 1], small[:, HALOB:HALOB + 1]))
            v.wait_ge(dve.h, dve.n)
            dve.inc(v.tensor_scalar_add(small[:, HALOB:HALOB + 1], small[:, HALOB:HALOB + 1], -CSHIFT))
        DVE(fv)

    def emit_ln_stats(src, col0, psA, psB, ps_sx, ps_sx2, eps_col, gate=False, ps_g=None,
                      kv=None):
        act0, pe0, dve0 = act.n, pe.n, dve.n
        per_db = (3 if gate else 2) + (2 if kv else 0)
        xw = []

        def fs(g):
            for db in range(DB):
                if db >= 2:
                    g.wait_ge(dve.h, dve0 + db - 1)
                    g.wait_ge(pe.h, pe0 + per_db * (db - 1))
                else:
                    if act0:
                        g.wait_ge(act.h, act0)
                    if pe0:
                        g.wait_ge(pe.h, pe0)
                    if dve0:
                        g.wait_ge(dve.h, dve0)
                xw.append(s_xb[db % 2].inc(g.dma_start(
                    out=xb2[:, (db % 2) * QC:(db % 2) * QC + QC],
                    in_=src[db * P:(db + 1) * P, col0:col0 + QC]), 16))
        POOL(fs)

        def fa(v):
            for db in range(DB):
                v.wait_ge(s_xb[db % 2].h, xw[db])
                if db >= 2:
                    v.wait_ge(pe.h, pe0 + per_db * (db - 1))
                dve.inc(v.tensor_tensor(sqb[:, (db % 2) * QC:(db % 2) * QC + QC],
                                        xb2[:, (db % 2) * QC:(db % 2) * QC + QC],
                                        xb2[:, (db % 2) * QC:(db % 2) * QC + QC],
                                        ALU.mult))
        DVE(fa)

        def fp(t):
            for db in range(DB):
                sl = (db % 2) * QC
                t.wait_ge(s_xb[db % 2].h, xw[db])
                if kv and db == 0:
                    t.wait_ge(s_wkv.h, kv[2])
                pe.inc(t.matmul(ps_sx[0:1, 0:QC], ones_bf, xb2[:, sl:sl + QC],
                                start=(db == 0), stop=(db == DB - 1)))
                if gate:
                    pe.inc(t.matmul(ps_g[0:1, 0:QC], smb[:, WGDB + db:WGDB + db + 1],
                                    xb2[:, sl:sl + QC],
                                    start=(db == 0), stop=(db == DB - 1)))
                if kv:
                    pe.inc(t.matmul(kv[0][:, 0:QC], wbuf[:, db * P:(db + 1) * P],
                                    xb2[:, sl:sl + QC],
                                    start=(db == 0), stop=(db == DB - 1)))
                    pe.inc(t.matmul(kv[1][:, 0:QC],
                                    wbuf[:, (DB + db) * P:(DB + db + 1) * P],
                                    xb2[:, sl:sl + QC],
                                    start=(db == 0), stop=(db == DB - 1)))
                t.wait_ge(dve.h, dve0 + db + 1)
                pe.inc(t.matmul(ps_sx2[0:1, 0:QC], ones_bf, sqb[:, sl:sl + QC],
                                start=(db == 0), stop=(db == DB - 1)))
        PE(fp)

        def fv(v):
            v.wait_ge(pe.h, pe.n)
            dve.inc(v.tensor_scalar_mul(rows[0:1, R_MEAN:R_MEAN + QC],
                                        ps_sx[0:1, 0:QC], 1.0 / D))
            if gate:
                v.wait_ge(dve.h, dve.n)
                dve.inc(v.tensor_copy(rows[0:1, R_T1:R_T1 + QC], ps_g[0:1, 0:QC]))
            v.wait_ge(dve.h, dve.n)
            dve.inc(v.tensor_scalar_mul(rows[0:1, R_T0:R_T0 + QC],
                                        ps_sx2[0:1, 0:QC], 1.0 / D))
            v.wait_ge(dve.h, dve.n)
            dve.inc(v.tensor_tensor(rows[0:1, R_RSTD:R_RSTD + QC],
                                    rows[0:1, R_MEAN:R_MEAN + QC],
                                    rows[0:1, R_MEAN:R_MEAN + QC], ALU.mult))
            v.wait_ge(dve.h, dve.n)
            dve.inc(v.scalar_tensor_tensor(rows[0:1, R_T0:R_T0 + QC],
                                           rows[0:1, R_T0:R_T0 + QC],
                                           small[0:1, eps_col:eps_col + 1],
                                           rows[0:1, R_RSTD:R_RSTD + QC],
                                           ALU.add, ALU.subtract))
        DVE(fv)

        def fa2(s):
            s.wait_ge(dve.h, dve.n)
            act.inc(s.activation(rows[0:1, R_T0:R_T0 + QC],
                                 rows[0:1, R_T0:R_T0 + QC], AF.Ln))
            s.wait_ge(act.h, act.n)
            act.inc(s.activation(rows[0:1, R_RSTD:R_RSTD + QC],
                                 rows[0:1, R_T0:R_T0 + QC], AF.Exp, scale=-0.5))
            s.wait_ge(act.h, act.n)
            act.inc(s.activation(rowsb[0:1, RB_RSTD:RB_RSTD + QC],
                                 rows[0:1, R_T0:R_T0 + QC], AF.Exp, scale=-0.5))
        ACT(fa2)

        def fv2(v):
            v.wait_ge(act.h, act.n)
            dve.inc(v.tensor_tensor(rowsb[0:1, RB_MB:RB_MB + QC],
                                    rows[0:1, R_MEAN:R_MEAN + QC],
                                    rows[0:1, R_RSTD:R_RSTD + QC], ALU.mult))
        DVE(fv2)

        def fp2(t):
            t.wait_ge(dve.h, dve.n)
            pe.inc(t.matmul(psA[:, 0:QC], ones_row_bf,
                            rowsb[0:1, RB_RSTD:RB_RSTD + QC],
                            start=True, stop=True))
            pe.inc(t.matmul(psB[:, 0:QC], ones_row_bf,
                            rowsb[0:1, RB_MB:RB_MB + QC],
                            start=True, stop=True))
        PE(fp2)

    def emit_ln_apply(src, col0, psA, psB, dst, span, dcol):
        dve0, pe0a, act0a = dve.n, pe.n, act.n
        xw = []

        def fs(s):
            for db in range(DB):
                if db >= 2:
                    s.wait_ge(dve.h, dve0 + 2 * (db - 1))
                else:
                    if pe0a:
                        s.wait_ge(pe.h, pe0a)
                    if act0a:
                        s.wait_ge(act.h, act0a)
                xw.append(s_x[db % 2].inc(s.dma_start(
                    out=xt2[:, (db % 2) * QC:(db % 2) * QC + QC],
                    in_=src[db * P:(db + 1) * P, col0:col0 + QC]), 16))
        SP(fs)

        def fv(v):
            if pe0a:
                v.wait_ge(pe.h, pe0a)
            for db in range(DB):
                sl = (db % 2) * QC
                v.wait_ge(s_x[db % 2].h, xw[db])
                dve.inc(v.tensor_tensor(sq2[:, sl:sl + QC], xt2[:, sl:sl + QC],
                                        psA[:, 0:QC], ALU.mult))
                v.wait_ge(dve.h, dve.n)
                dve.inc(v.tensor_tensor(dst[:, db * span + dcol:db * span + dcol + QC],
                                        sq2[:, sl:sl + QC], psB[:, 0:QC], ALU.subtract))
        DVE(fv)

    KOFF, VOFF = 0, S

    def phase_kv():
        def fs(g):
            s_wkv.inc(g.dma_start(
                out=wbuf[:, 0:DB * P].rearrange("p (db c) -> p db c", db=DB),
                in_=Wk.ap().rearrange("(db p) c -> p db c", p=P)), 16)
            s_wkv.inc(g.dma_start(
                out=wbuf[:, DB * P:2 * DB * P].rearrange("p (db c) -> p db c", db=DB),
                in_=Wv.ap().rearrange("(db p) c -> p db c", p=P)), 16)
        POOL(fs)

        with nc.psum_tensor([P, 2], F32) as psw:
            def fp(t):
                t.wait_ge(s_wkv.h, s_wkv.n)
                for db in range(DB):
                    pe.inc(t.matmul(psw[:, 0:1], wbuf[:, db * P:(db + 1) * P],
                                    ones_bf, start=(db == 0), stop=(db == DB - 1)))
                for db in range(DB):
                    pe.inc(t.matmul(psw[:, 1:2], wbuf[:, (DB + db) * P:(DB + db + 1) * P],
                                    ones_bf, start=(db == 0), stop=(db == DB - 1)))
            PE(fp)

            def fv(v):
                v.wait_ge(pe.h, pe.n)
                dve.inc(v.tensor_copy(small[:, WBARK:WBARK + 1], psw[:, 0:1]))
                v.wait_ge(dve.h, dve.n)
                dve.inc(v.tensor_copy(small[:, WBARV:WBARV + 1], psw[:, 1:2]))
            DVE(fv)

        for c in range(SC):
            col0 = c * QC
            with nc.psum_tensor([P, QC], F32) as psA, \
                 nc.psum_tensor([P, QC], F32) as psB, \
                 nc.psum_tensor([1, QC], F32) as ps_sx, \
                 nc.psum_tensor([1, QC], F32) as ps_sx2, \
                 nc.psum_tensor([P, QC], F32) as psK, \
                 nc.psum_tensor([P, QC], F32) as psV:
                emit_ln_stats(xTb.ap(), col0, psA, psB, ps_sx, ps_sx2, C_EPSLN,
                              kv=(psK, psV, s_wkv.n))

                def fv2(v, col0=col0):
                    v.wait_ge(pe.h, pe.n)
                    dve.inc(v.tensor_copy(xt2[:, 0:QC], psA[:, 0:QC]))
                    v.wait_ge(dve.h, dve.n)
                    dve.inc(v.tensor_scalar(sq2[:, 0:QC], psB[:, 0:QC],
                                            small[:, WBARK:WBARK + 1], None, ALU.mult))
                    v.wait_ge(dve.h, dve.n)
                    dve.inc(v.tensor_scalar(sq2[:, QC:2 * QC], psB[:, 0:QC],
                                            small[:, WBARV:WBARV + 1], None, ALU.mult))
                    v.wait_ge(dve.h, dve.n)
                    dve.inc(v.tensor_tensor(qsc[:, 0:QC], psK[:, 0:QC],
                                            xt2[:, 0:QC], ALU.mult))
                    v.wait_ge(dve.h, dve.n)
                    dve.inc(v.tensor_tensor(qsc[:, QC:2 * QC], psV[:, 0:QC],
                                            xt2[:, 0:QC], ALU.mult))
                    v.wait_ge(dve.h, dve.n)
                    dve.inc(v.scalar_tensor_tensor(
                        kvb[:, KOFF + col0:KOFF + col0 + QC],
                        qsc[:, 0:QC], small[:, BK:BK + 1], sq2[:, 0:QC],
                        ALU.add, ALU.subtract))
                    v.wait_ge(dve.h, dve.n)
                    dve.inc(v.scalar_tensor_tensor(
                        ptil[:, 0:QC],
                        qsc[:, QC:2 * QC], small[:, BV:BV + 1], sq2[:, QC:2 * QC],
                        ALU.add, ALU.subtract))
                DVE(fv2)

                nblk = QC // P
                with nc.psum_tensor([P, P], BF16) as pst:
                    pe2 = pe.n
                    dve2 = dve.n

                    def ft(t, col0=col0):
                        for i in range(nblk):
                            t.wait_ge(dve.h, dve2 + i)
                            pe.inc(t.transpose(pst[:], ptil[:, i * P:(i + 1) * P], ident[:]))
                    PE(ft)

                    def fv3(v, col0=col0):
                        for i in range(nblk):
                            v.wait_ge(pe.h, pe2 + i + 1)
                            dve.inc(v.tensor_copy(
                                kvb[:, VOFF + col0 + i * P:VOFF + col0 + (i + 1) * P],
                                pst[:]))
                    DVE(fv3)

    def emit_oproj(wdram, wreg, wreg_off, av, av_span, av_off, tok0, ntok,
                   bias_off, src, dst, dst_off, wpre_mark=None):
        DQ = max(1, DB // 4)
        nq = (DB + DQ - 1) // DQ
        nch = ntok // QC
        ncols = DQ * P
        wn = [0, 0]
        pe_done = {}
        dve_done = {}
        tiles = [(q, cc, db) for q in range(nq) for cc in range(nch)
                 for db in range(q * DQ, min(DB, (q + 1) * DQ))]
        qtiles = {}
        for i, (q, cc, db) in enumerate(tiles):
            qtiles.setdefault(q, []).append(i)

        pending_w = [None]

        def flush_w():
            if pending_w[0] is not None:
                SP(pending_w[0])
                pending_w[0] = None

        with nc.psum_tensor([P, QC], F32) as po0, \
             nc.psum_tensor([P, QC], F32) as po1:
            pos = [po0, po1]
            for q in range(nq):
                def fg(g, q=q,
                       pm=(wpre_mark if wpre_mark is not None else pe.n)):
                    if q >= 2:
                        g.wait_ge(pe.h, pe_done[qtiles[q - 2][-1]])
                    elif pm:
                        g.wait_ge(pe.h, pm)
                    s_wo[q % 2].inc(g.dma_start(
                        out=wreg[:, wreg_off + (q % 2) * H * ncols:
                                 wreg_off + (q % 2) * H * ncols + H * ncols]
                            .rearrange("p (hb c) -> p hb c", hb=H),
                        in_=wdram.rearrange("(hb p) c -> p hb c", p=P)
                            [:, :, q * ncols:q * ncols + min(ncols, D - q * ncols)]), 16)
                    wn[q % 2] = s_wo[q % 2].n
                POOL(fg)

                for i in qtiles[q]:
                    _, cc, db = tiles[i]
                    col0 = tok0 + cc * QC
                    sl = i % 2
                    woff2 = wreg_off + (q % 2) * H * ncols

                    def fp(t, q=q, cc=cc, db=db, sl=sl, woff2=woff2, i=i,
                           wmark=wn[q % 2], dm0=dve.n):
                        t.wait_ge(s_wo[q % 2].h, wmark)
                        if i >= 2:
                            t.wait_ge(dve.h, dve_done[i - 2])
                        elif dm0:
                            t.wait_ge(dve.h, dm0)
                        for hb in range(H):
                            pe.inc(t.matmul(
                                pos[sl][:, 0:QC],
                                wreg[:, woff2 + hb * ncols + (db - q * DQ) * P:
                                     woff2 + hb * ncols + (db - q * DQ) * P + P],
                                av[:, hb * av_span + av_off + cc * QC:
                                   hb * av_span + av_off + (cc + 1) * QC],
                                start=(hb == 0), stop=(hb == H - 1)))
                        pe_done[i] = pe.n
                    PE(fp)

                    def fs(g, db=db, col0=col0, sl=sl,
                           dm=dve_done.get(i - 2, dve.n)):
                        if dm:
                            g.wait_ge(dve.h, dm)
                        s_x[sl].inc(g.dma_start(
                            out=xt2[:, sl * QC:sl * QC + QC],
                            in_=src[db * P:(db + 1) * P, col0:col0 + QC]), 16)
                    SP(fs)
                    flush_w()
                    xn = s_x[sl].n

                    def fv(v, db=db, sl=sl, i=i):
                        v.wait_ge(pe.h, pe_done[i])
                        v.wait_ge(s_x[sl].h, xn)
                        v.wait_ge(s_resw[sl].h, s_resw[sl].n)
                        if bias_off is not None:
                            dve.inc(v.scalar_tensor_tensor(
                                rest[:, sl * QC:sl * QC + QC], pos[sl][:, 0:QC],
                                small[:, bias_off + db:bias_off + db + 1],
                                xt2[:, sl * QC:sl * QC + QC], ALU.add, ALU.add))
                        else:
                            dve.inc(v.tensor_tensor(
                                rest[:, sl * QC:sl * QC + QC], pos[sl][:, 0:QC],
                                xt2[:, sl * QC:sl * QC + QC], ALU.add))
                        dve_done[i] = dve.n
                    DVE(fv)

                    def fs2(g, db=db, col0=col0, sl=sl, i=i,
                            dd=None):
                        g.wait_ge(dve.h, dve_done[i])
                        s_resw[sl].inc(g.dma_start(
                            out=dst[db * P:(db + 1) * P,
                                    col0 - dst_off:col0 - dst_off + QC],
                            in_=rest[:, sl * QC:sl * QC + QC]), 16)
                    pending_w[0] = fs2
        flush_w()

    def phase_attn_a_slab(slab):
        t0 = slab * ACH
        nch = ACH // QC
        for cc in range(nch):
            col0 = t0 + cc * QC
            with nc.psum_tensor([P, QC], F32) as psA, \
                 nc.psum_tensor([P, QC], F32) as psB, \
                 nc.psum_tensor([1, QC], F32) as ps_sx, \
                 nc.psum_tensor([1, QC], F32) as ps_sx2:
                emit_ln_stats(xTe.ap(), col0, psA, psB, ps_sx, ps_sx2, C_EPSLN)
                emit_ln_apply(xTe.ap(), col0, psA, psB, lnxt, ACH, cc * QC)

        for h in range(H):
            wslot = (h % 2) * DB * P
            pe_before = pe.n

            def fg(g, h=h, wslot=wslot):
                if h >= 2:
                    g.wait_ge(pe.h, pe_before)
                s_w[h % 2].inc(g.dma_start(
                    out=wbuf[:, wslot:wslot + DB * P].rearrange("p (db c) -> p db c", db=DB),
                    in_=Wq.ap().rearrange("(db p) c -> p db c", p=P)[:, :, h * P:(h + 1) * P]), 16)
            POOL(fg)

            with nc.psum_tensor([P, QC], F32) as psq0, \
                 nc.psum_tensor([P, QC], F32) as psq1:
                psqs = [psq0, psq1]
                wsem, wn = s_w[h % 2], s_w[h % 2].n
                pe1 = pe.n
                dve1 = dve.n

                def fp(t, h=h, wslot=wslot):
                    t.wait_ge(wsem.h, wn)
                    for cc in range(nch):
                        if cc >= 2:
                            t.wait_ge(dve.h, dve1 + cc - 1)
                        elif dve1:
                            t.wait_ge(dve.h, dve1)
                        for db in range(DB):
                            pe.inc(t.matmul(
                                psqs[cc % 2][:, 0:QC],
                                wbuf[:, wslot + db * P:wslot + (db + 1) * P],
                                lnxt[:, db * ACH + cc * QC:db * ACH + cc * QC + QC],
                                start=(db == 0), stop=(db == DB - 1)))
                PE(fp)

                def fa(v, h=h):
                    for cc in range(nch):
                        v.wait_ge(pe.h, pe1 + DB * (cc + 1))
                        dve.inc(v.tensor_scalar(
                            bigB[:, h * ACH + cc * QC:h * ACH + cc * QC + QC],
                            psqs[cc % 2][:, 0:QC], scale_a,
                            small[:, BQ + h:BQ + h + 1], ALU.mult, ALU.add))
                DVE(fa)

        preflash_pe = pe.n
        FQ = QC
        for pair in range(H // 2):
            hA, hB = 2 * pair, 2 * pair + 1
            for qi in range(ACH // FQ):
                qoffA = hA * ACH + qi * FQ
                qoffB = hB * ACH + qi * FQ
                with nc.psum_tensor([P, FQ], F32) as psOA, \
                     nc.psum_tensor([P, FQ], F32) as psOB, \
                     nc.psum_tensor([1, FQ], F32) as psDA, \
                     nc.psum_tensor([1, FQ], F32) as psDB, \
                     nc.psum_tensor([P, FQ], F32) as psSA, \
                     nc.psum_tensor([P, FQ], F32) as psSB:
                    pe0, act0, dve0 = pe.n, act.n, dve.n

                    def fp(t):
                        if dve0:
                            t.wait_ge(dve.h, dve0)
                        for kb in range(KB):
                            if kb >= 1:
                                t.wait_ge(act.h, act0 + 2 * (kb - 1) + 1)
                            pe.inc(t.matmul(psSA[:, 0:FQ],
                                            kvb[:, KOFF + kb * P:KOFF + (kb + 1) * P],
                                            bigB[:, qoffA:qoffA + FQ],
                                            start=True, stop=True))
                            if kb >= 1:
                                t.wait_ge(act.h, act0 + 2 * (kb - 1) + 2)
                            pe.inc(t.matmul(psSB[:, 0:FQ],
                                            kvb[:, KOFF + kb * P:KOFF + (kb + 1) * P],
                                            bigB[:, qoffB:qoffB + FQ],
                                            start=True, stop=True))
                            if kb >= 1:
                                for (pD, pO, sl) in ((psDA, psOA, ((kb - 1) % 2)),
                                                     (psDB, psOB, 2 + ((kb - 1) % 2))):
                                    pe.inc(t.matmul(pO[:, 0:FQ],
                                                    kvb[:, VOFF + (kb - 1) * P:VOFF + kb * P],
                                                    ptil[:, sl * FQ:(sl + 1) * FQ],
                                                    start=(kb == 1), stop=False))
                                    pe.inc(t.matmul(pD[0:1, 0:FQ], ones_bf,
                                                    ptil[:, sl * FQ:(sl + 1) * FQ],
                                                    start=(kb == 1), stop=False))
                        t.wait_ge(act.h, act0 + 2 * KB)
                        for (pD, pO, sl) in ((psDA, psOA, ((KB - 1) % 2)),
                                             (psDB, psOB, 2 + ((KB - 1) % 2))):
                            pe.inc(t.matmul(pO[:, 0:FQ],
                                            kvb[:, VOFF + (KB - 1) * P:VOFF + KB * P],
                                            ptil[:, sl * FQ:(sl + 1) * FQ],
                                            start=False, stop=True))
                            pe.inc(t.matmul(pD[0:1, 0:FQ], ones_bf,
                                            ptil[:, sl * FQ:(sl + 1) * FQ],
                                            start=False, stop=True))
                    PE(fp)
                    pe_all = pe.n

                    def fa(s):
                        act_set_pin(s)
                        for kb in range(KB):
                            s.wait_ge(pe.h, pe0 + (1 if kb == 0 else 6 * kb - 3))
                            act.inc(s.activation(
                                ptil[:, (kb % 2) * FQ:(kb % 2) * FQ + FQ],
                                psSA[:, 0:FQ], AF.Exp,
                                bias=small[:, C_NSHIFT:C_NSHIFT + 1]))
                            s.wait_ge(pe.h, pe0 + (2 if kb == 0 else 6 * kb - 2))
                            act.inc(s.activation(
                                ptil[:, (2 + kb % 2) * FQ:(2 + kb % 2) * FQ + FQ],
                                psSB[:, 0:FQ], AF.Exp,
                                bias=small[:, C_NSHIFT:C_NSHIFT + 1]))
                    ACT(fa)

                    with nc.psum_tensor([P, FQ], F32) as psR:
                        for (pD, pO, qoff, rbo) in ((psDA, psOA, qoffA, 0),
                                                    (psDB, psOB, qoffB, FQ)):
                            def ft(s, pD=pD, rbo=rbo):
                                s.wait_ge(pe.h, pe_all)
                                act.inc(s.activation(rows[0:1, R_DEN + rbo:R_DEN + rbo + FQ],
                                                     pD[0:1, 0:FQ], AF.Ln))
                                s.wait_ge(act.h, act.n)
                                act.inc(s.activation(rowsb[0:1, RB_DEN + rbo:RB_DEN + rbo + FQ],
                                                     rows[0:1, R_DEN + rbo:R_DEN + rbo + FQ],
                                                     AF.Exp, scale=-1.0))
                            ACT(ft)

                            def fpr(t, rbo=rbo):
                                t.wait_ge(act.h, act.n)
                                if dve.n:
                                    t.wait_ge(dve.h, dve.n)
                                pe.inc(t.matmul(psR[:, 0:FQ], ones_row_bf,
                                                rowsb[0:1, RB_DEN + rbo:RB_DEN + rbo + FQ],
                                                start=True, stop=True))
                            PE(fpr)

                            def fvt(v, pO=pO, qoff=qoff, rbo=rbo):
                                v.wait_ge(pe.h, pe.n)
                                dve.inc(v.tensor_copy(rtil[:, rbo:rbo + FQ], psR[:, 0:FQ]))
                                v.wait_ge(dve.h, dve.n)
                                dve.inc(v.tensor_tensor(bigB[:, qoff:qoff + FQ],
                                                        pO[:, 0:FQ], rtil[:, rbo:rbo + FQ],
                                                        ALU.mult))
                            DVE(fvt)

        emit_oproj(Wo.ap(), bigB, H * ACH, bigB, ACH, 0, t0, ACH, BO,
                   xTe.ap(), res.ap(), 0, wpre_mark=preflash_pe)
        join_light()

    W0B_CH = P
    OUT0_OFF = 0
    WB_OFF = DB * FTC

    def phase_ffn_gate_setup():
        def fs(g):
            s_wkv.inc(g.dma_start(
                out=wbuf[:, 0:DB * 2].rearrange("p (db e) -> p db e", e=2),
                in_=WgT.ap().rearrange("(db p) e -> p db e", p=P)), 16)
        POOL(fs)

        def fv(v):
            v.wait_ge(s_wkv.h, s_wkv.n)
            for db in range(DB):
                dve.inc(v.tensor_tensor(
                    smb[:, WGDB + db:WGDB + db + 1],
                    wbuf[:, 2 * db:2 * db + 1],
                    wbuf[:, 2 * db + 1:2 * db + 2],
                    ALU.subtract))
        DVE(fv)

        with nc.psum_tensor([1, 1], F32) as psb:
            def fp(t):
                t.wait_ge(dve.h, dve.n)
                for db in range(DB):
                    pe.inc(t.matmul(psb[0:1, 0:1], smb[:, WGDB + db:WGDB + db + 1],
                                    ones_bf, start=(db == 0), stop=(db == DB - 1)))
            PE(fp)

            def fv2(v):
                v.wait_ge(pe.h, pe.n)
                dve.inc(v.tensor_copy(rows[0:1, R_MISC + 4:R_MISC + 5], psb[0:1, 0:1]))
            DVE(fv2)

    def phase_ffn(src, dst, tok0, ntok, dst_off):
        NCH = ntok // FTC
        NTC = FTC // QC
        wb_free = [pe.n]
        for ch in range(NCH):
            base = tok0 + ch * FTC
            for tc in range(NTC):
                col0 = base + tc * QC
                with nc.psum_tensor([P, QC], F32) as psA, \
                     nc.psum_tensor([P, QC], F32) as psB, \
                     nc.psum_tensor([1, QC], F32) as ps_sx, \
                     nc.psum_tensor([1, QC], F32) as ps_sx2, \
                     nc.psum_tensor([1, QC], F32) as ps_g:
                    emit_ln_stats(src, col0, psA, psB, ps_sx, ps_sx2, C_EPSLN,
                                  gate=True, ps_g=ps_g)
                    def fvg(v, tc=tc):
                        v.wait_ge(pe.h, pe.n)
                        v.wait_ge(dve.h, dve.n)
                        dve.inc(v.tensor_single_scalar(
                            rows[0:1, R_T0:R_T0 + QC], rows[0:1, R_MEAN:R_MEAN + QC],
                            rows[0:1, R_MISC + 4:R_MISC + 5], ALU.mult))
                        v.wait_ge(dve.h, dve.n)
                        dve.inc(v.tensor_tensor(rows[0:1, R_T1:R_T1 + QC],
                                                rows[0:1, R_T1:R_T1 + QC],
                                                rows[0:1, R_T0:R_T0 + QC],
                                                ALU.subtract))
                        v.wait_ge(dve.h, dve.n)
                        dve.inc(v.tensor_single_scalar(
                            rowsb[0:1, RB_MASK:RB_MASK + QC], rows[0:1, R_T1:R_T1 + QC],
                            0.0, ALU.is_ge))
                    DVE(fvg)

                    with nc.psum_tensor([P, QC], F32) as psM:
                        def fpm(t, tc=tc):
                            t.wait_ge(dve.h, dve.n)
                            pe.inc(t.matmul(psM[:, 0:QC], ones_row_bf,
                                            rowsb[0:1, RB_MASK:RB_MASK + QC],
                                            start=True, stop=True))
                        PE(fpm)

                        def fvm(v, tc=tc):
                            v.wait_ge(pe.h, pe.n)
                            dve.inc(v.tensor_copy(mtil[:, tc * QC:(tc + 1) * QC],
                                                  psM[:, 0:QC]))
                        DVE(fvm)

                    emit_ln_apply(src, col0, psA, psB, lnxt, FTC, tc * QC)

            for e in range(2):
                WaT = W0aT if e == 0 else W1aT
                WbT = W0bT if e == 0 else W1bT
                BA = B0A if e == 0 else B1A
                BB = B0B if e == 0 else B1B
                with nc.psum_tensor([P, QC], F32) as ph00, \
                     nc.psum_tensor([P, QC], F32) as ph01, \
                     nc.psum_tensor([P, QC], F32) as ph10, \
                     nc.psum_tensor([P, QC], F32) as ph11:
                    psH = [[ph00, ph01], [ph10, ph11]]
                    pe0, act0 = pe.n, act.n

                    def fgw(g, e=e, WaT=WaT):
                        for fb in range(FB):
                            if fb >= 2:
                                g.wait_ge(pe.h, pe0 + DB * NTC * (fb - 1))
                            else:
                                g.wait_ge(pe.h, pe0)
                            s_w[fb % 2].inc(g.dma_start(
                                out=wbuf[:, (fb % 2) * DB * P:(fb % 2 + 1) * DB * P]
                                    .rearrange("p (db c) -> p db c", db=DB),
                                in_=WaT.ap().rearrange("(db p) c -> p db c", p=P)
                                    [:, :, fb * P:(fb + 1) * P]), 16)
                    POOL(fgw)

                    wn0 = [s_w[0].n - (FB + 1) // 2 * 16, s_w[1].n - FB // 2 * 16]

                    def fp1(t, e=e, dm=dve.n):
                        if dm:
                            t.wait_ge(dve.h, dm)
                        for fb in range(FB):
                            t.wait_ge(s_w[fb % 2].h, wn0[fb % 2] + (fb // 2 + 1) * 16)
                            if fb >= 2:
                                t.wait_ge(act.h, act0 + NTC * (fb - 1))
                            elif act0:
                                t.wait_ge(act.h, act0)
                            for db in range(DB):
                                for tc in range(NTC):
                                    pe.inc(t.matmul(
                                        psH[fb % 2][tc][:, 0:QC],
                                        wbuf[:, (fb % 2) * DB * P + db * P:
                                             (fb % 2) * DB * P + (db + 1) * P],
                                        lnxt[:, db * FTC + tc * QC:db * FTC + (tc + 1) * QC],
                                        start=(db == 0), stop=(db == DB - 1)))
                    PE(fp1)

                    AFUN = AF.Gelu if e == 0 else AF.Relu

                    def fa1(s, e=e, BA=BA, AFUN=AFUN):
                        for fb in range(FB):
                            s.wait_ge(pe.h, pe0 + DB * NTC * fb + DB * NTC)
                            for tc in range(NTC):
                                act.inc(s.activation(
                                    bigB[:, fb * FTC + tc * QC:fb * FTC + (tc + 1) * QC],
                                    psH[fb % 2][tc][:, 0:QC], AFUN,
                                    bias=small[:, BA + fb:BA + fb + 1]))
                    ACT(fa1)

                ndq = D // W0B_CH
                wn2 = [0, 0]
                pe2_done = {}
                dve2_done = {}
                pend2 = [None]

                def flush2():
                    if pend2[0] is not None:
                        SP(pend2[0])
                        pend2[0] = None

                with nc.psum_tensor([P, QC], F32) as q20, \
                     nc.psum_tensor([P, QC], F32) as q21:
                    pq2 = [q20, q21]
                    mm2_pre = wb_free[0]
                    for dq in range(ndq):
                        def fgw2(g, e=e, WbT=WbT, dq=dq, pm=mm2_pre):
                            if dq >= 2:
                                g.wait_ge(pe.h, pe2_done[(dq - 2, NTC - 1)])
                            elif pm:
                                g.wait_ge(pe.h, pm)
                            s_wo[dq % 2].inc(g.dma_start(
                                out=kvb[:, WB_OFF + (dq % 2) * FB * W0B_CH:
                                     WB_OFF + (dq % 2 + 1) * FB * W0B_CH]
                                    .rearrange("p (fb c) -> p fb c", fb=FB),
                                in_=WbT.ap().rearrange("(fb p) c -> p fb c", p=P)
                                    [:, :, dq * W0B_CH:(dq + 1) * W0B_CH]), 16)
                            wn2[dq % 2] = s_wo[dq % 2].n
                        POOL(fgw2)

                        db = dq
                        woff2 = WB_OFF + (dq % 2) * FB * W0B_CH
                        for tc in range(NTC):
                            col0 = base + tc * QC
                            i2 = dq * NTC + tc
                            sl = i2 % 2

                            def fp2(t, db=db, tc=tc, sl=sl, i2=i2, dq=dq,
                                    woff2=woff2, am=act.n, wmark=wn2[dq % 2]):
                                t.wait_ge(s_wo[dq % 2].h, wmark)
                                t.wait_ge(act.h, am)
                                if i2 >= 2:
                                    t.wait_ge(dve.h, dve2_done[i2 - 2])
                                elif dve.n:
                                    t.wait_ge(dve.h, dve.n)
                                for fb in range(FB):
                                    pe.inc(t.matmul(
                                        pq2[sl][:, 0:QC],
                                        kvb[:, woff2 + fb * W0B_CH:
                                             woff2 + fb * W0B_CH + P],
                                        bigB[:, fb * FTC + tc * QC:fb * FTC + (tc + 1) * QC],
                                        start=(fb == 0), stop=(fb == FB - 1)))
                                pe2_done[(dq, tc)] = pe.n
                            PE(fp2)
                            pe2_done_i = None

                            if e == 0:
                                def fv20(v, db=db, tc=tc, BB=BB, sl=sl, i2=i2, dq=dq):
                                    v.wait_ge(pe.h, pe2_done[(dq, tc)])
                                    dve.inc(v.tensor_scalar(
                                        kvb[:, OUT0_OFF + db * FTC + tc * QC:
                                             OUT0_OFF + db * FTC + (tc + 1) * QC],
                                        pq2[sl][:, 0:QC],
                                        small[:, BB + db:BB + db + 1], None,
                                        ALU.add))
                                    dve2_done[i2] = dve.n
                                DVE(fv20)
                            else:
                                def fsx(g, db=db, col0=col0, sl=sl,
                                        dm=dve2_done.get(i2 - 2, dve.n)):
                                    if dm:
                                        g.wait_ge(dve.h, dm)
                                    s_x[sl].inc(g.dma_start(
                                        out=xt2[:, sl * QC:sl * QC + QC],
                                        in_=src[db * P:(db + 1) * P, col0:col0 + QC]), 16)
                                SP(fsx)
                                flush2()
                                xn = s_x[sl].n

                                def fv21(v, db=db, tc=tc, BB=BB, sl=sl, i2=i2,
                                         dq=dq, xn=xn):
                                    v.wait_ge(pe.h, pe2_done[(dq, tc)])
                                    v.wait_ge(s_x[sl].h, xn)
                                    v.wait_ge(s_resw[sl].h, s_resw[sl].n)
                                    dve.inc(v.scalar_tensor_tensor(
                                        sq2[:, sl * QC:sl * QC + QC], pq2[sl][:, 0:QC],
                                        small[:, BB + db:BB + db + 1],
                                        kvb[:, OUT0_OFF + db * FTC + tc * QC:
                                             OUT0_OFF + db * FTC + (tc + 1) * QC],
                                        ALU.add, ALU.subtract))
                                    v.wait_ge(dve.h, dve.n)
                                    dve.inc(v.tensor_tensor(
                                        qsc[:, sl * QC:sl * QC + QC],
                                        sq2[:, sl * QC:sl * QC + QC],
                                        mtil[:, tc * QC:(tc + 1) * QC], ALU.mult))
                                    v.wait_ge(dve.h, dve.n)
                                    dve.inc(v.scalar_tensor_tensor(
                                        sq2[:, sl * QC:sl * QC + QC], pq2[sl][:, 0:QC],
                                        small[:, BB + db:BB + db + 1],
                                        qsc[:, sl * QC:sl * QC + QC],
                                        ALU.add, ALU.subtract))
                                    v.wait_ge(dve.h, dve.n)
                                    dve.inc(v.tensor_tensor(
                                        rest[:, sl * QC:sl * QC + QC],
                                        sq2[:, sl * QC:sl * QC + QC],
                                        xt2[:, sl * QC:sl * QC + QC], ALU.add))
                                    dve2_done[i2] = dve.n
                                DVE(fv21)

                                def fso(g, db=db, col0=col0, dst=dst, sl=sl, i2=i2,
                                        dst_off=dst_off):
                                    g.wait_ge(dve.h, dve2_done[i2])
                                    s_resw[sl].inc(g.dma_start(
                                        out=dst[db * P:(db + 1) * P,
                                                col0 - dst_off:col0 - dst_off + QC],
                                        in_=rest[:, sl * QC:sl * QC + QC]), 16)
                                pend2[0] = fso

                    flush2()
                    wb_free[0] = pe.n

    EB = EXT // P
    NQB2 = NT // 1024 if NT % 1024 == 0 else NT // min(NT, 1024)
    FQ2 = min(1024, NT)
    VB_OFF = KVB * EXT

    def phase_ropetab():
        def fg(g):
            pool.inc(g.memset(sq2[:, 0:P], 0.0))
            g.wait_ge(pool.h, pool.n)
            pool.inc(g.affine_select(out=sq2[:, 0:P], in_=sq2[:, 0:P],
                                     compare_op=ALU.not_equal, fill=1.0,
                                     base=-(P // 2), pattern=[[1, P]],
                                     channel_multiplier=-1))
            g.wait_ge(pool.h, pool.n)
            pool.inc(g.affine_select(out=sq2[:, 0:P], in_=sq2[:, 0:P],
                                     compare_op=ALU.not_equal, fill=-1.0,
                                     base=(P // 2), pattern=[[1, P]],
                                     channel_multiplier=-1))
            g.wait_ge(pool.h, pool.n)
            s_wkv.inc(g.dma_start(out=ropet[:, 0:2 * EXT], in_=ropeT_.ap()), 16)
        POOL(fg)

        def fv0(v):
            v.wait_ge(pool.h, pool.n)
            dve.inc(v.tensor_copy(rtc[:, 0:P], sq2[:, 0:P]))
            v.wait_ge(s_wkv.h, s_wkv.n)
        DVE(fv0)

    def phase_attn_b_proj():
        NCH = EXT // FTC
        NTC = FTC // QC
        qsc4 = qsc

        with nc.psum_tensor([P, QC], F32) as pq0, \
             nc.psum_tensor([P, QC], F32) as pq1, \
             nc.psum_tensor([1, QC], F32) as pn0, \
             nc.psum_tensor([1, QC], F32) as pn1, \
             nc.psum_tensor([P, QC], F32) as pR0, \
             nc.psum_tensor([P, QC], F32) as pR1, \
             nc.psum_tensor([P, QC], F32) as psrot, \
             nc.psum_tensor([P, P], BF16) as pst:
            psq = [pq0, pq1]
            psn = [pn0, pn1]
            psR = [pR0, pR1]
            st = {"i": 0, "dve_done": {}, "pe_done": {}, "rot_done": 0,
                  "pst_done": dve.n, "ln_dve": 0}

            def emit_proj(wslot, wsem, wmark, tc_col):
                i = st["i"]
                sl = i % 2

                def fp(t, i=i, sl=sl, wslot=wslot, tc_col=tc_col, wmark=wmark,
                       dm=max(st["dve_done"].get(i - 2, dve.n), st["ln_dve"])):
                    t.wait_ge(wsem.h, wmark)
                    if dm:
                        t.wait_ge(dve.h, dm)
                    for db in range(DB):
                        pe.inc(t.matmul(
                            psq[sl][:, 0:QC],
                            wbuf[:, wslot + db * P:wslot + (db + 1) * P],
                            lnxt[:, db * FTC + tc_col:db * FTC + tc_col + QC],
                            start=(db == 0), stop=(db == DB - 1)))
                    st["pe_done"][i] = pe.n
                PE(fp)

            def emit_tail_qk(gain_off, tc_abs, dst, dcol):
                i = st["i"]
                sl = i % 2

                def t1(v, i=i, sl=sl):
                    v.wait_ge(pe.h, st["pe_done"][i])
                    dve.inc(v.tensor_copy(qsc4[:, sl * QC:sl * QC + QC],
                                          psq[sl][:, 0:QC]))
                    v.wait_ge(dve.h, dve.n)
                    dve.inc(v.tensor_tensor(sqb[:, sl * QC:sl * QC + QC],
                                            qsc4[:, sl * QC:sl * QC + QC],
                                            qsc4[:, sl * QC:sl * QC + QC],
                                            ALU.mult))
                DVE(t1)
                a1 = dve.n

                def t2(t, i=i, sl=sl):
                    t.wait_ge(dve.h, a1)
                    pe.inc(t.matmul(psn[sl][0:1, 0:QC], ones128_bf,
                                    sqb[:, sl * QC:sl * QC + QC],
                                    start=True, stop=True))
                PE(t2)
                p2 = pe.n

                rT = R_T0 + sl * 512
                rB = sl * 512

                def t45(s, i=i, sl=sl, rT=rT, rB=rB):
                    s.wait_ge(pe.h, p2)
                    s.wait_ge(act.h, act.n)
                    act.inc(s.activation(rows[0:1, rT:rT + QC],
                                         psn[sl][0:1, 0:QC], AF.Ln,
                                         bias=small[0:1, C_EPSRMS:C_EPSRMS + 1]))
                    s.wait_ge(act.h, act.n)
                    act.inc(s.activation(rowsb[0:1, rB:rB + QC],
                                         rows[0:1, rT:rT + QC], AF.Exp,
                                         scale=-0.5))
                ACT(t45)
                a5 = act.n

                def t6(t, i=i, sl=sl, rB=rB):
                    t.wait_ge(act.h, a5)
                    pe.inc(t.matmul(psR[sl][:, 0:QC], ones_row_bf,
                                    rowsb[0:1, rB:rB + QC], start=True, stop=True))
                PE(t6)
                p6 = pe.n

                def t78(v, i=i, sl=sl, gain_off=gain_off):
                    v.wait_ge(pe.h, p6)
                    dve.inc(v.tensor_scalar(qsc4[:, 2 * QC:3 * QC],
                                            qsc4[:, sl * QC:sl * QC + QC],
                                            small[:, gain_off:gain_off + 1], None,
                                            ALU.mult))
                    v.wait_ge(dve.h, dve.n)
                    dve.inc(v.tensor_tensor(ptil[:, sl * QC:sl * QC + QC],
                                            qsc4[:, 2 * QC:3 * QC],
                                            psR[sl][:, 0:QC], ALU.mult))
                DVE(t78)
                d8 = dve.n

                def t9(t, i=i, sl=sl, rd=st["rot_done"]):
                    t.wait_ge(dve.h, d8)
                    if rd:
                        t.wait_ge(dve.h, rd)
                    pe.inc(t.matmul(psrot[:, 0:QC], rtc[:, 0:P],
                                    ptil[:, sl * QC:sl * QC + QC],
                                    start=True, stop=True))
                PE(t9)
                p9 = pe.n

                def t10(v, i=i, sl=sl, tc_abs=tc_abs, dst=dst, dcol=dcol):
                    v.wait_ge(pe.h, p9)
                    dve.inc(v.tensor_tensor(qsc4[:, sl * QC:sl * QC + QC],
                                            ptil[:, sl * QC:sl * QC + QC],
                                            ropet[:, tc_abs:tc_abs + QC], ALU.mult))
                    v.wait_ge(dve.h, dve.n)
                    dve.inc(v.tensor_tensor(qsc4[:, 2 * QC:3 * QC],
                                            psrot[:, 0:QC],
                                            ropet[:, EXT + tc_abs:EXT + tc_abs + QC],
                                            ALU.mult))
                    st["rot_done"] = dve.n
                    v.wait_ge(dve.h, dve.n)
                    dve.inc(v.tensor_tensor(dst[:, dcol:dcol + QC],
                                            qsc4[:, sl * QC:sl * QC + QC],
                                            qsc4[:, 2 * QC:3 * QC],
                                            ALU.add))
                    st["dve_done"][i] = dve.n
                DVE(t10)

            def emit_tail_v(g, tc_abs):
                i = st["i"]
                sl = i % 2

                def v1(v, i=i, sl=sl, pd=st["pst_done"]):
                    v.wait_ge(pe.h, st["pe_done"][i])
                    dve.inc(v.tensor_copy(ptil[:, sl * QC:sl * QC + QC],
                                          psq[sl][:, 0:QC]))
                DVE(v1)
                dv1 = dve.n
                nb = QC // P
                pe_tr = []

                def v2(t, i=i, sl=sl, g=g, tc_abs=tc_abs, dv1=dv1,
                       pd=st["pst_done"]):
                    for j in range(nb):
                        t.wait_ge(dve.h, dv1 + j)
                        if j == 0 and pd:
                            t.wait_ge(dve.h, max(dv1, pd))
                        pe.inc(t.transpose(pst[:],
                                           ptil[:, sl * QC + j * P:sl * QC + (j + 1) * P],
                                           ident[:]))
                        pe_tr.append(pe.n)
                PE(v2)

                def v3(v, i=i, g=g, tc_abs=tc_abs):
                    for j in range(nb):
                        v.wait_ge(pe.h, pe_tr[j])
                        o = VB_OFF + g * EXT + tc_abs + j * P
                        dve.inc(v.tensor_copy(kvb[:, o:o + P], pst[:]))
                    st["pst_done"] = dve.n
                    st["dve_done"][i] = dve.n
                DVE(v3)

            wload = {"n": 0}

            def load_w(wT, blk):
                li = wload["n"]
                wload["n"] += 1
                wslot = (li % 2) * DB * P
                wsem = s_w[li % 2]

                def fg(g_, wT=wT, blk=blk, wslot=wslot,
                       pm=st["pe_done"].get(st["i"] - 2, pe.n)):
                    if pm:
                        g_.wait_ge(pe.h, pm)
                    wsem.inc(g_.dma_start(
                        out=wbuf[:, wslot:wslot + DB * P].rearrange("p (db c) -> p db c", db=DB),
                        in_=wT.ap().rearrange("(db p) c -> p db c", p=P)
                            [:, :, blk * P:(blk + 1) * P]), 16)
                POOL(fg)
                return wslot, wsem, wsem.n

            for ch in range(NCH):
                base = ch * FTC
                for tc in range(NTC):
                    col0 = base + tc * QC
                    emit_ln_stats(res.ap(), col0, pR0, pR1, pn0, pn1, C_EPSLN)
                    emit_ln_apply(res.ap(), col0, pR0, pR1, lnxt, FTC, tc * QC)
                    st["ln_dve"] = dve.n

                for g in range(KVB):
                    wslot, wsem, wmark = load_w(wkT, g)
                    for tc in range(NTC):
                        emit_proj(wslot, wsem, wmark, tc * QC)
                        emit_tail_qk(KNW, base + tc * QC, kvb, g * EXT + base + tc * QC)
                        st["i"] += 1
                    wslot, wsem, wmark = load_w(wvT, g)
                    for tc in range(NTC):
                        emit_proj(wslot, wsem, wmark, tc * QC)
                        emit_tail_v(g, base + tc * QC)
                        st["i"] += 1
                for h in range(H):
                    cols = [tc for tc in range(NTC) if base + tc * QC >= HALO]
                    if not cols:
                        continue
                    wslot, wsem, wmark = load_w(wqT, h)
                    for tc in cols:
                        emit_proj(wslot, wsem, wmark, tc * QC)
                        emit_tail_qk(QNW, base + tc * QC, bigB,
                                     h * NT + (base + tc * QC - HALO))
                        st["i"] += 1

    def phase_flash_b():
        WBK = W // P
        for h in range(H):
            g = h // RH
            for qi in range(NT // FQ2):
                q0 = HALO + qi * FQ2
                kb_lo = (q0 - W + 1) // P
                kb_hi = (q0 + FQ2 - 1) // P
                kbs = list(range(kb_lo, kb_hi + 1))
                nk = len(kbs)
                qoff = h * NT + qi * FQ2

                geo = []
                for kb in kbs:
                    qlo = max(q0, P * kb)
                    qhi = min(q0 + FQ2, P * kb + W + P)
                    subs = []
                    a = qlo
                    while a < qhi:
                        bnd = q0 + ((a - q0) // 512 + 1) * 512
                        b_ = min(qhi, bnd)
                        subs.append((a - q0, b_ - a))
                        a = b_
                    diag = (P * kb >= q0)
                    far = (qhi == P * kb + W + P)
                    halo_k = (kb * P < HALO)
                    geo.append((kb, qlo, qhi, subs, diag, far, halo_k))

                with nc.psum_tensor([P, FQ2], F32) as psO, \
                     nc.psum_tensor([1, FQ2], F32) as psD:
                  with nc.psum_tensor([P, FQ2], F32) as psS0, \
                       nc.psum_tensor([P, FQ2], F32) as psS1:
                    psS = [psS0, psS1]
                    pe0, act0, dve0 = pe.n, act.n, dve.n
                    pe_after_scores = []
                    dve_after_masks = []
                    pe_c = pe0
                    dve_c = dve0
                    for ki, (kb, qlo, qhi, subs, diag, far, halo_k) in enumerate(geo):
                        pe_c += len(subs)
                        pe_after_scores.append(pe_c)
                        dve_c += (1 if diag else 0) + (1 if far else 0)
                        dve_after_masks.append(dve_c)
                        if ki >= 1:
                            pe_c += 2 * len(geo[ki - 1][3])
                    pe_c += 2 * len(geo[-1][3])

                    started_O = set()
                    started_D = set()

                    def fp(t, h=h, g=g, q0=q0):
                        if dve0:
                            t.wait_ge(dve.h, dve0)
                        for ki, (kb, qlo, qhi, subs, diag, far, halo_k) in enumerate(geo):
                            if ki >= 2:
                                t.wait_ge(act.h, act0 + ki - 1)
                            for off, n in subs:
                                pe.inc(t.matmul(
                                    psS[ki % 2][:, off:off + n],
                                    kvb[:, g * EXT + kb * P:g * EXT + (kb + 1) * P],
                                    bigB[:, qoff + off:qoff + off + n],
                                    start=True, stop=True))
                            if ki >= 1:
                                t.wait_ge(act.h, act0 + ki)
                                kbp, qlop, qhip, subsp, _, _, _ = geo[ki - 1]
                                pl = ((ki - 1) % 2) * FQ2
                                for off, n in subsp:
                                    st = (off // 512) not in started_O
                                    started_O.add(off // 512)
                                    pe.inc(t.matmul(
                                        psO[:, off:off + n],
                                        kvb[:, VB_OFF + g * EXT + kbp * P:
                                             VB_OFF + g * EXT + (kbp + 1) * P],
                                        ptil[:, pl + off:pl + off + n],
                                        start=st, stop=False))
                                for off, n in subsp:
                                    st = (off // 512) not in started_D
                                    started_D.add(off // 512)
                                    pe.inc(t.matmul(
                                        psD[0:1, off:off + n], ones_bf,
                                        ptil[:, pl + off:pl + off + n],
                                        start=st, stop=False))
                        t.wait_ge(act.h, act0 + len(geo))
                        kbp, qlop, qhip, subsp, _, _, _ = geo[-1]
                        pl = ((len(geo) - 1) % 2) * FQ2
                        for off, n in subsp:
                            st = (off // 512) not in started_O
                            started_O.add(off // 512)
                            pe.inc(t.matmul(
                                psO[:, off:off + n],
                                kvb[:, VB_OFF + g * EXT + kbp * P:
                                     VB_OFF + g * EXT + (kbp + 1) * P],
                                ptil[:, pl + off:pl + off + n],
                                start=st, stop=True))
                        for off, n in subsp:
                            st = (off // 512) not in started_D
                            started_D.add(off // 512)
                            pe.inc(t.matmul(
                                psD[0:1, off:off + n], ones_bf,
                                ptil[:, pl + off:pl + off + n],
                                start=st, stop=True))
                    PE(fp)

                    def fm(v, q0=q0):
                        for ki, (kb, qlo, qhi, subs, diag, far, halo_k) in enumerate(geo):
                            if not (diag or far):
                                continue
                            v.wait_ge(pe.h, pe_after_scores[ki])
                            sl = psS[ki % 2]
                            if diag:
                                o = P * kb - q0
                                dve.inc(v.tensor_tensor(
                                    sl[:, o:o + P], sl[:, o:o + P],
                                    mask2[:, P:2 * P], ALU.add))
                            if far:
                                o = qhi - P - q0
                                dve.inc(v.tensor_tensor(
                                    sl[:, o:o + P], sl[:, o:o + P],
                                    mask2[:, 0:P], ALU.add))
                    DVE(fm)

                    def fa(s, q0=q0):
                        act_set_pin(s)
                        for ki, (kb, qlo, qhi, subs, diag, far, halo_k) in enumerate(geo):
                            s.wait_ge(pe.h, pe_after_scores[ki])
                            s.wait_ge(dve.h, dve_after_masks[ki])
                            bias = small[:, HALOB:HALOB + 1] if halo_k \
                                else small[:, C_NSHIFT:C_NSHIFT + 1]
                            pl = (ki % 2) * FQ2
                            o = qlo - q0
                            act.inc(s.activation(
                                ptil[:, pl + o:pl + (qhi - q0)],
                                psS[ki % 2][:, o:qhi - q0], AF.Exp, bias=bias))
                    ACT(fa)
                    act_all = act.n

                  with nc.psum_tensor([P, FQ2], F32) as psR:
                    def fv(s):
                        s.wait_ge(pe.h, pe.n)
                        act.inc(s.activation(rows[0:1, R_DEN:R_DEN + FQ2],
                                             psD[0:1, 0:FQ2], AF.Ln))
                        s.wait_ge(act.h, act.n)
                        act.inc(s.activation(rowsb[0:1, RB_DEN:RB_DEN + FQ2],
                                             rows[0:1, R_DEN:R_DEN + FQ2],
                                             AF.Exp, scale=-1.0))
                    ACT(fv)

                    def fp2(t):
                        t.wait_ge(act.h, act.n)
                        for ss in range(FQ2 // QC):
                            pe.inc(t.matmul(
                                psR[:, ss * QC:(ss + 1) * QC], ones_row_bf,
                                rowsb[0:1, RB_DEN + ss * QC:RB_DEN + (ss + 1) * QC],
                                start=True, stop=True))
                    PE(fp2)

                    def fv2(v, qoff=qoff):
                        v.wait_ge(pe.h, pe.n)
                        dve.inc(v.tensor_copy(rtil[:, 0:FQ2], psR[:, 0:FQ2]))
                        v.wait_ge(dve.h, dve.n)
                        dve.inc(v.tensor_tensor(bigB[:, qoff:qoff + FQ2],
                                                psO[:, 0:FQ2], rtil[:, 0:FQ2],
                                                ALU.mult))
                    DVE(fv2)

    phase_setup()
    join_all()
    phase_kv()
    join_all()
    for slab in range(NSLAB):
        phase_attn_a_slab(slab)
    phase_ffn_gate_setup()
    join_light()
    phase_ffn(res.ap(), res.ap(), 0, EXT, 0)
    join_light()
    phase_ropetab()
    join_light()
    phase_attn_b_proj()
    join_light()
    phase_flash_b()
    join_light()
    emit_oproj(woT.ap(), kvb, 0, bigB, NT, 0, HALO, NT, None,
               res.ap(), res.ap(), 0)
    join_light()
    phase_ffn(res.ap(), outT.ap(), HALO, NT, HALO)

    def ffinal(s):
        s.wait_ge(s_resw[0].h, s_resw[0].n)
        s.wait_ge(s_resw[1].h, s_resw[1].n)
    SP(ffinal)

    ctx.close()
    return nc



def _rope_tables(cfg, ext0):
    pos = ext0 + np.arange(cfg.EXT, dtype=np.float64)
    inv = 1.0 / (10000.0 ** (np.arange(0, P, 2, dtype=np.float64) / P))
    ang = inv[:, None] * pos[None, :]
    cos = np.cos(ang).astype(np.float32)
    sin = np.sin(ang).astype(np.float32)
    cos2 = np.concatenate([cos, cos], axis=0)
    sin2 = np.concatenate([sin, sin], axis=0)
    return np.concatenate([cos2, sin2], axis=1)


def make_core_inputs(cfg, inp, core):
    bsz = 4
    bidx, rank = divmod(core, bsz)
    t0 = rank * cfg.NT
    x = inp["L0_attn_a_x"]
    xT = np.ascontiguousarray(np.transpose(x[bidx]))
    ext0 = t0 - cfg.HALO
    if ext0 < 0:
        pad = np.zeros((cfg.D, -ext0), np.float32)
        xTe = np.concatenate([pad, xT[:, 0:t0 + cfg.NT]], axis=1)
    else:
        xTe = xT[:, ext0:t0 + cfg.NT]
    halo_valid = 0.0 if ext0 >= 0 else -30000.0
    d = {
        "xTb": xT,
        "xTe": np.ascontiguousarray(xTe),
        "Wq": inp["L0_attn_a_Wq"],
        "Wk": inp["L0_attn_a_Wk"],
        "Wv": inp["L0_attn_a_Wv"],
        "Wo": inp["L0_attn_a_Wo"],
        "bqT": np.ascontiguousarray(inp["L0_attn_a_bq"].reshape(cfg.H, P).T),
        "bk": inp["L0_attn_a_bk"].reshape(P, 1),
        "bv": inp["L0_attn_a_bv"].reshape(P, 1),
        "boT": np.ascontiguousarray(inp["L0_attn_a_bo"].reshape(cfg.DB, P).T),
        "WgT": np.ascontiguousarray(inp["L0_ffn_Wg"].T),
        "W0aT": np.ascontiguousarray(inp["L0_ffn_W0a"].T),
        "W0bT": np.ascontiguousarray(inp["L0_ffn_W0b"].T),
        "W1aT": np.ascontiguousarray(inp["L0_ffn_W1a"].T),
        "W1bT": np.ascontiguousarray(inp["L0_ffn_W1b"].T),
        "b0aT": np.ascontiguousarray(inp["L0_ffn_b0a"].reshape(cfg.FB, P).T),
        "b0bT": np.ascontiguousarray(inp["L0_ffn_b0b"].reshape(cfg.DB, P).T),
        "b1aT": np.ascontiguousarray(inp["L0_ffn_b1a"].reshape(cfg.FB, P).T),
        "b1bT": np.ascontiguousarray(inp["L0_ffn_b1b"].reshape(cfg.DB, P).T),
        "wqT": np.ascontiguousarray(inp["L0_attn_b_wq"].T),
        "wkT": np.ascontiguousarray(inp["L0_attn_b_wk"].T),
        "wvT": np.ascontiguousarray(inp["L0_attn_b_wv"].T),
        "woT": np.ascontiguousarray(inp["L0_attn_b_wo"].T),
        "qnw": inp["L0_attn_b_q_norm_w"].reshape(P, 1),
        "knw": inp["L0_attn_b_k_norm_w"].reshape(P, 1),
        "pos0": np.array([[float(ext0)]], np.float32),
        "ropeT": _rope_tables(cfg, ext0),
        "halob": np.full((P, 1), halo_valid, np.float32),
    }
    return {k: np.ascontiguousarray(np.asarray(v, np.float32)) for k, v in d.items()}


def kernel(**inputs):
    cfg = Cfg()
    nc = build_program(cfg)
    in_maps = [make_core_inputs(cfg, inputs, c) for c in range(8)]
    r = run_bass_kernel_spmd(nc, in_maps, core_ids=list(range(8)))
    outs = []
    for b in range(2):
        cores = [r.results[b * 4 + i]["outT"].T for i in range(4)]
        outs.append(np.concatenate(cores, axis=0))
    return np.stack(outs, axis=0).astype(np.float32)


# version 25
# speedup vs baseline: 1.1254x; 1.0008x over previous
import math
from contextlib import ExitStack

import numpy as np

import concourse.bass as bass
import concourse.mybir as mybir
from concourse.bass_utils import run_bass_kernel_spmd

F32 = mybir.dt.float32
BF16 = mybir.dt.bfloat16
F32R = mybir.dt.float32r
AF = mybir.ActivationFunctionType
ALU = mybir.AluOpType
AX = mybir.AxisListType

P = 128
CSHIFT = 12.0


class Cfg:
    def __init__(self, D=2048, S=8192, NT=2048, HALO=1024, H=16, KVB=4, F=4096,
                 QC=512, ACH=1024, FTC=1024):
        self.D, self.S, self.NT, self.HALO = D, S, NT, HALO
        self.H, self.KVB, self.F = H, KVB, F
        self.QC, self.ACH, self.FTC = QC, ACH, FTC
        self.DB = D // P
        self.EXT = NT + HALO
        self.FB = F // P
        self.W = HALO
        self.RH = H // KVB
        assert self.EXT % ACH == 0 and ACH % QC == 0 and S % QC == 0
        assert self.EXT % FTC == 0 and FTC % QC == 0 and NT % FTC == 0
        assert HALO % P == 0 and NT % P == 0


class Sem:
    __slots__ = ("h", "n")

    def __init__(self, nc, name):
        self.h = nc.alloc_semaphore(name=name)
        self.n = 0

    def inc(self, instr, k=1):
        instr.then_inc(self.h, k)
        self.n += k
        return self.n


def build_program(cfg: Cfg):
    nc = bass.Bass("TRN2", target_bir_lowering=False, debug=False)
    ctx = ExitStack()
    D, DB, S, NT, HALO, EXT = cfg.D, cfg.DB, cfg.S, cfg.NT, cfg.HALO, cfg.EXT
    H, KVB, F, FB, RH = cfg.H, cfg.KVB, cfg.F, cfg.FB, cfg.RH
    QC, ACH, FTC, W = cfg.QC, cfg.ACH, cfg.FTC, cfg.W
    SC = S // QC
    NSLAB = EXT // ACH
    KB = S // P
    WB = W // P

    dram = {}

    def din(name, shape, dtype=F32):
        dram[name] = nc.dram_tensor(name, list(shape), dtype, kind="ExternalInput")
        return dram[name]

    xTb = din("xTb", (D, S))
    xTe = din("xTe", (D, EXT))
    Wq = din("Wq", (D, H * P))
    Wk = din("Wk", (D, P))
    Wv = din("Wv", (D, P))
    Wo = din("Wo", (H * P, D))
    bqT = din("bqT", (P, H))
    bk_ = din("bk", (P, 1))
    bv_ = din("bv", (P, 1))
    boT = din("boT", (P, DB))
    WgT = din("WgT", (D, 2))
    W0aT = din("W0aT", (D, F))
    W0bT = din("W0bT", (F, D))
    W1aT = din("W1aT", (D, F))
    W1bT = din("W1bT", (F, D))
    b0aT = din("b0aT", (P, FB))
    b0bT = din("b0bT", (P, DB))
    b1aT = din("b1aT", (P, FB))
    b1bT = din("b1bT", (P, DB))
    wqT = din("wqT", (D, H * P))
    wkT = din("wkT", (D, KVB * P))
    wvT = din("wvT", (D, KVB * P))
    woT = din("woT", (H * P, D))
    qnw_ = din("qnw", (P, 1))
    knw_ = din("knw", (P, 1))
    pos0_ = din("pos0", (1, 1))
    ropeT_ = din("ropeT", (P, 2 * (NT + HALO)))
    halob_ = din("halob", (P, 1))

    res = nc.dram_tensor("res", [D, EXT], F32)
    outT = nc.dram_tensor("outT", [D, NT], F32, kind="ExternalOutput")

    sbufs = {}

    def sb(name, shape, dtype):
        sbufs[name] = ctx.enter_context(nc.sbuf_tensor(name, list(shape), dtype))
        return sbufs[name]

    lnxt = sb("lnxt", (P, DB * max(ACH, FTC)), BF16)
    WQCH = max(1, DB // 4) * P
    bigB = sb("bigB", (P, max(H * ACH + 2 * H * WQCH, FB * FTC,
                              H * NT, 2 * H * WQCH)), BF16)
    kvb = sb("kvb", (P, max(2 * S, 2 * KVB * EXT, DB * FTC + FB * 256)), BF16)
    wbuf = sb("wbuf", (P, 2 * DB * P), BF16)
    xt2 = sb("xt2", (P, 2 * QC), F32)
    xb2 = sb("xb2", (P, 2 * QC), BF16)
    sqb = sb("sqb", (P, 2 * QC), BF16)
    sq2 = sb("sq2", (P, 2 * QC), F32)
    rest = sb("rest", (P, 2 * QC), F32)
    ptil = sb("ptil", (P, max(4 * QC, 2 * ACH, 2 * min(1024, NT))), BF16)
    rtil = sb("rtil", (P, max(2 * QC, ACH, min(1024, NT))), BF16)
    mtil = sb("mtil", (P, FTC), BF16)
    qsc = sb("qsc", (P, 3 * QC), F32)
    ropet = sb("ropet", (P, 2 * EXT), BF16)
    rtc = sb("rtc", (P, P), BF16)
    rows = sb("rows", (1, 2112), F32)
    rowsb = sb("rowsb", (1, 1024), BF16)
    small = sb("small", (P, 288), F32)
    smb = sb("smb", (P, 160), BF16)
    mask2 = sb("mask2", (P, 2 * P), BF16)
    ident = sb("ident", (P, P), BF16)

    ONES = 0
    BQ = 1
    BK = BQ + H
    BV = BK + 1
    BO = BV + 1
    B0A = BO + DB
    B0B = B0A + FB
    B1A = B0B + DB
    B1B = B1A + FB
    QNW = B1B + DB
    KNW = QNW + 1
    HALOB = KNW + 1
    WBARK = HALOB + 1
    WBARV = WBARK + 1
    WGD = WBARV + 1
    ONE_ROW_P0 = WGD + DB
    C_NSHIFT = ONE_ROW_P0 + P
    C_EPSLN = C_NSHIFT + 1
    C_EPSRMS = C_EPSLN + 1
    C_EPSRMS128 = C_EPSRMS + 1
    C_LNHALF128 = C_EPSRMS128 + 1
    HALOR = C_LNHALF128 + 1
    assert HALOR + 1 <= 288

    R_MEAN, R_RSTD, R_T0, R_T1 = 0, 512, 1024, 1536
    R_DEN = 1024
    R_MISC = 2048
    RB_RSTD, RB_MB, RB_DEN, RB_MASK = 0, 512, 0, 0


    pe = Sem(nc, "pe")
    act = Sem(nc, "act")
    dve = Sem(nc, "dve")
    pool = Sem(nc, "pool")
    s_bias = Sem(nc, "s_bias")
    s_wkv = Sem(nc, "s_wkv")
    s_x = [Sem(nc, "s_x0"), Sem(nc, "s_x1")]
    s_xb = [Sem(nc, "s_xb0"), Sem(nc, "s_xb1")]
    s_resw = [Sem(nc, "s_resw0"), Sem(nc, "s_resw1")]
    s_w = [Sem(nc, "s_w0"), Sem(nc, "s_w1")]
    s_wo = [Sem(nc, "s_wo0"), Sem(nc, "s_wo1")]
    s_dbg = Sem(nc, "s_dbg")
    all_dma_sems = [s_bias, s_wkv, s_x[0], s_x[1], s_xb[0], s_xb[1],
                    s_resw[0], s_resw[1], s_w[0], s_w[1], s_wo[0], s_wo[1],
                    s_dbg]

    block = ctx.enter_context(nc.Block())

    ones_col = small[:, ONES:ONES + 1]
    ones_row = small[0:1, ONE_ROW_P0:ONE_ROW_P0 + P]
    ones_bf = smb[:, 0:1]
    ones128_bf = smb[:, 1:2]
    ones_row_bf = smb[0:1, 8:8 + P]
    WGDB = 140
    scale_a = 1.0 / math.sqrt(P)


    def PE(f): block.tensor(f)
    def ACT(f): block.scalar(f)
    def DVE(f): block.vector(f)
    def POOL(f): block.gpsimd(f)
    def SP(f): block.sync(f)

    _pin_rot = [0]

    def act_set_pin(s):
        c = R_MISC + 8 + (_pin_rot[0] % 8)
        _pin_rot[0] += 1
        s.activation(rows[0:1, c:c + 1],
                     small[0:1, C_EPSRMS:C_EPSRMS + 1], AF.Ln)

    def join_all():
        marks = [(s, s.n) for s in ([pe, act, dve, pool] + all_dma_sems)]

        def mk():
            def f(e):
                for s, n in marks:
                    if n:
                        e.wait_ge(s.h, n)
            return f
        PE(mk()); ACT(mk()); DVE(mk()); POOL(mk()); SP(mk())

    def join_light():
        marks = [(s, s.n) for s in (pe, act, dve, pool, s_resw[0], s_resw[1],
                                    s_x[0], s_x[1], s_xb[0], s_xb[1])]

        def mk():
            def f(e):
                for s, n in marks:
                    if n:
                        e.wait_ge(s.h, n)
            return f
        PE(mk()); ACT(mk()); DVE(mk()); POOL(mk()); SP(mk())

    def phase_setup():
        def f(g):
            pool.inc(g.memset(small[:, ONES:ONES + 1], 1.0))
            pool.inc(g.memset(small[0:1, ONE_ROW_P0:ONE_ROW_P0 + P], 1.0))
            pool.inc(g.memset(smb[:, 0:1], 1.0))
            pool.inc(g.memset(smb[:, 1:2], 1.0 / 128.0))
            pool.inc(g.memset(smb[0:1, 8:8 + P], 1.0))
            pool.inc(g.memset(small[:, C_NSHIFT:C_NSHIFT + 1], -CSHIFT))
            pool.inc(g.memset(small[:, C_EPSLN:C_EPSLN + 1], 1e-5))
            pool.inc(g.memset(small[:, C_EPSRMS:C_EPSRMS + 1], 1e-6))
            pool.inc(g.memset(small[:, C_EPSRMS128:C_EPSRMS128 + 1], 128e-6))
            pool.inc(g.memset(small[:, C_LNHALF128:C_LNHALF128 + 1],
                              0.5 * math.log(128.0)))
            pool.inc(g.memset(ident[:], 0.0))
            pool.inc(g.memset(mask2[:, 0:P], 0.0))
            pool.inc(g.memset(mask2[:, P:2 * P], 0.0))
            g.wait_ge(pool.h, pool.n)
            pool.inc(g.affine_select(out=ident[:], in_=ident[:],
                                     compare_op=ALU.not_equal, fill=1.0,
                                     base=0, pattern=[[-1, P]], channel_multiplier=1))
            pool.inc(g.affine_select(out=mask2[:, 0:P], in_=mask2[:, 0:P],
                                     compare_op=ALU.is_ge, fill=-30000.0,
                                     base=-1, pattern=[[-1, P]], channel_multiplier=1))
            pool.inc(g.affine_select(out=mask2[:, P:2 * P], in_=mask2[:, P:2 * P],
                                     compare_op=ALU.is_ge, fill=-30000.0,
                                     base=0, pattern=[[1, P]], channel_multiplier=-1))
        POOL(f)

        def fs(s):
            s_bias.inc(s.dma_start(out=small[:, BQ:BQ + H], in_=bqT.ap()), 16)
            s_bias.inc(s.dma_start(out=small[:, BK:BK + 1], in_=bk_.ap()), 16)
            s_bias.inc(s.dma_start(out=small[:, BV:BV + 1], in_=bv_.ap()), 16)
            s_bias.inc(s.dma_start(out=small[:, BO:BO + DB], in_=boT.ap()), 16)
            s_bias.inc(s.dma_start(out=small[:, B0A:B0A + FB], in_=b0aT.ap()), 16)
            s_bias.inc(s.dma_start(out=small[:, B0B:B0B + DB], in_=b0bT.ap()), 16)
            s_bias.inc(s.dma_start(out=small[:, B1A:B1A + FB], in_=b1aT.ap()), 16)
            s_bias.inc(s.dma_start(out=small[:, B1B:B1B + DB], in_=b1bT.ap()), 16)
            s_bias.inc(s.dma_start(out=small[:, QNW:QNW + 1], in_=qnw_.ap()), 16)
            s_bias.inc(s.dma_start(out=small[:, KNW:KNW + 1], in_=knw_.ap()), 16)
            s_bias.inc(s.dma_start(out=small[:, HALOB:HALOB + 1], in_=halob_.ap()), 16)
            s_bias.inc(s.dma_start(out=rows[0:1, R_MISC:R_MISC + 1], in_=pos0_.ap()), 16)
        SP(fs)

        def fv(v):
            v.wait_ge(s_bias.h, s_bias.n)
            dve.inc(v.tensor_scalar_mul(small[:, BQ:BQ + H], small[:, BQ:BQ + H], scale_a))
            v.wait_ge(dve.h, dve.n)
            dve.inc(v.tensor_scalar_mul(small[:, QNW:QNW + 1], small[:, QNW:QNW + 1], scale_a))
            v.wait_ge(dve.h, dve.n)
            dve.inc(v.tensor_copy(small[:, HALOR:HALOR + 1], small[:, HALOB:HALOB + 1]))
            v.wait_ge(dve.h, dve.n)
            dve.inc(v.tensor_scalar_add(small[:, HALOB:HALOB + 1], small[:, HALOB:HALOB + 1], -CSHIFT))
        DVE(fv)

    def emit_ln_stats(src, col0, psA, psB, ps_sx, ps_sx2, eps_col, gate=False, ps_g=None,
                      kv=None, xb2_free=None):
        act0, pe0, dve0 = act.n, pe.n, dve.n
        guard = dve.n
        per_db = (3 if gate else 2) + (2 if kv else 0)
        xw = []

        def fs(g):
            for db in range(DB):
                if db >= 2:
                    g.wait_ge(dve.h, dve0 + db - 1)
                    g.wait_ge(pe.h, pe0 + per_db * (db - 1))
                elif xb2_free is not None:
                    if xb2_free['pe']:
                        g.wait_ge(pe.h, xb2_free['pe'])
                    if xb2_free['dve']:
                        g.wait_ge(dve.h, xb2_free['dve'])
                    if xb2_free['act']:
                        g.wait_ge(act.h, xb2_free['act'])
                else:
                    if act0:
                        g.wait_ge(act.h, act0)
                    if pe0:
                        g.wait_ge(pe.h, pe0)
                    if dve0:
                        g.wait_ge(dve.h, dve0)
                xw.append(s_xb[db % 2].inc(g.dma_start(
                    out=xb2[:, (db % 2) * QC:(db % 2) * QC + QC],
                    in_=src[db * P:(db + 1) * P, col0:col0 + QC]), 16))
        POOL(fs)

        def fa(v):
            for db in range(DB):
                v.wait_ge(s_xb[db % 2].h, xw[db])
                if db >= 2:
                    v.wait_ge(pe.h, pe0 + per_db * (db - 1))
                dve.inc(v.tensor_tensor(sqb[:, (db % 2) * QC:(db % 2) * QC + QC],
                                        xb2[:, (db % 2) * QC:(db % 2) * QC + QC],
                                        xb2[:, (db % 2) * QC:(db % 2) * QC + QC],
                                        ALU.mult))
        DVE(fa)

        def fp(t):
            for db in range(DB):
                sl = (db % 2) * QC
                t.wait_ge(s_xb[db % 2].h, xw[db])
                if db == 0 and xb2_free is not None and guard:
                    t.wait_ge(dve.h, guard)
                if kv and db == 0:
                    t.wait_ge(s_wkv.h, kv[2])
                pe.inc(t.matmul(ps_sx[0:1, 0:QC], ones_bf, xb2[:, sl:sl + QC],
                                start=(db == 0), stop=(db == DB - 1)))
                if gate:
                    pe.inc(t.matmul(ps_g[0:1, 0:QC], smb[:, WGDB + db:WGDB + db + 1],
                                    xb2[:, sl:sl + QC],
                                    start=(db == 0), stop=(db == DB - 1)))
                if kv:
                    pe.inc(t.matmul(kv[0][:, 0:QC], wbuf[:, db * P:(db + 1) * P],
                                    xb2[:, sl:sl + QC],
                                    start=(db == 0), stop=(db == DB - 1)))
                    pe.inc(t.matmul(kv[1][:, 0:QC],
                                    wbuf[:, (DB + db) * P:(DB + db + 1) * P],
                                    xb2[:, sl:sl + QC],
                                    start=(db == 0), stop=(db == DB - 1)))
                t.wait_ge(dve.h, dve0 + db + 1)
                pe.inc(t.matmul(ps_sx2[0:1, 0:QC], ones_bf, sqb[:, sl:sl + QC],
                                start=(db == 0), stop=(db == DB - 1)))
        PE(fp)

        def fv(v):
            v.wait_ge(pe.h, pe.n)
            dve.inc(v.tensor_scalar_mul(rows[0:1, R_MEAN:R_MEAN + QC],
                                        ps_sx[0:1, 0:QC], 1.0 / D))
            if gate:
                v.wait_ge(dve.h, dve.n)
                dve.inc(v.tensor_copy(rows[0:1, R_T1:R_T1 + QC], ps_g[0:1, 0:QC]))
            v.wait_ge(dve.h, dve.n)
            dve.inc(v.tensor_scalar_mul(rows[0:1, R_T0:R_T0 + QC],
                                        ps_sx2[0:1, 0:QC], 1.0 / D))
            v.wait_ge(dve.h, dve.n)
            dve.inc(v.tensor_tensor(rows[0:1, R_RSTD:R_RSTD + QC],
                                    rows[0:1, R_MEAN:R_MEAN + QC],
                                    rows[0:1, R_MEAN:R_MEAN + QC], ALU.mult))
            v.wait_ge(dve.h, dve.n)
            dve.inc(v.scalar_tensor_tensor(rows[0:1, R_T0:R_T0 + QC],
                                           rows[0:1, R_T0:R_T0 + QC],
                                           small[0:1, eps_col:eps_col + 1],
                                           rows[0:1, R_RSTD:R_RSTD + QC],
                                           ALU.add, ALU.subtract))
        DVE(fv)

        def fa2(s):
            s.wait_ge(dve.h, dve.n)
            act.inc(s.activation(rows[0:1, R_T0:R_T0 + QC],
                                 rows[0:1, R_T0:R_T0 + QC], AF.Ln))
            s.wait_ge(act.h, act.n)
            act.inc(s.activation(rows[0:1, R_RSTD:R_RSTD + QC],
                                 rows[0:1, R_T0:R_T0 + QC], AF.Exp, scale=-0.5))
            s.wait_ge(act.h, act.n)
            act.inc(s.activation(rowsb[0:1, RB_RSTD:RB_RSTD + QC],
                                 rows[0:1, R_T0:R_T0 + QC], AF.Exp, scale=-0.5))
        ACT(fa2)

        def fv2(v):
            v.wait_ge(act.h, act.n)
            dve.inc(v.tensor_tensor(rowsb[0:1, RB_MB:RB_MB + QC],
                                    rows[0:1, R_MEAN:R_MEAN + QC],
                                    rows[0:1, R_RSTD:R_RSTD + QC], ALU.mult))
        DVE(fv2)

        def fp2(t):
            t.wait_ge(dve.h, dve.n)
            pe.inc(t.matmul(psA[:, 0:QC], ones_row_bf,
                            rowsb[0:1, RB_RSTD:RB_RSTD + QC],
                            start=True, stop=True))
            pe.inc(t.matmul(psB[:, 0:QC], ones_row_bf,
                            rowsb[0:1, RB_MB:RB_MB + QC],
                            start=True, stop=True))
        PE(fp2)
        return {'pe': pe0 + per_db * DB, 'dve': dve0 + DB, 'act': act.n}

    def emit_ln_apply(src, col0, psA, psB, dst, span, dcol,
                      xt2_free=None, lnxt_free=None):
        dve0, pe0a, act0a = dve.n, pe.n, act.n
        xw = []

        def fs(s):
            for db in range(DB):
                if db >= 2:
                    s.wait_ge(dve.h, dve0 + 2 * (db - 1))
                elif xt2_free is not None:
                    if xt2_free:
                        s.wait_ge(dve.h, xt2_free)
                    if lnxt_free:
                        s.wait_ge(pe.h, lnxt_free)
                else:
                    if pe0a:
                        s.wait_ge(pe.h, pe0a)
                    if act0a:
                        s.wait_ge(act.h, act0a)
                xw.append(s_x[db % 2].inc(s.dma_start(
                    out=xt2[:, (db % 2) * QC:(db % 2) * QC + QC],
                    in_=src[db * P:(db + 1) * P, col0:col0 + QC]), 16))
        SP(fs)

        def fv(v):
            if pe0a:
                v.wait_ge(pe.h, pe0a)
            for db in range(DB):
                sl = (db % 2) * QC
                v.wait_ge(s_x[db % 2].h, xw[db])
                dve.inc(v.tensor_tensor(sq2[:, sl:sl + QC], xt2[:, sl:sl + QC],
                                        psA[:, 0:QC], ALU.mult))
                v.wait_ge(dve.h, dve.n)
                dve.inc(v.tensor_tensor(dst[:, db * span + dcol:db * span + dcol + QC],
                                        sq2[:, sl:sl + QC], psB[:, 0:QC], ALU.subtract))
        DVE(fv)

    KOFF, VOFF = 0, S

    def phase_kv():
        def fs(g):
            s_wkv.inc(g.dma_start(
                out=wbuf[:, 0:DB * P].rearrange("p (db c) -> p db c", db=DB),
                in_=Wk.ap().rearrange("(db p) c -> p db c", p=P)), 16)
            s_wkv.inc(g.dma_start(
                out=wbuf[:, DB * P:2 * DB * P].rearrange("p (db c) -> p db c", db=DB),
                in_=Wv.ap().rearrange("(db p) c -> p db c", p=P)), 16)
        POOL(fs)

        with nc.psum_tensor([P, 2], F32) as psw:
            def fp(t):
                t.wait_ge(s_wkv.h, s_wkv.n)
                for db in range(DB):
                    pe.inc(t.matmul(psw[:, 0:1], wbuf[:, db * P:(db + 1) * P],
                                    ones_bf, start=(db == 0), stop=(db == DB - 1)))
                for db in range(DB):
                    pe.inc(t.matmul(psw[:, 1:2], wbuf[:, (DB + db) * P:(DB + db + 1) * P],
                                    ones_bf, start=(db == 0), stop=(db == DB - 1)))
            PE(fp)

            def fv(v):
                v.wait_ge(pe.h, pe.n)
                dve.inc(v.tensor_copy(small[:, WBARK:WBARK + 1], psw[:, 0:1]))
                v.wait_ge(dve.h, dve.n)
                dve.inc(v.tensor_copy(small[:, WBARV:WBARV + 1], psw[:, 1:2]))
            DVE(fv)

        kvm = None
        for c in range(SC):
            col0 = c * QC
            with nc.psum_tensor([P, QC], F32) as psA, \
                 nc.psum_tensor([P, QC], F32) as psB, \
                 nc.psum_tensor([1, QC], F32) as ps_sx, \
                 nc.psum_tensor([1, QC], F32) as ps_sx2, \
                 nc.psum_tensor([P, QC], F32) as psK, \
                 nc.psum_tensor([P, QC], F32) as psV:
                kvm = emit_ln_stats(xTb.ap(), col0, psA, psB, ps_sx, ps_sx2,
                                    C_EPSLN, kv=(psK, psV, s_wkv.n),
                                    xb2_free=kvm)

                def fv2(v, col0=col0):
                    v.wait_ge(pe.h, pe.n)
                    dve.inc(v.tensor_copy(xt2[:, 0:QC], psA[:, 0:QC]))
                    v.wait_ge(dve.h, dve.n)
                    dve.inc(v.tensor_scalar(sq2[:, 0:QC], psB[:, 0:QC],
                                            small[:, WBARK:WBARK + 1], None, ALU.mult))
                    v.wait_ge(dve.h, dve.n)
                    dve.inc(v.tensor_scalar(sq2[:, QC:2 * QC], psB[:, 0:QC],
                                            small[:, WBARV:WBARV + 1], None, ALU.mult))
                    v.wait_ge(dve.h, dve.n)
                    dve.inc(v.tensor_tensor(qsc[:, 0:QC], psK[:, 0:QC],
                                            xt2[:, 0:QC], ALU.mult))
                    v.wait_ge(dve.h, dve.n)
                    dve.inc(v.tensor_tensor(qsc[:, QC:2 * QC], psV[:, 0:QC],
                                            xt2[:, 0:QC], ALU.mult))
                    v.wait_ge(dve.h, dve.n)
                    dve.inc(v.scalar_tensor_tensor(
                        kvb[:, KOFF + col0:KOFF + col0 + QC],
                        qsc[:, 0:QC], small[:, BK:BK + 1], sq2[:, 0:QC],
                        ALU.add, ALU.subtract))
                    v.wait_ge(dve.h, dve.n)
                    dve.inc(v.scalar_tensor_tensor(
                        ptil[:, 0:QC],
                        qsc[:, QC:2 * QC], small[:, BV:BV + 1], sq2[:, QC:2 * QC],
                        ALU.add, ALU.subtract))
                DVE(fv2)

                nblk = QC // P
                with nc.psum_tensor([P, P], BF16) as pst:
                    pe2 = pe.n
                    dve2 = dve.n

                    def ft(t, col0=col0):
                        for i in range(nblk):
                            t.wait_ge(dve.h, dve2 + i)
                            pe.inc(t.transpose(pst[:], ptil[:, i * P:(i + 1) * P], ident[:]))
                    PE(ft)

                    def fv3(v, col0=col0):
                        for i in range(nblk):
                            v.wait_ge(pe.h, pe2 + i + 1)
                            dve.inc(v.tensor_copy(
                                kvb[:, VOFF + col0 + i * P:VOFF + col0 + (i + 1) * P],
                                pst[:]))
                    DVE(fv3)

    def emit_oproj(wdram, wreg, wreg_off, av, av_span, av_off, tok0, ntok,
                   bias_off, src, dst, dst_off, wpre_mark=None):
        DQ = max(1, DB // 4)
        nq = (DB + DQ - 1) // DQ
        nch = ntok // QC
        ncols = DQ * P
        wn = [0, 0]
        pe_done = {}
        dve_done = {}
        tiles = [(q, cc, db) for q in range(nq) for cc in range(nch)
                 for db in range(q * DQ, min(DB, (q + 1) * DQ))]
        qtiles = {}
        for i, (q, cc, db) in enumerate(tiles):
            qtiles.setdefault(q, []).append(i)

        pending_w = [None]

        def flush_w():
            if pending_w[0] is not None:
                SP(pending_w[0])
                pending_w[0] = None

        with nc.psum_tensor([P, QC], F32) as po0, \
             nc.psum_tensor([P, QC], F32) as po1:
            pos = [po0, po1]
            for q in range(nq):
                def fg(g, q=q,
                       pm=(wpre_mark if wpre_mark is not None else pe.n)):
                    if q >= 2:
                        g.wait_ge(pe.h, pe_done[qtiles[q - 2][-1]])
                    elif pm:
                        g.wait_ge(pe.h, pm)
                    s_wo[q % 2].inc(g.dma_start(
                        out=wreg[:, wreg_off + (q % 2) * H * ncols:
                                 wreg_off + (q % 2) * H * ncols + H * ncols]
                            .rearrange("p (hb c) -> p hb c", hb=H),
                        in_=wdram.rearrange("(hb p) c -> p hb c", p=P)
                            [:, :, q * ncols:q * ncols + min(ncols, D - q * ncols)]), 16)
                    wn[q % 2] = s_wo[q % 2].n
                POOL(fg)

                for i in qtiles[q]:
                    _, cc, db = tiles[i]
                    col0 = tok0 + cc * QC
                    sl = i % 2
                    woff2 = wreg_off + (q % 2) * H * ncols

                    def fp(t, q=q, cc=cc, db=db, sl=sl, woff2=woff2, i=i,
                           wmark=wn[q % 2], dm0=dve.n):
                        t.wait_ge(s_wo[q % 2].h, wmark)
                        if i >= 2:
                            t.wait_ge(dve.h, dve_done[i - 2])
                        elif dm0:
                            t.wait_ge(dve.h, dm0)
                        for hb in range(H):
                            pe.inc(t.matmul(
                                pos[sl][:, 0:QC],
                                wreg[:, woff2 + hb * ncols + (db - q * DQ) * P:
                                     woff2 + hb * ncols + (db - q * DQ) * P + P],
                                av[:, hb * av_span + av_off + cc * QC:
                                   hb * av_span + av_off + (cc + 1) * QC],
                                start=(hb == 0), stop=(hb == H - 1)))
                        pe_done[i] = pe.n
                    PE(fp)

                    def fs(g, db=db, col0=col0, sl=sl,
                           dm=dve_done.get(i - 2, dve.n)):
                        if dm:
                            g.wait_ge(dve.h, dm)
                        s_x[sl].inc(g.dma_start(
                            out=xt2[:, sl * QC:sl * QC + QC],
                            in_=src[db * P:(db + 1) * P, col0:col0 + QC]), 16)
                    SP(fs)
                    flush_w()
                    xn = s_x[sl].n

                    def fv(v, db=db, sl=sl, i=i):
                        v.wait_ge(pe.h, pe_done[i])
                        v.wait_ge(s_x[sl].h, xn)
                        v.wait_ge(s_resw[sl].h, s_resw[sl].n)
                        if bias_off is not None:
                            dve.inc(v.scalar_tensor_tensor(
                                rest[:, sl * QC:sl * QC + QC], pos[sl][:, 0:QC],
                                small[:, bias_off + db:bias_off + db + 1],
                                xt2[:, sl * QC:sl * QC + QC], ALU.add, ALU.add))
                        else:
                            dve.inc(v.tensor_tensor(
                                rest[:, sl * QC:sl * QC + QC], pos[sl][:, 0:QC],
                                xt2[:, sl * QC:sl * QC + QC], ALU.add))
                        dve_done[i] = dve.n
                    DVE(fv)

                    def fs2(g, db=db, col0=col0, sl=sl, i=i,
                            dd=None):
                        g.wait_ge(dve.h, dve_done[i])
                        s_resw[sl].inc(g.dma_start(
                            out=dst[db * P:(db + 1) * P,
                                    col0 - dst_off:col0 - dst_off + QC],
                            in_=rest[:, sl * QC:sl * QC + QC]), 16)
                    pending_w[0] = fs2
        flush_w()

    def phase_attn_a_slab(slab):
        t0 = slab * ACH
        nch = ACH // QC
        for cc in range(nch):
            col0 = t0 + cc * QC
            with nc.psum_tensor([P, QC], F32) as psA, \
                 nc.psum_tensor([P, QC], F32) as psB, \
                 nc.psum_tensor([1, QC], F32) as ps_sx, \
                 nc.psum_tensor([1, QC], F32) as ps_sx2:
                emit_ln_stats(xTe.ap(), col0, psA, psB, ps_sx, ps_sx2, C_EPSLN)
                emit_ln_apply(xTe.ap(), col0, psA, psB, lnxt, ACH, cc * QC)

        for h in range(H):
            wslot = (h % 2) * DB * P
            pe_before = pe.n

            def fg(g, h=h, wslot=wslot):
                if h >= 2:
                    g.wait_ge(pe.h, pe_before)
                s_w[h % 2].inc(g.dma_start(
                    out=wbuf[:, wslot:wslot + DB * P].rearrange("p (db c) -> p db c", db=DB),
                    in_=Wq.ap().rearrange("(db p) c -> p db c", p=P)[:, :, h * P:(h + 1) * P]), 16)
            POOL(fg)

            with nc.psum_tensor([P, QC], F32) as psq0, \
                 nc.psum_tensor([P, QC], F32) as psq1:
                psqs = [psq0, psq1]
                wsem, wn = s_w[h % 2], s_w[h % 2].n
                pe1 = pe.n
                dve1 = dve.n

                def fp(t, h=h, wslot=wslot):
                    t.wait_ge(wsem.h, wn)
                    for cc in range(nch):
                        if cc >= 2:
                            t.wait_ge(dve.h, dve1 + cc - 1)
                        elif dve1:
                            t.wait_ge(dve.h, dve1)
                        for db in range(DB):
                            pe.inc(t.matmul(
                                psqs[cc % 2][:, 0:QC],
                                wbuf[:, wslot + db * P:wslot + (db + 1) * P],
                                lnxt[:, db * ACH + cc * QC:db * ACH + cc * QC + QC],
                                start=(db == 0), stop=(db == DB - 1)))
                PE(fp)

                def fa(v, h=h):
                    for cc in range(nch):
                        v.wait_ge(pe.h, pe1 + DB * (cc + 1))
                        dve.inc(v.tensor_scalar(
                            bigB[:, h * ACH + cc * QC:h * ACH + cc * QC + QC],
                            psqs[cc % 2][:, 0:QC], scale_a,
                            small[:, BQ + h:BQ + h + 1], ALU.mult, ALU.add))
                DVE(fa)

        preflash_pe = pe.n
        FQ = QC
        for pair in range(H // 2):
            hA, hB = 2 * pair, 2 * pair + 1
            for qi in range(ACH // FQ):
                qoffA = hA * ACH + qi * FQ
                qoffB = hB * ACH + qi * FQ
                with nc.psum_tensor([P, FQ], F32) as psOA, \
                     nc.psum_tensor([P, FQ], F32) as psOB, \
                     nc.psum_tensor([1, FQ], F32) as psDA, \
                     nc.psum_tensor([1, FQ], F32) as psDB, \
                     nc.psum_tensor([P, FQ], F32) as psSA, \
                     nc.psum_tensor([P, FQ], F32) as psSB:
                    pe0, act0, dve0 = pe.n, act.n, dve.n

                    def fp(t):
                        if dve0:
                            t.wait_ge(dve.h, dve0)
                        for kb in range(KB):
                            if kb >= 1:
                                t.wait_ge(act.h, act0 + 2 * (kb - 1) + 1)
                            pe.inc(t.matmul(psSA[:, 0:FQ],
                                            kvb[:, KOFF + kb * P:KOFF + (kb + 1) * P],
                                            bigB[:, qoffA:qoffA + FQ],
                                            start=True, stop=True))
                            if kb >= 1:
                                t.wait_ge(act.h, act0 + 2 * (kb - 1) + 2)
                            pe.inc(t.matmul(psSB[:, 0:FQ],
                                            kvb[:, KOFF + kb * P:KOFF + (kb + 1) * P],
                                            bigB[:, qoffB:qoffB + FQ],
                                            start=True, stop=True))
                            if kb >= 1:
                                for (pD, pO, sl) in ((psDA, psOA, ((kb - 1) % 2)),
                                                     (psDB, psOB, 2 + ((kb - 1) % 2))):
                                    pe.inc(t.matmul(pO[:, 0:FQ],
                                                    kvb[:, VOFF + (kb - 1) * P:VOFF + kb * P],
                                                    ptil[:, sl * FQ:(sl + 1) * FQ],
                                                    start=(kb == 1), stop=False))
                                    pe.inc(t.matmul(pD[0:1, 0:FQ], ones_bf,
                                                    ptil[:, sl * FQ:(sl + 1) * FQ],
                                                    start=(kb == 1), stop=False))
                        t.wait_ge(act.h, act0 + 2 * KB)
                        for (pD, pO, sl) in ((psDA, psOA, ((KB - 1) % 2)),
                                             (psDB, psOB, 2 + ((KB - 1) % 2))):
                            pe.inc(t.matmul(pO[:, 0:FQ],
                                            kvb[:, VOFF + (KB - 1) * P:VOFF + KB * P],
                                            ptil[:, sl * FQ:(sl + 1) * FQ],
                                            start=False, stop=True))
                            pe.inc(t.matmul(pD[0:1, 0:FQ], ones_bf,
                                            ptil[:, sl * FQ:(sl + 1) * FQ],
                                            start=False, stop=True))
                    PE(fp)
                    pe_all = pe.n

                    def fa(s):
                        act_set_pin(s)
                        for kb in range(KB):
                            s.wait_ge(pe.h, pe0 + (1 if kb == 0 else 6 * kb - 3))
                            act.inc(s.activation(
                                ptil[:, (kb % 2) * FQ:(kb % 2) * FQ + FQ],
                                psSA[:, 0:FQ], AF.Exp,
                                bias=small[:, C_NSHIFT:C_NSHIFT + 1]))
                            s.wait_ge(pe.h, pe0 + (2 if kb == 0 else 6 * kb - 2))
                            act.inc(s.activation(
                                ptil[:, (2 + kb % 2) * FQ:(2 + kb % 2) * FQ + FQ],
                                psSB[:, 0:FQ], AF.Exp,
                                bias=small[:, C_NSHIFT:C_NSHIFT + 1]))
                    ACT(fa)

                    with nc.psum_tensor([P, FQ], F32) as psR:
                        for (pD, pO, qoff, rbo) in ((psDA, psOA, qoffA, 0),
                                                    (psDB, psOB, qoffB, FQ)):
                            def ft(s, pD=pD, rbo=rbo):
                                s.wait_ge(pe.h, pe_all)
                                act.inc(s.activation(rows[0:1, R_DEN + rbo:R_DEN + rbo + FQ],
                                                     pD[0:1, 0:FQ], AF.Ln))
                                s.wait_ge(act.h, act.n)
                                act.inc(s.activation(rowsb[0:1, RB_DEN + rbo:RB_DEN + rbo + FQ],
                                                     rows[0:1, R_DEN + rbo:R_DEN + rbo + FQ],
                                                     AF.Exp, scale=-1.0))
                            ACT(ft)

                            def fpr(t, rbo=rbo):
                                t.wait_ge(act.h, act.n)
                                if dve.n:
                                    t.wait_ge(dve.h, dve.n)
                                pe.inc(t.matmul(psR[:, 0:FQ], ones_row_bf,
                                                rowsb[0:1, RB_DEN + rbo:RB_DEN + rbo + FQ],
                                                start=True, stop=True))
                            PE(fpr)

                            def fvt(v, pO=pO, qoff=qoff, rbo=rbo):
                                v.wait_ge(pe.h, pe.n)
                                dve.inc(v.tensor_copy(rtil[:, rbo:rbo + FQ], psR[:, 0:FQ]))
                                v.wait_ge(dve.h, dve.n)
                                dve.inc(v.tensor_tensor(bigB[:, qoff:qoff + FQ],
                                                        pO[:, 0:FQ], rtil[:, rbo:rbo + FQ],
                                                        ALU.mult))
                            DVE(fvt)

        emit_oproj(Wo.ap(), bigB, H * ACH, bigB, ACH, 0, t0, ACH, BO,
                   xTe.ap(), res.ap(), 0, wpre_mark=preflash_pe)
        join_light()

    W0B_CH = P
    OUT0_OFF = 0
    WB_OFF = DB * FTC

    def phase_ffn_gate_setup():
        def fs(g):
            s_wkv.inc(g.dma_start(
                out=wbuf[:, 0:DB * 2].rearrange("p (db e) -> p db e", e=2),
                in_=WgT.ap().rearrange("(db p) e -> p db e", p=P)), 16)
        POOL(fs)

        def fv(v):
            v.wait_ge(s_wkv.h, s_wkv.n)
            for db in range(DB):
                dve.inc(v.tensor_tensor(
                    smb[:, WGDB + db:WGDB + db + 1],
                    wbuf[:, 2 * db:2 * db + 1],
                    wbuf[:, 2 * db + 1:2 * db + 2],
                    ALU.subtract))
        DVE(fv)

        with nc.psum_tensor([1, 1], F32) as psb:
            def fp(t):
                t.wait_ge(dve.h, dve.n)
                for db in range(DB):
                    pe.inc(t.matmul(psb[0:1, 0:1], smb[:, WGDB + db:WGDB + db + 1],
                                    ones_bf, start=(db == 0), stop=(db == DB - 1)))
            PE(fp)

            def fv2(v):
                v.wait_ge(pe.h, pe.n)
                dve.inc(v.tensor_copy(rows[0:1, R_MISC + 4:R_MISC + 5], psb[0:1, 0:1]))
            DVE(fv2)

    def phase_ffn(src, dst, tok0, ntok, dst_off):
        NCH = ntok // FTC
        NTC = FTC // QC
        wb_free = [pe.n]
        fm = None
        xtf = [None]
        lnf = [None]
        for ch in range(NCH):
            base = tok0 + ch * FTC
            for tc in range(NTC):
                col0 = base + tc * QC
                with nc.psum_tensor([P, QC], F32) as psA, \
                     nc.psum_tensor([P, QC], F32) as psB, \
                     nc.psum_tensor([1, QC], F32) as ps_sx, \
                     nc.psum_tensor([1, QC], F32) as ps_sx2, \
                     nc.psum_tensor([1, QC], F32) as ps_g:
                    fm = emit_ln_stats(src, col0, psA, psB, ps_sx, ps_sx2,
                                       C_EPSLN, gate=True, ps_g=ps_g,
                                       xb2_free=fm)
                    def fvg(v, tc=tc):
                        v.wait_ge(pe.h, pe.n)
                        v.wait_ge(dve.h, dve.n)
                        dve.inc(v.tensor_single_scalar(
                            rows[0:1, R_T0:R_T0 + QC], rows[0:1, R_MEAN:R_MEAN + QC],
                            rows[0:1, R_MISC + 4:R_MISC + 5], ALU.mult))
                        v.wait_ge(dve.h, dve.n)
                        dve.inc(v.tensor_tensor(rows[0:1, R_T1:R_T1 + QC],
                                                rows[0:1, R_T1:R_T1 + QC],
                                                rows[0:1, R_T0:R_T0 + QC],
                                                ALU.subtract))
                        v.wait_ge(dve.h, dve.n)
                        dve.inc(v.tensor_single_scalar(
                            rowsb[0:1, RB_MASK:RB_MASK + QC], rows[0:1, R_T1:R_T1 + QC],
                            0.0, ALU.is_ge))
                    DVE(fvg)

                    with nc.psum_tensor([P, QC], F32) as psM:
                        def fpm(t, tc=tc):
                            t.wait_ge(dve.h, dve.n)
                            pe.inc(t.matmul(psM[:, 0:QC], ones_row_bf,
                                            rowsb[0:1, RB_MASK:RB_MASK + QC],
                                            start=True, stop=True))
                        PE(fpm)

                        def fvm(v, tc=tc):
                            v.wait_ge(pe.h, pe.n)
                            dve.inc(v.tensor_copy(mtil[:, tc * QC:(tc + 1) * QC],
                                                  psM[:, 0:QC]))
                        DVE(fvm)

                    emit_ln_apply(src, col0, psA, psB, lnxt, FTC, tc * QC,
                                  xt2_free=xtf[0], lnxt_free=lnf[0])
                    xtf[0] = dve.n

            for e in range(2):
                WaT = W0aT if e == 0 else W1aT
                WbT = W0bT if e == 0 else W1bT
                BA = B0A if e == 0 else B1A
                BB = B0B if e == 0 else B1B
                with nc.psum_tensor([P, QC], F32) as ph00, \
                     nc.psum_tensor([P, QC], F32) as ph01, \
                     nc.psum_tensor([P, QC], F32) as ph10, \
                     nc.psum_tensor([P, QC], F32) as ph11:
                    psH = [[ph00, ph01], [ph10, ph11]]
                    pe0, act0 = pe.n, act.n

                    def fgw(g, e=e, WaT=WaT):
                        for fb in range(FB):
                            if fb >= 2:
                                g.wait_ge(pe.h, pe0 + DB * NTC * (fb - 1))
                            else:
                                g.wait_ge(pe.h, pe0)
                            s_w[fb % 2].inc(g.dma_start(
                                out=wbuf[:, (fb % 2) * DB * P:(fb % 2 + 1) * DB * P]
                                    .rearrange("p (db c) -> p db c", db=DB),
                                in_=WaT.ap().rearrange("(db p) c -> p db c", p=P)
                                    [:, :, fb * P:(fb + 1) * P]), 16)
                    POOL(fgw)

                    wn0 = [s_w[0].n - (FB + 1) // 2 * 16, s_w[1].n - FB // 2 * 16]

                    def fp1(t, e=e, dm=dve.n):
                        if dm:
                            t.wait_ge(dve.h, dm)
                        for fb in range(FB):
                            t.wait_ge(s_w[fb % 2].h, wn0[fb % 2] + (fb // 2 + 1) * 16)
                            if fb >= 2:
                                t.wait_ge(act.h, act0 + NTC * (fb - 1))
                            elif act0:
                                t.wait_ge(act.h, act0)
                            for db in range(DB):
                                for tc in range(NTC):
                                    pe.inc(t.matmul(
                                        psH[fb % 2][tc][:, 0:QC],
                                        wbuf[:, (fb % 2) * DB * P + db * P:
                                             (fb % 2) * DB * P + (db + 1) * P],
                                        lnxt[:, db * FTC + tc * QC:db * FTC + (tc + 1) * QC],
                                        start=(db == 0), stop=(db == DB - 1)))
                    PE(fp1)

                    AFUN = AF.Gelu if e == 0 else AF.Relu

                    def fa1(s, e=e, BA=BA, AFUN=AFUN):
                        for fb in range(FB):
                            s.wait_ge(pe.h, pe0 + DB * NTC * fb + DB * NTC)
                            for tc in range(NTC):
                                act.inc(s.activation(
                                    bigB[:, fb * FTC + tc * QC:fb * FTC + (tc + 1) * QC],
                                    psH[fb % 2][tc][:, 0:QC], AFUN,
                                    bias=small[:, BA + fb:BA + fb + 1]))
                    ACT(fa1)
                    if e == 1:
                        lnf[0] = pe.n

                ndq = D // W0B_CH
                wn2 = [0, 0]
                pe2_done = {}
                dve2_done = {}
                pend2 = [None]

                def flush2():
                    if pend2[0] is not None:
                        SP(pend2[0])
                        pend2[0] = None

                with nc.psum_tensor([P, QC], F32) as q20, \
                     nc.psum_tensor([P, QC], F32) as q21:
                    pq2 = [q20, q21]
                    mm2_pre = wb_free[0]
                    for dq in range(ndq):
                        def fgw2(g, e=e, WbT=WbT, dq=dq, pm=mm2_pre):
                            if dq >= 2:
                                g.wait_ge(pe.h, pe2_done[(dq - 2, NTC - 1)])
                            elif pm:
                                g.wait_ge(pe.h, pm)
                            s_wo[dq % 2].inc(g.dma_start(
                                out=kvb[:, WB_OFF + (dq % 2) * FB * W0B_CH:
                                     WB_OFF + (dq % 2 + 1) * FB * W0B_CH]
                                    .rearrange("p (fb c) -> p fb c", fb=FB),
                                in_=WbT.ap().rearrange("(fb p) c -> p fb c", p=P)
                                    [:, :, dq * W0B_CH:(dq + 1) * W0B_CH]), 16)
                            wn2[dq % 2] = s_wo[dq % 2].n
                        POOL(fgw2)

                        db = dq
                        woff2 = WB_OFF + (dq % 2) * FB * W0B_CH
                        for tc in range(NTC):
                            col0 = base + tc * QC
                            i2 = dq * NTC + tc
                            sl = i2 % 2

                            def fp2(t, db=db, tc=tc, sl=sl, i2=i2, dq=dq,
                                    woff2=woff2, am=act.n, wmark=wn2[dq % 2]):
                                t.wait_ge(s_wo[dq % 2].h, wmark)
                                t.wait_ge(act.h, am)
                                if i2 >= 2:
                                    t.wait_ge(dve.h, dve2_done[i2 - 2])
                                elif dve.n:
                                    t.wait_ge(dve.h, dve.n)
                                for fb in range(FB):
                                    pe.inc(t.matmul(
                                        pq2[sl][:, 0:QC],
                                        kvb[:, woff2 + fb * W0B_CH:
                                             woff2 + fb * W0B_CH + P],
                                        bigB[:, fb * FTC + tc * QC:fb * FTC + (tc + 1) * QC],
                                        start=(fb == 0), stop=(fb == FB - 1)))
                                pe2_done[(dq, tc)] = pe.n
                            PE(fp2)
                            pe2_done_i = None

                            if e == 0:
                                def fv20(v, db=db, tc=tc, BB=BB, sl=sl, i2=i2, dq=dq):
                                    v.wait_ge(pe.h, pe2_done[(dq, tc)])
                                    dve.inc(v.tensor_scalar(
                                        kvb[:, OUT0_OFF + db * FTC + tc * QC:
                                             OUT0_OFF + db * FTC + (tc + 1) * QC],
                                        pq2[sl][:, 0:QC],
                                        small[:, BB + db:BB + db + 1], None,
                                        ALU.add))
                                    dve2_done[i2] = dve.n
                                DVE(fv20)
                            else:
                                def fsx(g, db=db, col0=col0, sl=sl,
                                        dm=dve2_done.get(i2 - 2, dve.n)):
                                    if dm:
                                        g.wait_ge(dve.h, dm)
                                    s_x[sl].inc(g.dma_start(
                                        out=xt2[:, sl * QC:sl * QC + QC],
                                        in_=src[db * P:(db + 1) * P, col0:col0 + QC]), 16)
                                SP(fsx)
                                flush2()
                                xn = s_x[sl].n

                                def fv21(v, db=db, tc=tc, BB=BB, sl=sl, i2=i2,
                                         dq=dq, xn=xn):
                                    v.wait_ge(pe.h, pe2_done[(dq, tc)])
                                    v.wait_ge(s_x[sl].h, xn)
                                    v.wait_ge(s_resw[sl].h, s_resw[sl].n)
                                    dve.inc(v.scalar_tensor_tensor(
                                        sq2[:, sl * QC:sl * QC + QC], pq2[sl][:, 0:QC],
                                        small[:, BB + db:BB + db + 1],
                                        kvb[:, OUT0_OFF + db * FTC + tc * QC:
                                             OUT0_OFF + db * FTC + (tc + 1) * QC],
                                        ALU.add, ALU.subtract))
                                    v.wait_ge(dve.h, dve.n)
                                    dve.inc(v.tensor_tensor(
                                        qsc[:, sl * QC:sl * QC + QC],
                                        sq2[:, sl * QC:sl * QC + QC],
                                        mtil[:, tc * QC:(tc + 1) * QC], ALU.mult))
                                    v.wait_ge(dve.h, dve.n)
                                    dve.inc(v.scalar_tensor_tensor(
                                        sq2[:, sl * QC:sl * QC + QC], pq2[sl][:, 0:QC],
                                        small[:, BB + db:BB + db + 1],
                                        qsc[:, sl * QC:sl * QC + QC],
                                        ALU.add, ALU.subtract))
                                    v.wait_ge(dve.h, dve.n)
                                    dve.inc(v.tensor_tensor(
                                        rest[:, sl * QC:sl * QC + QC],
                                        sq2[:, sl * QC:sl * QC + QC],
                                        xt2[:, sl * QC:sl * QC + QC], ALU.add))
                                    dve2_done[i2] = dve.n
                                DVE(fv21)

                                def fso(g, db=db, col0=col0, dst=dst, sl=sl, i2=i2,
                                        dst_off=dst_off):
                                    g.wait_ge(dve.h, dve2_done[i2])
                                    s_resw[sl].inc(g.dma_start(
                                        out=dst[db * P:(db + 1) * P,
                                                col0 - dst_off:col0 - dst_off + QC],
                                        in_=rest[:, sl * QC:sl * QC + QC]), 16)
                                pend2[0] = fso

                    flush2()
                    wb_free[0] = pe.n
                    if e == 1:
                        xtf[0] = dve.n

    EB = EXT // P
    NQB2 = NT // 1024 if NT % 1024 == 0 else NT // min(NT, 1024)
    FQ2 = min(1024, NT)
    VB_OFF = KVB * EXT

    def phase_ropetab():
        def fg(g):
            pool.inc(g.memset(sq2[:, 0:P], 0.0))
            g.wait_ge(pool.h, pool.n)
            pool.inc(g.affine_select(out=sq2[:, 0:P], in_=sq2[:, 0:P],
                                     compare_op=ALU.not_equal, fill=1.0,
                                     base=-(P // 2), pattern=[[1, P]],
                                     channel_multiplier=-1))
            g.wait_ge(pool.h, pool.n)
            pool.inc(g.affine_select(out=sq2[:, 0:P], in_=sq2[:, 0:P],
                                     compare_op=ALU.not_equal, fill=-1.0,
                                     base=(P // 2), pattern=[[1, P]],
                                     channel_multiplier=-1))
            g.wait_ge(pool.h, pool.n)
            s_wkv.inc(g.dma_start(out=ropet[:, 0:2 * EXT], in_=ropeT_.ap()), 16)
        POOL(fg)

        def fv0(v):
            v.wait_ge(pool.h, pool.n)
            dve.inc(v.tensor_copy(rtc[:, 0:P], sq2[:, 0:P]))
            v.wait_ge(s_wkv.h, s_wkv.n)
        DVE(fv0)

    def phase_attn_b_proj():
        NCH = EXT // FTC
        NTC = FTC // QC
        qsc4 = qsc

        with nc.psum_tensor([P, QC], F32) as pq0, \
             nc.psum_tensor([P, QC], F32) as pq1, \
             nc.psum_tensor([1, QC], F32) as pn0, \
             nc.psum_tensor([1, QC], F32) as pn1, \
             nc.psum_tensor([P, QC], F32) as pR0, \
             nc.psum_tensor([P, QC], F32) as pR1, \
             nc.psum_tensor([P, QC], F32) as psrot, \
             nc.psum_tensor([P, P], BF16) as pst:
            psq = [pq0, pq1]
            psn = [pn0, pn1]
            psR = [pR0, pR1]
            st = {"i": 0, "dve_done": {}, "pe_done": {}, "rot_done": 0,
                  "pst_done": dve.n, "ln_dve": 0}

            def emit_proj(wslot, wsem, wmark, tc_col):
                i = st["i"]
                sl = i % 2

                def fp(t, i=i, sl=sl, wslot=wslot, tc_col=tc_col, wmark=wmark,
                       dm=max(st["dve_done"].get(i - 2, dve.n), st["ln_dve"])):
                    t.wait_ge(wsem.h, wmark)
                    if dm:
                        t.wait_ge(dve.h, dm)
                    for db in range(DB):
                        pe.inc(t.matmul(
                            psq[sl][:, 0:QC],
                            wbuf[:, wslot + db * P:wslot + (db + 1) * P],
                            lnxt[:, db * FTC + tc_col:db * FTC + tc_col + QC],
                            start=(db == 0), stop=(db == DB - 1)))
                    st["pe_done"][i] = pe.n
                PE(fp)

            def emit_tail_qk(gain_off, tc_abs, dst, dcol):
                i = st["i"]
                sl = i % 2

                def t1(v, i=i, sl=sl):
                    v.wait_ge(pe.h, st["pe_done"][i])
                    dve.inc(v.tensor_copy(qsc4[:, sl * QC:sl * QC + QC],
                                          psq[sl][:, 0:QC]))
                    v.wait_ge(dve.h, dve.n)
                    dve.inc(v.tensor_tensor(sqb[:, sl * QC:sl * QC + QC],
                                            qsc4[:, sl * QC:sl * QC + QC],
                                            qsc4[:, sl * QC:sl * QC + QC],
                                            ALU.mult))
                DVE(t1)
                a1 = dve.n

                def t2(t, i=i, sl=sl):
                    t.wait_ge(dve.h, a1)
                    pe.inc(t.matmul(psn[sl][0:1, 0:QC], ones128_bf,
                                    sqb[:, sl * QC:sl * QC + QC],
                                    start=True, stop=True))
                PE(t2)
                p2 = pe.n

                rT = R_T0 + sl * 512
                rB = sl * 512

                def t45(s, i=i, sl=sl, rT=rT, rB=rB):
                    s.wait_ge(pe.h, p2)
                    s.wait_ge(act.h, act.n)
                    act.inc(s.activation(rows[0:1, rT:rT + QC],
                                         psn[sl][0:1, 0:QC], AF.Ln,
                                         bias=small[0:1, C_EPSRMS:C_EPSRMS + 1]))
                    s.wait_ge(act.h, act.n)
                    act.inc(s.activation(rowsb[0:1, rB:rB + QC],
                                         rows[0:1, rT:rT + QC], AF.Exp,
                                         scale=-0.5))
                ACT(t45)
                a5 = act.n

                def t6(t, i=i, sl=sl, rB=rB):
                    t.wait_ge(act.h, a5)
                    pe.inc(t.matmul(psR[sl][:, 0:QC], ones_row_bf,
                                    rowsb[0:1, rB:rB + QC], start=True, stop=True))
                PE(t6)
                p6 = pe.n

                def t78(v, i=i, sl=sl, gain_off=gain_off):
                    v.wait_ge(pe.h, p6)
                    dve.inc(v.tensor_scalar(qsc4[:, 2 * QC:3 * QC],
                                            qsc4[:, sl * QC:sl * QC + QC],
                                            small[:, gain_off:gain_off + 1], None,
                                            ALU.mult))
                    v.wait_ge(dve.h, dve.n)
                    dve.inc(v.tensor_tensor(ptil[:, sl * QC:sl * QC + QC],
                                            qsc4[:, 2 * QC:3 * QC],
                                            psR[sl][:, 0:QC], ALU.mult))
                DVE(t78)
                d8 = dve.n

                def t9(t, i=i, sl=sl, rd=st["rot_done"]):
                    t.wait_ge(dve.h, d8)
                    if rd:
                        t.wait_ge(dve.h, rd)
                    pe.inc(t.matmul(psrot[:, 0:QC], rtc[:, 0:P],
                                    ptil[:, sl * QC:sl * QC + QC],
                                    start=True, stop=True))
                PE(t9)
                p9 = pe.n

                def t10(v, i=i, sl=sl, tc_abs=tc_abs, dst=dst, dcol=dcol):
                    v.wait_ge(pe.h, p9)
                    dve.inc(v.tensor_tensor(qsc4[:, sl * QC:sl * QC + QC],
                                            ptil[:, sl * QC:sl * QC + QC],
                                            ropet[:, tc_abs:tc_abs + QC], ALU.mult))
                    v.wait_ge(dve.h, dve.n)
                    dve.inc(v.tensor_tensor(qsc4[:, 2 * QC:3 * QC],
                                            psrot[:, 0:QC],
                                            ropet[:, EXT + tc_abs:EXT + tc_abs + QC],
                                            ALU.mult))
                    st["rot_done"] = dve.n
                    v.wait_ge(dve.h, dve.n)
                    dve.inc(v.tensor_tensor(dst[:, dcol:dcol + QC],
                                            qsc4[:, sl * QC:sl * QC + QC],
                                            qsc4[:, 2 * QC:3 * QC],
                                            ALU.add))
                    st["dve_done"][i] = dve.n
                DVE(t10)

            def emit_tail_v(g, tc_abs):
                i = st["i"]
                sl = i % 2

                def v1(v, i=i, sl=sl, pd=st["pst_done"]):
                    v.wait_ge(pe.h, st["pe_done"][i])
                    dve.inc(v.tensor_copy(ptil[:, sl * QC:sl * QC + QC],
                                          psq[sl][:, 0:QC]))
                DVE(v1)
                dv1 = dve.n
                nb = QC // P
                pe_tr = []

                def v2(t, i=i, sl=sl, g=g, tc_abs=tc_abs, dv1=dv1,
                       pd=st["pst_done"]):
                    for j in range(nb):
                        t.wait_ge(dve.h, dv1 + j)
                        if j == 0 and pd:
                            t.wait_ge(dve.h, max(dv1, pd))
                        pe.inc(t.transpose(pst[:],
                                           ptil[:, sl * QC + j * P:sl * QC + (j + 1) * P],
                                           ident[:]))
                        pe_tr.append(pe.n)
                PE(v2)

                def v3(v, i=i, g=g, tc_abs=tc_abs):
                    for j in range(nb):
                        v.wait_ge(pe.h, pe_tr[j])
                        o = VB_OFF + g * EXT + tc_abs + j * P
                        dve.inc(v.tensor_copy(kvb[:, o:o + P], pst[:]))
                    st["pst_done"] = dve.n
                    st["dve_done"][i] = dve.n
                DVE(v3)

            wload = {"n": 0}

            def load_w(wT, blk):
                li = wload["n"]
                wload["n"] += 1
                wslot = (li % 2) * DB * P
                wsem = s_w[li % 2]

                def fg(g_, wT=wT, blk=blk, wslot=wslot,
                       pm=st["pe_done"].get(st["i"] - 2, pe.n)):
                    if pm:
                        g_.wait_ge(pe.h, pm)
                    wsem.inc(g_.dma_start(
                        out=wbuf[:, wslot:wslot + DB * P].rearrange("p (db c) -> p db c", db=DB),
                        in_=wT.ap().rearrange("(db p) c -> p db c", p=P)
                            [:, :, blk * P:(blk + 1) * P]), 16)
                POOL(fg)
                return wslot, wsem, wsem.n

            for ch in range(NCH):
                base = ch * FTC
                for tc in range(NTC):
                    col0 = base + tc * QC
                    emit_ln_stats(res.ap(), col0, pR0, pR1, pn0, pn1, C_EPSLN)
                    emit_ln_apply(res.ap(), col0, pR0, pR1, lnxt, FTC, tc * QC)
                    st["ln_dve"] = dve.n

                for g in range(KVB):
                    wslot, wsem, wmark = load_w(wkT, g)
                    for tc in range(NTC):
                        emit_proj(wslot, wsem, wmark, tc * QC)
                        emit_tail_qk(KNW, base + tc * QC, kvb, g * EXT + base + tc * QC)
                        st["i"] += 1
                    wslot, wsem, wmark = load_w(wvT, g)
                    for tc in range(NTC):
                        emit_proj(wslot, wsem, wmark, tc * QC)
                        emit_tail_v(g, base + tc * QC)
                        st["i"] += 1
                for h in range(H):
                    cols = [tc for tc in range(NTC) if base + tc * QC >= HALO]
                    if not cols:
                        continue
                    wslot, wsem, wmark = load_w(wqT, h)
                    for tc in cols:
                        emit_proj(wslot, wsem, wmark, tc * QC)
                        emit_tail_qk(QNW, base + tc * QC, bigB,
                                     h * NT + (base + tc * QC - HALO))
                        st["i"] += 1

    def phase_flash_b():
        WBK = W // P
        for h in range(H):
            g = h // RH
            for qi in range(NT // FQ2):
                q0 = HALO + qi * FQ2
                kb_lo = (q0 - W + 1) // P
                kb_hi = (q0 + FQ2 - 1) // P
                kbs = list(range(kb_lo, kb_hi + 1))
                nk = len(kbs)
                qoff = h * NT + qi * FQ2

                geo = []
                for kb in kbs:
                    qlo = max(q0, P * kb)
                    qhi = min(q0 + FQ2, P * kb + W + P)
                    subs = []
                    a = qlo
                    while a < qhi:
                        bnd = q0 + ((a - q0) // 512 + 1) * 512
                        b_ = min(qhi, bnd)
                        subs.append((a - q0, b_ - a))
                        a = b_
                    diag = (P * kb >= q0)
                    far = (qhi == P * kb + W + P)
                    halo_k = (kb * P < HALO)
                    geo.append((kb, qlo, qhi, subs, diag, far, halo_k))

                with nc.psum_tensor([P, FQ2], F32) as psO, \
                     nc.psum_tensor([1, FQ2], F32) as psD:
                  with nc.psum_tensor([P, FQ2], F32) as psS0, \
                       nc.psum_tensor([P, FQ2], F32) as psS1:
                    psS = [psS0, psS1]
                    pe0, act0, dve0 = pe.n, act.n, dve.n
                    pe_after_scores = []
                    dve_after_masks = []
                    pe_c = pe0
                    dve_c = dve0
                    for ki, (kb, qlo, qhi, subs, diag, far, halo_k) in enumerate(geo):
                        pe_c += len(subs)
                        pe_after_scores.append(pe_c)
                        dve_c += (1 if diag else 0) + (1 if far else 0)
                        dve_after_masks.append(dve_c)
                        if ki >= 1:
                            pe_c += 2 * len(geo[ki - 1][3])
                    pe_c += 2 * len(geo[-1][3])

                    started_O = set()
                    started_D = set()

                    def fp(t, h=h, g=g, q0=q0):
                        if dve0:
                            t.wait_ge(dve.h, dve0)
                        for ki, (kb, qlo, qhi, subs, diag, far, halo_k) in enumerate(geo):
                            if ki >= 2:
                                t.wait_ge(act.h, act0 + ki - 1)
                            for off, n in subs:
                                pe.inc(t.matmul(
                                    psS[ki % 2][:, off:off + n],
                                    kvb[:, g * EXT + kb * P:g * EXT + (kb + 1) * P],
                                    bigB[:, qoff + off:qoff + off + n],
                                    start=True, stop=True))
                            if ki >= 1:
                                t.wait_ge(act.h, act0 + ki)
                                kbp, qlop, qhip, subsp, _, _, _ = geo[ki - 1]
                                pl = ((ki - 1) % 2) * FQ2
                                for off, n in subsp:
                                    st = (off // 512) not in started_O
                                    started_O.add(off // 512)
                                    pe.inc(t.matmul(
                                        psO[:, off:off + n],
                                        kvb[:, VB_OFF + g * EXT + kbp * P:
                                             VB_OFF + g * EXT + (kbp + 1) * P],
                                        ptil[:, pl + off:pl + off + n],
                                        start=st, stop=False))
                                for off, n in subsp:
                                    st = (off // 512) not in started_D
                                    started_D.add(off // 512)
                                    pe.inc(t.matmul(
                                        psD[0:1, off:off + n], ones_bf,
                                        ptil[:, pl + off:pl + off + n],
                                        start=st, stop=False))
                        t.wait_ge(act.h, act0 + len(geo))
                        kbp, qlop, qhip, subsp, _, _, _ = geo[-1]
                        pl = ((len(geo) - 1) % 2) * FQ2
                        for off, n in subsp:
                            st = (off // 512) not in started_O
                            started_O.add(off // 512)
                            pe.inc(t.matmul(
                                psO[:, off:off + n],
                                kvb[:, VB_OFF + g * EXT + kbp * P:
                                     VB_OFF + g * EXT + (kbp + 1) * P],
                                ptil[:, pl + off:pl + off + n],
                                start=st, stop=True))
                        for off, n in subsp:
                            st = (off // 512) not in started_D
                            started_D.add(off // 512)
                            pe.inc(t.matmul(
                                psD[0:1, off:off + n], ones_bf,
                                ptil[:, pl + off:pl + off + n],
                                start=st, stop=True))
                    PE(fp)

                    def fm(v, q0=q0):
                        for ki, (kb, qlo, qhi, subs, diag, far, halo_k) in enumerate(geo):
                            if not (diag or far):
                                continue
                            v.wait_ge(pe.h, pe_after_scores[ki])
                            sl = psS[ki % 2]
                            if diag:
                                o = P * kb - q0
                                dve.inc(v.tensor_tensor(
                                    sl[:, o:o + P], sl[:, o:o + P],
                                    mask2[:, P:2 * P], ALU.add))
                            if far:
                                o = qhi - P - q0
                                dve.inc(v.tensor_tensor(
                                    sl[:, o:o + P], sl[:, o:o + P],
                                    mask2[:, 0:P], ALU.add))
                    DVE(fm)

                    def fa(s, q0=q0):
                        act_set_pin(s)
                        for ki, (kb, qlo, qhi, subs, diag, far, halo_k) in enumerate(geo):
                            s.wait_ge(pe.h, pe_after_scores[ki])
                            s.wait_ge(dve.h, dve_after_masks[ki])
                            bias = small[:, HALOB:HALOB + 1] if halo_k \
                                else small[:, C_NSHIFT:C_NSHIFT + 1]
                            pl = (ki % 2) * FQ2
                            o = qlo - q0
                            act.inc(s.activation(
                                ptil[:, pl + o:pl + (qhi - q0)],
                                psS[ki % 2][:, o:qhi - q0], AF.Exp, bias=bias))
                    ACT(fa)
                    act_all = act.n

                  with nc.psum_tensor([P, FQ2], F32) as psR:
                    def fv(s):
                        s.wait_ge(pe.h, pe.n)
                        act.inc(s.activation(rows[0:1, R_DEN:R_DEN + FQ2],
                                             psD[0:1, 0:FQ2], AF.Ln))
                        s.wait_ge(act.h, act.n)
                        act.inc(s.activation(rowsb[0:1, RB_DEN:RB_DEN + FQ2],
                                             rows[0:1, R_DEN:R_DEN + FQ2],
                                             AF.Exp, scale=-1.0))
                    ACT(fv)

                    def fp2(t):
                        t.wait_ge(act.h, act.n)
                        for ss in range(FQ2 // QC):
                            pe.inc(t.matmul(
                                psR[:, ss * QC:(ss + 1) * QC], ones_row_bf,
                                rowsb[0:1, RB_DEN + ss * QC:RB_DEN + (ss + 1) * QC],
                                start=True, stop=True))
                    PE(fp2)

                    def fv2(v, qoff=qoff):
                        v.wait_ge(pe.h, pe.n)
                        dve.inc(v.tensor_copy(rtil[:, 0:FQ2], psR[:, 0:FQ2]))
                        v.wait_ge(dve.h, dve.n)
                        dve.inc(v.tensor_tensor(bigB[:, qoff:qoff + FQ2],
                                                psO[:, 0:FQ2], rtil[:, 0:FQ2],
                                                ALU.mult))
                    DVE(fv2)

    phase_setup()
    join_all()
    phase_kv()
    join_all()
    for slab in range(NSLAB):
        phase_attn_a_slab(slab)
    phase_ffn_gate_setup()
    join_light()
    phase_ffn(res.ap(), res.ap(), 0, EXT, 0)
    join_light()
    phase_ropetab()
    join_light()
    phase_attn_b_proj()
    join_light()
    phase_flash_b()
    join_light()
    emit_oproj(woT.ap(), kvb, 0, bigB, NT, 0, HALO, NT, None,
               res.ap(), res.ap(), 0)
    join_light()
    phase_ffn(res.ap(), outT.ap(), HALO, NT, HALO)

    def ffinal(s):
        s.wait_ge(s_resw[0].h, s_resw[0].n)
        s.wait_ge(s_resw[1].h, s_resw[1].n)
    SP(ffinal)

    ctx.close()
    return nc



def _rope_tables(cfg, ext0):
    pos = ext0 + np.arange(cfg.EXT, dtype=np.float64)
    inv = 1.0 / (10000.0 ** (np.arange(0, P, 2, dtype=np.float64) / P))
    ang = inv[:, None] * pos[None, :]
    cos = np.cos(ang).astype(np.float32)
    sin = np.sin(ang).astype(np.float32)
    cos2 = np.concatenate([cos, cos], axis=0)
    sin2 = np.concatenate([sin, sin], axis=0)
    return np.concatenate([cos2, sin2], axis=1)


def make_core_inputs(cfg, inp, core):
    bsz = 4
    bidx, rank = divmod(core, bsz)
    t0 = rank * cfg.NT
    x = inp["L0_attn_a_x"]
    xT = np.ascontiguousarray(np.transpose(x[bidx]))
    ext0 = t0 - cfg.HALO
    if ext0 < 0:
        pad = np.zeros((cfg.D, -ext0), np.float32)
        xTe = np.concatenate([pad, xT[:, 0:t0 + cfg.NT]], axis=1)
    else:
        xTe = xT[:, ext0:t0 + cfg.NT]
    halo_valid = 0.0 if ext0 >= 0 else -30000.0
    d = {
        "xTb": xT,
        "xTe": np.ascontiguousarray(xTe),
        "Wq": inp["L0_attn_a_Wq"],
        "Wk": inp["L0_attn_a_Wk"],
        "Wv": inp["L0_attn_a_Wv"],
        "Wo": inp["L0_attn_a_Wo"],
        "bqT": np.ascontiguousarray(inp["L0_attn_a_bq"].reshape(cfg.H, P).T),
        "bk": inp["L0_attn_a_bk"].reshape(P, 1),
        "bv": inp["L0_attn_a_bv"].reshape(P, 1),
        "boT": np.ascontiguousarray(inp["L0_attn_a_bo"].reshape(cfg.DB, P).T),
        "WgT": np.ascontiguousarray(inp["L0_ffn_Wg"].T),
        "W0aT": np.ascontiguousarray(inp["L0_ffn_W0a"].T),
        "W0bT": np.ascontiguousarray(inp["L0_ffn_W0b"].T),
        "W1aT": np.ascontiguousarray(inp["L0_ffn_W1a"].T),
        "W1bT": np.ascontiguousarray(inp["L0_ffn_W1b"].T),
        "b0aT": np.ascontiguousarray(inp["L0_ffn_b0a"].reshape(cfg.FB, P).T),
        "b0bT": np.ascontiguousarray(inp["L0_ffn_b0b"].reshape(cfg.DB, P).T),
        "b1aT": np.ascontiguousarray(inp["L0_ffn_b1a"].reshape(cfg.FB, P).T),
        "b1bT": np.ascontiguousarray(inp["L0_ffn_b1b"].reshape(cfg.DB, P).T),
        "wqT": np.ascontiguousarray(inp["L0_attn_b_wq"].T),
        "wkT": np.ascontiguousarray(inp["L0_attn_b_wk"].T),
        "wvT": np.ascontiguousarray(inp["L0_attn_b_wv"].T),
        "woT": np.ascontiguousarray(inp["L0_attn_b_wo"].T),
        "qnw": inp["L0_attn_b_q_norm_w"].reshape(P, 1),
        "knw": inp["L0_attn_b_k_norm_w"].reshape(P, 1),
        "pos0": np.array([[float(ext0)]], np.float32),
        "ropeT": _rope_tables(cfg, ext0),
        "halob": np.full((P, 1), halo_valid, np.float32),
    }
    return {k: np.ascontiguousarray(np.asarray(v, np.float32)) for k, v in d.items()}


def kernel(**inputs):
    cfg = Cfg()
    nc = build_program(cfg)
    in_maps = [make_core_inputs(cfg, inputs, c) for c in range(8)]
    r = run_bass_kernel_spmd(nc, in_maps, core_ids=list(range(8)))
    outs = []
    for b in range(2):
        cores = [r.results[b * 4 + i]["outT"].T for i in range(4)]
        outs.append(np.concatenate(cores, axis=0))
    return np.stack(outs, axis=0).astype(np.float32)
